# version 16
# baseline (speedup 1.0000x reference)
import numpy as np
from contextlib import ExitStack

import concourse.bass as bass
import concourse.mybir as mybir
from concourse.bass_utils import run_bass_kernel_spmd

F32 = mybir.dt.float32
FP16 = mybir.dt.float16
AF = mybir.ActivationFunctionType
ALU = mybir.AluOpType
AX = mybir.AxisListType

B_GLOB, T_GLOB, C = 2, 8192, 1024
NCORES = 8
L = 128
H_ATT, DH_ATT = 16, 64
H_FFN, DH_FFN = 8, 128
EPS = 1e-5
ENGS = ("sync", "scalar", "vector", "tensor", "gpsimd")


class Sem:
    __slots__ = ("h", "n", "is_cc")

    def __init__(self, stack, nc, name):
        self.h = stack.enter_context(nc.semaphore(name))
        self.n = 0
        self.is_cc = False


class Buf:
    __slots__ = ("ap", "w", "rs")

    def __init__(self, ap=None):
        self.ap = ap
        self.w = None
        self.rs = []


class DmaPool:

    def __init__(self, prog, name, n):
        self.sems = [Sem(prog.stack, prog.nc, f"{name}{i}") for i in range(n)]
        self.last = [None] * n
        self.i = 0


class Tick:
    __slots__ = ("eng", "seq", "sem", "val")

    def __init__(self, eng, seq):
        self.eng = eng
        self.seq = seq
        self.sem = None
        self.val = None


class Prog:

    def __init__(self, nc, stack):
        self.nc = nc
        self.stack = stack
        self.ops = {e: [] for e in ENGS}
        self.tick = {e: Sem(stack, nc, f"tick_{e}") for e in ENGS}
        self.extra_sems = []

    def op(self, eng, fn, reads=(), writes=(), extra=(), inc=True, step=1,
           sem=None):
        deps = []
        for b in reads:
            if b.w is not None:
                deps.append(b.w)
        for b in writes:
            if b.w is not None:
                deps.append(b.w)
            deps.extend(b.rs)
        for d in extra:
            if d is None:
                continue
            if isinstance(d, tuple):
                t0 = Tick(d[0], -1)
                t0.sem, t0.val = d[0], d[1]
                deps.append(t0)
            else:
                deps.append(d)
        seq = len(self.ops[eng])
        if sem is not None:
            t = Tick(sem, seq)
            sem.n += step
            t.sem, t.val = sem, sem.n
        elif inc:
            t = Tick(eng, seq)
        else:
            t = None
        self.ops[eng].append((fn, deps, t, sem, step))
        if t is not None:
            for b in reads:
                b.rs.append(t)
            for b in writes:
                b.w = t
                b.rs = []
        return t

    @staticmethod
    def war(buf):
        d = list(buf.rs)
        if buf.w is not None:
            d.append(buf.w)
        return d

    def dma(self, eng, pool, out_ap, in_ap, reads=(), writes=(), extra=(),
            nobc=False):
        i = pool.i
        pool.i = (i + 1) % len(pool.sems)
        s = pool.sems[i]
        ex = list(extra)
        if pool.last[i] is not None:
            ex.append(pool.last[i])
        if nobc:
            t = self.op(eng, lambda e: e.dma_start(out=out_ap, in_=in_ap,
                                                   bounds_check=None),
                        reads=reads, writes=writes, extra=ex, sem=s, step=16)
        else:
            t = self.op(eng, lambda e: e.dma_start(out=out_ap, in_=in_ap),
                        reads=reads, writes=writes, extra=ex, sem=s, step=16)
        pool.last[i] = t
        return t

    def barrier(self, skip_cc=False):
        self._skip_cc = skip_cc
        marks = []
        for e in ENGS:
            for seq in range(len(self.ops[e]) - 1, -1, -1):
                fn = self.ops[e][seq][0]
                if fn == "BARRIER":
                    continue
                t_existing = self.ops[e][seq][2]
                sem_over = self.ops[e][seq][3]
                if sem_over is not None and skip_cc and \
                        getattr(sem_over, "is_cc", False):
                    continue
                if sem_over is not None and t_existing is not None:
                    marks.append((sem_over, t_existing.val))
                else:
                    marks.append(Tick(e, seq))
                break
        sems = [(s, s.n) for s in self.extra_sems
                if s.n > 0 and not (skip_cc and getattr(s, "is_cc", False))]
        for e in ENGS:
            deps = []
            sdeps = list(sems)
            for m in marks:
                if isinstance(m, tuple):
                    sdeps.append(m)
                elif m.eng != e:
                    deps.append(m)
            self.ops[e].append(("BARRIER", deps, None, sdeps, 1))

    def resolve_and_replay(self, blk):
        marked = {e: set() for e in ENGS}
        for e in ENGS:
            for (fn, deps, t, sem, step) in self.ops[e]:
                for d in deps:
                    if isinstance(d, Tick) and d.sem is None and d.val is None:
                        marked[d.eng].add(d.seq)
        tickval = {}
        for e in ENGS:
            for rank, seq in enumerate(sorted(marked[e])):
                tickval[(e, seq)] = rank + 1
        def run_engine(en, ename):
            wm = {}
            cnt = 0
            for (fn, deps, t, sem, step) in self.ops[ename]:
                waits = []
                if fn == "BARRIER":
                    for d in deps:
                        s = self.tick[d.eng]
                        v = tickval.get((d.eng, d.seq))
                        if v is None:
                            v = sum(1 for q in marked[d.eng] if q <= d.seq)
                        else:
                            pass
                        if v and wm.get(id(s), 0) < v:
                            wm[id(s)] = v
                            waits.append((s.h, v))
                    for (s, v) in sem:
                        if wm.get(id(s), 0) < v:
                            wm[id(s)] = v
                            waits.append((s.h, v))
                    for sh, v in waits:
                        en.wait_ge(sh, v)
                    continue
                for d in deps:
                    if d.sem is not None:
                        s, v = d.sem, d.val
                    else:
                        s = self.tick[d.eng]
                        v = tickval.get((d.eng, d.seq))
                        if v is None:
                            v = sum(1 for q in marked[d.eng] if q <= d.seq)
                        if v == 0:
                            continue
                    if wm.get(id(s), 0) < v:
                        wm[id(s)] = v
                        waits.append((s.h, v))
                for sh, v in waits:
                    en.wait_ge(sh, v)
                ins = fn(en)
                if sem is not None and t is not None:
                    ins.then_inc(sem.h, step)
                elif t is not None and t.seq in marked[ename]:
                    ins.then_inc(self.tick[ename].h, 1)
                cnt += 1
        for ename in ENGS:
            getattr(blk, ename)(lambda en, ename=ename: run_engine(en, ename))

def host_constants(T, core):
    r = core % 4
    own = T // 4
    q1 = own * (r + 1)
    g = 1.0 - 2.0 ** (-5.0 - np.arange(H_ATT, dtype=np.float64))
    cons = {}
    pos = np.arange(T, dtype=np.float64) + (q1 - T)
    invf = 10000.0 ** (-(np.arange(32, dtype=np.float64) / 32.0))
    fr = pos[:, None] * invf[None, :]
    emb = np.concatenate([fr, fr], axis=1)
    cons["cos_t"] = np.cos(emb).astype(np.float32)
    cons["sin_t"] = np.sin(emb).astype(np.float32)
    i_loc = np.arange(L, dtype=np.float64)
    dt_mask = np.zeros((L, H_ATT * L), dtype=np.float32)
    for h in range(H_ATT):
        M = np.where(i_loc[None, :] >= i_loc[:, None],
                     g[h] ** (i_loc[None, :] - i_loc[:, None]), 0.0) / 8.0
        dt_mask[:, h * L:(h + 1) * L] = M.astype(np.float32)
    cons["dt_mask"] = dt_mask
    cons["gq_c"] = (np.repeat(g[None, :] ** i_loc[:, None], DH_ATT,
                              axis=1) / 8.0).astype(np.float32)
    cons["gk_c"] = np.repeat(g[None, :] ** (L - i_loc)[:, None], DH_ATT,
                           axis=1).astype(np.float32)
    cons["gl_c"] = np.repeat(g[None, :] ** L, DH_ATT)[None, :].repeat(
        64, axis=0).astype(np.float32)
    nch = T // L
    J0 = (own // L) * r
    wst = np.zeros((128, nch * H_ATT), np.float64)
    for j in range(J0):
        wst[:, j * H_ATT:(j + 1) * H_ATT] = g ** ((J0 - 1 - j) * L)
    cons["wstA_c"] = wst.astype(np.float32)
    cons["ones_col"] = np.ones((128, 1), np.float32)
    cons["ones_row"] = np.ones((1, 128), np.float32)
    cons["ident32"] = np.eye(128, dtype=np.float32)
    cons["ident16"] = np.eye(128, dtype=np.float16)
    return cons


def host_inputs(T, core, inp):
    beta, r = core // 4, core % 4
    own = T // 4
    q1 = own * (r + 1)
    d = {}
    x = np.asarray(inp["L0_attn_x"])[:, :T]
    xa = np.zeros((T, C), np.float32)
    xa[T - q1:] = x[beta, 0:q1]
    d["x_attn"] = xa
    for nm, w in (("wq", "L0_attn_Wq"), ("wk", "L0_attn_Wk"),
                  ("wv", "L0_attn_Wv"), ("wo", "L0_attn_Wo")):
        d[nm + "_t"] = np.ascontiguousarray(np.asarray(inp[w]).T)
    d["gnw_rep"] = np.broadcast_to(np.asarray(inp["L0_attn_gn_weight"]),
                                   (128, C)).copy()
    d["gnb_rep"] = np.broadcast_to(np.asarray(inp["L0_attn_gn_bias"]),
                                   (128, C)).copy()
    m = C >> r
    qw = np.zeros((C, C), np.float32)
    qw[:m, :] = np.asarray(inp["L0_ffn_q_w"]).T[:m, :]
    d["fqw_t"] = qw
    kw = np.zeros((C, C), np.float32)
    kw[:m, :] = np.asarray(inp["L0_ffn_k_w"]).T[:m, :]
    d["fkw_t"] = kw
    vw = np.zeros((C, C), np.float32)
    vw[:m, :] = np.asarray(inp["L0_ffn_v_w"]).T[:m, :]
    d["fvw_t"] = vw
    ow = np.zeros((C, C), np.float32)
    ow[:m, :m] = np.asarray(inp["L0_ffn_o_w"]).T[:m, :m]
    d["fow_t"] = ow
    d["fqb_fm"] = np.ascontiguousarray(
        np.asarray(inp["L0_ffn_q_b"], np.float32).reshape(8, 128).T)
    d["fkb_fm"] = np.ascontiguousarray(
        np.asarray(inp["L0_ffn_k_b"], np.float32).reshape(8, 128).T)
    d["fvb_rep"] = np.broadcast_to(np.asarray(inp["L0_ffn_v_b"]),
                                   (128, C)).copy()
    ob = np.zeros(C, np.float32)
    ob[:m] = np.asarray(inp["L0_ffn_o_b"])[:m]
    d["fob_rep"] = np.broadcast_to(ob, (128, C)).copy()
    d["fnw_rep"] = np.broadcast_to(np.asarray(inp["L0_ffn_norm_w"]),
                                   (128, C)).copy()
    d["fnb_rep"] = np.broadcast_to(np.asarray(inp["L0_ffn_norm_b"]),
                                   (128, C)).copy()
    return d


def build_program(T=T_GLOB, dbg=(), phases=6):
    own = T // 4
    NCH = T // L
    OCH = own // L
    PCH = NCH - OCH
    NG = NCH // 4
    OG = OCH // 4
    QT = own // 512

    nc = bass.Bass("TRN2", target_bir_lowering=False)
    st = ExitStack()
    P = Prog(nc, st)

    def din(name, shape, dt=F32):
        return nc.dram_tensor(name, shape, dt, kind="ExternalInput").ap()

    x_attn = din("x_attn", [T, C])
    wq_t, wk_t, wv_t, wo_t = (din("wq_t", [C, C]), din("wk_t", [C, C]),
                              din("wv_t", [C, C]), din("wo_t", [C, C]))
    gnw_rep, gnb_rep = din("gnw_rep", [128, C]), din("gnb_rep", [128, C])
    fqw_t, fkw_t = din("fqw_t", [C, C]), din("fkw_t", [C, C])
    fvw_t, fow_t = din("fvw_t", [C, C]), din("fow_t", [C, C])
    fqb_fm, fkb_fm = din("fqb_fm", [128, 8]), din("fkb_fm", [128, 8])
    fvb_rep, fob_rep = din("fvb_rep", [128, C]), din("fob_rep", [128, C])
    fnw_rep, fnb_rep = din("fnw_rep", [128, C]), din("fnb_rep", [128, C])
    cos_t, sin_t = din("cos_t", [T, 64]), din("sin_t", [T, 64])
    dt_mask = din("dt_mask", [L, H_ATT * L])
    gq_c, gk_c, gl_c = din("gq_c", [L, C]), din("gk_c", [L, C]), din("gl_c", [64, C])
    wstA_c = din("wstA_c", [128, NCH * H_ATT])
    ones_col_d, ones_row_d = din("ones_col", [128, 1]), din("ones_row", [1, 128])
    id32_d = din("ident32", [128, 128])
    id16_d = din("ident16", [128, 128], FP16)

    out_d = nc.dram_tensor("out", [own, C], F32, kind="ExternalOutput").ap()

    def dscr(name, shape, dt=F32):
        return Buf(nc.dram_tensor(name, shape, dt).ap())

    kv_own_d = dscr("kv_own_d", [2, OCH, L, C])
    kv_all_d = dscr("kv_all_d", [4, 2, OCH, L, C])
    kfh_d = dscr("kfh_d", [OCH, H_ATT, DH_ATT, L], FP16)
    kfl_d = dscr("kfl_d", [OCH, H_ATT, DH_ATT, L], FP16)
    q8h_d = dscr("q8h_d", [OCH, H_ATT, DH_ATT, L], FP16)
    q8l_d = dscr("q8l_d", [OCH, H_ATT, DH_ATT, L], FP16)
    qs_d = dscr("qs_d", [OCH, H_ATT, DH_ATT, L])
    ret_d = dscr("ret_d", [OCH, L, C])
    ag_in = dscr("ag_in", [own, C])
    sa_d = dscr("sa_d", [64, C])
    KS = OG * H_FFN * DH_FFN * 512
    pk_own_t = nc.dram_tensor("pk_own", [2 * KS], FP16)
    pk_all_t = nc.dram_tensor("pk_all", [4 * 2 * KS], FP16)
    pk_own_d = Buf(pk_own_t.ap())
    pk_all_d = Buf(pk_all_t.ap())
    vf_own_d = dscr("vf_own_d", [OCH, L, C])
    vf_all_d = dscr("vf_all_d", [4, OCH, L, C])

    def pk_kf(hl, g, h, rk=None):
        off = (0 if rk is None else rk * 2 * KS) + hl * KS + \
            (g * H_FFN + h) * DH_FFN * 512
        t = pk_own_t if rk is None else pk_all_t
        return bass.AP(t, off, [[512, DH_FFN], [1, 512]])
    ah_d = dscr("ah_d", [H_FFN, DH_FFN, own], FP16)
    al_d = dscr("al_d", [H_FFN, DH_FFN, own], FP16)
    qh_d = dscr("qh_d", [H_FFN, DH_FFN, own], FP16)
    ql_d = dscr("ql_d", [H_FFN, DH_FFN, own], FP16)

    dbg_aps = {}
    for tag, shape in (("ret", [OCH, L, C]), ("x1", [own, C]),
                       ("ks", [NCH, L, C]), ("va", [NCH, L, C]),
                       ("xff", [NCH, L, C]),
                       ("vff", [NCH, L, C]), ("qf", [own, C]),
                       ("kff", [NCH, L, C]), ("fattn", [own, C])):
        if tag in dbg:
            dbg_aps[tag] = nc.dram_tensor("dbg_" + tag, shape, F32,
                                          kind="ExternalOutput").ap()

    sd = DmaPool(P, "sd", 12)
    sd2 = DmaPool(P, "se", 8)
    ad = DmaPool(P, "ad", 12)
    cc = Sem(st, nc, "cc")
    cc.is_cc = True
    P.extra_sems = sd.sems + sd2.sems + ad.sems + [cc]

    def sbuf(stack, name, shape, dt=F32):
        return Buf(stack.enter_context(nc.sbuf_tensor(name, shape, dt)).ap())

    def psum(stack, name, shape, dt=F32):
        return Buf(stack.enter_context(nc.psum_tensor(name, shape, dt)).ap())

    onesc_b = sbuf(st, "onesc_b", [128, 1])
    onesr_b = sbuf(st, "onesr_b", [1, 128])
    id16_b = sbuf(st, "id16_b", [128, 128], FP16)
    id32_b = sbuf(st, "id32_b", [128, 128])
    for bb, ap in ((onesc_b, ones_col_d), (onesr_b, ones_row_d),
                   (id16_b, id16_d), (id32_b, id32_d)):
        P.dma("sync", sd, bb.ap[:], ap, writes=[bb])

    _wsl_ctr = [0]

    def mk_wslots(stack, names):
        _wsl_ctr[0] += 1
        w_hi = {}
        w_lo = {}
        for wn in names:
            w_hi[wn] = sbuf(stack, f"w{_wsl_ctr[0]}_{wn}_hi", [128, 8 * C], FP16)
            w_lo[wn] = sbuf(stack, f"w{_wsl_ctr[0]}_{wn}_lo", [128, 8 * C], FP16)
        wstage = [sbuf(stack, f"wstage{_wsl_ctr[0]}_{i}", [128, 512])
                  for i in range(2)]
        return w_hi, w_lo, wstage

    def load_weight_into(hi, lo, wstage, dram_ap):
        src = dram_ap.rearrange("(cc p) (hf o) -> cc hf p o", p=128, hf=2)
        for q in range(16):
            ws = wstage[q % 2]
            P.dma("sync", sd, ws.ap[:], src[q // 2, q % 2], writes=[ws])
            P.op("vector",
                 lambda e, q=q, ws=ws: e.tensor_copy(
                     hi.ap[:, q * 512:(q + 1) * 512], ws.ap[:]),
                 reads=[ws], writes=[hi], inc=True)
            P.op("gpsimd",
                 lambda e, q=q, ws=ws: e.tensor_sub(
                     lo.ap[:, q * 512:(q + 1) * 512], ws.ap[:],
                     hi.ap[:, q * 512:(q + 1) * 512]),
                 reads=[ws, hi], writes=[lo], inc=True)

    def load_weight(slots, name, dram_ap):
        w_hi, w_lo, wstage = slots
        load_weight_into(w_hi[name], w_lo[name], wstage, dram_ap)

    def ln_tile(ph, x_buf, x_ap, out_buf, out_ap, scale_w=None, bias_w=None,
                double=False):
        s = ph["stats"]
        mu, var, r, sq_b = s["mu"], s["var"], s["r"], s["sq"]
        P.op("vector", lambda e: e.reduce_sum(mu.ap[:, 0:1], x_ap, axis=AX.X),
             reads=[x_buf], writes=[mu])
        P.op("scalar", lambda e: e.activation(sq_b.ap[:], x_ap, AF.Square),
             reads=[x_buf], writes=[sq_b])
        P.op("vector", lambda e: e.reduce_sum(var.ap[:, 0:1], sq_b.ap[:],
                                              axis=AX.X),
             reads=[sq_b], writes=[var])
        P.op("vector", lambda e: e.tensor_scalar_mul(mu.ap[:, 1:2],
                                                     mu.ap[:, 0:1], 1.0 / C),
             reads=[mu], writes=[mu])
        P.op("vector", lambda e: e.tensor_mul(r.ap[:, 3:4], mu.ap[:, 1:2],
                                              mu.ap[:, 1:2]),
             reads=[mu], writes=[r])
        P.op("vector", lambda e: e.tensor_scalar(var.ap[:, 1:2],
                                                 var.ap[:, 0:1],
                                                 1.0 / C, None, op0=ALU.mult),
             reads=[var], writes=[var])
        P.op("vector", lambda e: e.tensor_sub(var.ap[:, 2:3], var.ap[:, 1:2],
                                              r.ap[:, 3:4]),
             reads=[var, r], writes=[var])
        P.op("vector", lambda e: e.tensor_scalar(var.ap[:, 3:4],
                                                 var.ap[:, 2:3],
                                                 EPS, None, op0=ALU.add),
             reads=[var], writes=[var])
        P.op("vector", lambda e: e.reciprocal(r.ap[:, 0:1], var.ap[:, 3:4]),
             reads=[var], writes=[r])
        P.op("scalar", lambda e: e.activation(r.ap[:, 1:2], r.ap[:, 0:1],
                                              AF.Sqrt),
             reads=[r], writes=[r])
        rr_col = r.ap[:, 1:2]
        if double:
            P.op("vector", lambda e: e.tensor_mul(var.ap[:, 0:1],
                                                  var.ap[:, 2:3],
                                                  r.ap[:, 0:1]),
                 reads=[var, r], writes=[var])
            P.op("vector", lambda e: e.tensor_scalar(var.ap[:, 1:2],
                                                     var.ap[:, 0:1],
                                                     EPS, None, op0=ALU.add),
                 reads=[var], writes=[var])
            P.op("vector", lambda e: e.reciprocal(r.ap[:, 2:3],
                                                  var.ap[:, 1:2]),
                 reads=[var], writes=[r])
            P.op("scalar", lambda e: e.activation(r.ap[:, 3:4], r.ap[:, 2:3],
                                                  AF.Sqrt),
                 reads=[r], writes=[r])
            P.op("vector", lambda e: e.tensor_mul(r.ap[:, 2:3], r.ap[:, 1:2],
                                                  r.ap[:, 3:4]),
                 reads=[r], writes=[r])
            rr_col = r.ap[:, 2:3]
        if scale_w is None:
            P.op("vector", lambda e: e.tensor_scalar(out_ap, x_ap,
                                                     mu.ap[:, 1:2], rr_col,
                                                     op0=ALU.subtract,
                                                     op1=ALU.mult),
                 reads=[x_buf, mu, r], writes=[out_buf])
        else:
            P.op("vector", lambda e: e.tensor_scalar(s["d"].ap[:], x_ap,
                                                     mu.ap[:, 1:2], rr_col,
                                                     op0=ALU.subtract,
                                                     op1=ALU.mult),
                 reads=[x_buf, mu, r], writes=[s["d"]])
            P.op("vector",
                 lambda e: e.tensor_mul(out_ap, s["d"].ap[:], scale_w.ap[:]),
                 reads=[s["d"], scale_w], writes=[out_buf])
            P.op("vector", lambda e: e.tensor_add(out_ap, out_ap, bias_w.ap[:]),
                 reads=[out_buf, bias_w], writes=[out_buf])

    def split16(src_buf, src_ap, hi_buf, hi_ap, lo_buf, lo_ap,
                sub_eng="vector"):
        P.op("vector", lambda e: e.tensor_copy(hi_ap, src_ap),
             reads=[src_buf], writes=[hi_buf])
        P.op(sub_eng, lambda e: e.tensor_sub(lo_ap, src_ap, hi_ap),
             reads=[src_buf, hi_buf], writes=[lo_buf])

    def transpose16(src_buf, src_ap_base, ps_buf, n_blocks=8):
        for j in range(n_blocks):
            last = j == n_blocks - 1
            P.op("tensor",
                 lambda e, j=j: e.transpose(
                     ps_buf.ap[:, j * 128:(j + 1) * 128],
                     src_ap_base[:, j * 128:(j + 1) * 128], id16_b.ap[:]),
                 reads=[src_buf, id16_b],
                 extra=P.war(ps_buf) if j == 0 else (),
                 writes=[ps_buf] if last else (), inc=last)

    def mm3(ps_ap, lT_hi, lT_lo, r_hi, r_lo, start, stop, reads=(), writes=(),
            inc=False, ps_buf=None):
        extra0 = P.war(ps_buf) if (start and ps_buf is not None) else ()
        P.op("tensor", lambda e: e.matmul(ps_ap, lT_hi, r_hi,
                                          start=start, stop=False),
             reads=reads, extra=extra0, inc=False)
        P.op("tensor", lambda e: e.matmul(ps_ap, lT_hi, r_lo,
                                          start=False, stop=False), inc=False)
        return P.op("tensor", lambda e: e.matmul(ps_ap, lT_lo, r_hi,
                                                 start=False, stop=stop),
                    reads=reads, writes=writes, inc=inc)

    p1 = ExitStack()
    ph = {}
    ph["stats"] = {k: sbuf(p1, f"st_{k}", [128, 4]) for k in ("mu", "var", "r")}
    cosl = [sbuf(p1, f"cosl{i}", [128, 64]) for i in range(2)]
    sinl = [sbuf(p1, f"sinl{i}", [128, 64]) for i in range(2)]
    gq_b = sbuf(p1, "gq_b", [L, C])
    gk_b = sbuf(p1, "gk_b", [L, C])
    P.dma("sync", sd, gq_b.ap[:], gq_c, writes=[gq_b])
    P.dma("sync", sd, gk_b.ap[:], gk_c, writes=[gk_b])

    x_in = [sbuf(p1, f"x_in{i}", [128, C]) for i in range(3)]
    xn_tm = [sbuf(p1, f"xn_tm{i}", [128, C]) for i in range(2)]
    t16al = [sbuf(p1, f"t16a{i}", [128, C], FP16) for i in range(2)]
    t16bl = [sbuf(p1, f"t16b{i}", [128, C], FP16) for i in range(2)]
    xf_hi = [sbuf(p1, f"xf_hi{i}", [128, C], FP16) for i in range(2)]
    xf_lo = [sbuf(p1, f"xf_lo{i}", [128, C], FP16) for i in range(2)]
    rt1_s = sbuf(p1, "rt1", [128, C])
    rt2_s = sbuf(p1, "rt2", [128, C])
    rt1l = [rt1_s, rt1_s]
    rt2l = [rt2_s, rt2_s]
    ph["stats"]["sq"] = rt2_s
    krfl = [sbuf(p1, f"krf{i}", [128, C]) for i in range(2)]
    kraw = [sbuf(p1, f"kraw{i}", [128, C]) for i in range(2)]
    qraw = sbuf(p1, "qraw", [128, C])
    ksc = [sbuf(p1, f"ksc{i}", [128, C]) for i in range(2)]
    vtm = [sbuf(p1, f"vtm{i}", [128, C]) for i in range(2)]
    q8 = sbuf(p1, "q8", [128, C])
    qsh = sbuf(p1, "qsh", [128, C])
    fm16 = sbuf(p1, "fm16", [128, C], FP16)
    fm16b = sbuf(p1, "fm16b", [128, C], FP16)
    fm32 = sbuf(p1, "fm32", [128, C])

    p_trh = psum(p1, "p_trh", [128, C], FP16)
    p_trl = psum(p1, "p_trl", [128, C], FP16)
    p_k = psum(p1, "p_k", [128, C])
    p_v = psum(p1, "p_v", [128, C])
    p_q = psum(p1, "p_q", [128, C])

    wsl1 = mk_wslots(p1, ("a", "b", "c"))
    load_weight(wsl1, "a", wk_t)
    load_weight(wsl1, "b", wv_t)
    load_weight(wsl1, "c", wq_t)

    def proj_tm(slots, ps_buf, slot, fh, fl):
        hiw, low = slots[0][slot], slots[1][slot]
        for ccx in range(8):
            lh = fh.ap[:, ccx * 128:(ccx + 1) * 128]
            ll = fl.ap[:, ccx * 128:(ccx + 1) * 128]
            for half in range(2):
                wh = hiw.ap[:, ccx * C + half * 512:ccx * C + (half + 1) * 512]
                wl = low.ap[:, ccx * C + half * 512:ccx * C + (half + 1) * 512]
                po = ps_buf.ap[:, half * 512:(half + 1) * 512]
                P.op("tensor", lambda e, lh=lh, wh=wh, po=po, s=(ccx == 0):
                     e.matmul(po, lh, wh, start=s, stop=False),
                     reads=[fh, fl, hiw, low],
                     extra=P.war(ps_buf) if (ccx == 0 and half == 0) else (),
                     inc=False)
                P.op("tensor", lambda e, lh=lh, wl=wl, po=po:
                     e.matmul(po, lh, wl, start=False, stop=False), inc=False)
            for half in range(2):
                wh = hiw.ap[:, ccx * C + half * 512:ccx * C + (half + 1) * 512]
                po = ps_buf.ap[:, half * 512:(half + 1) * 512]
                last = ccx == 7 and half == 1
                P.op("tensor", lambda e, ll=ll, wh=wh, po=po, s=(ccx == 7):
                     e.matmul(po, ll, wh, start=False, stop=s),
                     reads=[fh, fl, hiw, low] if last else (),
                     writes=[ps_buf] if last else (), inc=last)

    def rope(ps_buf, cos_sb, sin_sb, out_buf, out_ap, rt1, rt2):
        ps3 = ps_buf.ap[:].rearrange("p (h d) -> p h d", h=H_ATT)
        cosv = cos_sb.ap[:].unsqueeze(1).broadcast_to([128, H_ATT, 64])
        sinv4 = sin_sb.ap[:].unsqueeze(1).broadcast_to(
            [128, H_ATT, 64]).rearrange("p h (a b) -> p h a b", a=2)
        P.op("vector", lambda e: e.tensor_mul(
            rt1.ap[:].rearrange("p (h d) -> p h d", h=H_ATT), ps3, cosv),
            reads=[ps_buf, cos_sb], writes=[rt1])
        swap = bass.AP(ps_buf.ap.tensor, ps_buf.ap.offset + 32,
                       [list(ps_buf.ap.ap[0]), [64, H_ATT], [-32, 2], [1, 32]])
        P.op("vector", lambda e: e.tensor_mul(
            rt2.ap[:].rearrange("p (h a b) -> p h a b", h=H_ATT, a=2),
            swap, sinv4),
            reads=[ps_buf, sin_sb], writes=[rt2])
        r4 = rt1.ap[:].rearrange("p (h a b) -> p h a b", h=H_ATT, a=2)
        o4 = rt2.ap[:].rearrange("p (h a b) -> p h a b", h=H_ATT, a=2)
        d4 = out_ap.rearrange("p (h a b) -> p h a b", h=H_ATT, a=2)
        P.op("vector", lambda e: e.tensor_sub(d4[:, :, 0], r4[:, :, 0],
                                              o4[:, :, 0]),
             reads=[rt1, rt2], writes=[out_buf])
        P.op("vector", lambda e: e.tensor_add(d4[:, :, 1], r4[:, :, 1],
                                              o4[:, :, 1]),
             reads=[rt1, rt2], writes=[out_buf])

    def store_fm16(dram_buf, i, src_buf):
        view = src_buf.ap[:].rearrange("p (cc t) -> p cc t", cc=8)
        d4 = dram_buf.ap[i].rearrange("(hh sub) d t -> sub d hh t", sub=2)
        P.dma("scalar", ad, d4[0], view[0:64], reads=[src_buf],
              writes=[dram_buf])
        P.dma("scalar", ad, d4[1], view[64:128], reads=[src_buf],
              writes=[dram_buf])

    if phases >= 1:
        for tt in range(PCH, NCH):
            xi = x_in[tt % 3]
            P.dma("sync", sd, xi.ap[:],
                  x_attn.rearrange("(n p) c -> n p c", p=128)[tt], writes=[xi])
            cs, sn = cosl[tt % 2], sinl[tt % 2]
            P.dma("sync", sd2, cs.ap[:],
                  cos_t.rearrange("(n p) d -> n p d", p=128)[tt], writes=[cs])
            P.dma("sync", sd2, sn.ap[:],
                  sin_t.rearrange("(n p) d -> n p d", p=128)[tt], writes=[sn])
            xn = xn_tm[tt % 2]
            t16a, t16b = t16al[tt % 2], t16bl[tt % 2]
            ln_tile(ph, xi, xi.ap[:], xn, xn.ap[:])
            split16(xn, xn.ap[:], t16a, t16a.ap[:], t16b, t16b.ap[:],
                    sub_eng="gpsimd")
            transpose16(t16a, t16a.ap, p_trh)
            transpose16(t16b, t16b.ap, p_trl)
            fh, fl = xf_hi[tt % 2], xf_lo[tt % 2]
            P.op("scalar", lambda e, fh=fh: e.copy(fh.ap[:], p_trh.ap[:]),
                 reads=[p_trh], writes=[fh])
            P.op("scalar", lambda e, fl=fl: e.copy(fl.ap[:], p_trl.ap[:]),
                 reads=[p_trl], writes=[fl])
            own_tile = True

            proj_tm(wsl1, p_k, "a", fh, fl)
            proj_tm(wsl1, p_v, "b", fh, fl)
            if own_tile:
                proj_tm(wsl1, p_q, "c", fh, fl)

            kr_ = kraw[tt % 2]
            P.op("scalar", lambda e, kr_=kr_: e.copy(kr_.ap[:], p_k.ap[:]),
                 reads=[p_k], writes=[kr_])
            krf = krfl[tt % 2]
            rope(kr_, cs, sn, krf, krf.ap[:], rt1l[tt % 2], rt2l[tt % 2])
            ks = ksc[tt % 2]
            P.op("vector", lambda e, ks=ks, krf=krf: e.tensor_mul(
                ks.ap[:], krf.ap[:], gk_b.ap[:]),
                 reads=[krf, gk_b], writes=[ks])
            vt = vtm[tt % 2]
            P.op("scalar", lambda e, vt=vt: e.copy(vt.ap[:], p_v.ap[:]),
                 reads=[p_v], writes=[vt])
            P.dma("scalar", ad, kv_own_d.ap[0, tt - PCH], ks.ap[:],
                  reads=[ks], writes=[kv_own_d])
            P.dma("scalar", ad, kv_own_d.ap[1, tt - PCH], vt.ap[:],
                  reads=[vt], writes=[kv_own_d])

            if own_tile:
                i = tt - PCH
                split16(krf, krf.ap[:], t16a, t16a.ap[:], t16b, t16b.ap[:])
                transpose16(t16a, t16a.ap, p_trh)
                transpose16(t16b, t16b.ap, p_trl)
                P.op("vector", lambda e: e.tensor_copy(fm16.ap[:], p_trh.ap[:]),
                     reads=[p_trh], writes=[fm16])
                P.op("vector", lambda e: e.tensor_copy(fm16b.ap[:],
                                                       p_trl.ap[:]),
                     reads=[p_trl], writes=[fm16b])
                store_fm16(kfh_d, i, fm16)
                store_fm16(kfl_d, i, fm16b)
                P.op("scalar", lambda e: e.copy(qraw.ap[:], p_q.ap[:]),
                     reads=[p_q], writes=[qraw])
                rope(qraw, cs, sn, q8, q8.ap[:], rt1l[(tt + 1) % 2],
                     rt2l[(tt + 1) % 2])
                P.op("vector", lambda e: e.tensor_mul(qsh.ap[:], q8.ap[:],
                                                      gq_b.ap[:]),
                     reads=[q8, gq_b], writes=[qsh])
                split16(q8, q8.ap[:], t16a, t16a.ap[:], t16b, t16b.ap[:])
                transpose16(t16a, t16a.ap, p_trh)
                transpose16(t16b, t16b.ap, p_trl)
                P.op("vector", lambda e: e.tensor_copy(fm16.ap[:], p_trh.ap[:]),
                     reads=[p_trh], writes=[fm16])
                P.op("vector", lambda e: e.tensor_copy(fm16b.ap[:],
                                                       p_trl.ap[:]),
                     reads=[p_trl], writes=[fm16b])
                store_fm16(q8h_d, i, fm16)
                store_fm16(q8l_d, i, fm16b)
                for j in range(8):
                    last = j == 7
                    P.op("tensor",
                         lambda e, j=j: e.transpose(
                             p_k.ap[:, j * 128:(j + 1) * 128],
                             qsh.ap[:, j * 128:(j + 1) * 128], id32_b.ap[:]),
                         reads=[qsh, id32_b],
                         extra=P.war(p_k) if j == 0 else (),
                         writes=[p_k] if last else (), inc=last)
                P.op("vector", lambda e: e.tensor_copy(fm32.ap[:], p_k.ap[:]),
                     reads=[p_k], writes=[fm32])
                v32 = fm32.ap[:].rearrange("p (cc t) -> p cc t", cc=8)
                d4 = qs_d.ap[i].rearrange("(hh sub) d t -> sub d hh t", sub=2)
                P.dma("scalar", ad, d4[0], v32[0:64], reads=[fm32],
                      writes=[qs_d])
                P.dma("scalar", ad, d4[1], v32[64:128], reads=[fm32],
                      writes=[qs_d])

        P.op("gpsimd", lambda e: e.collective_compute(
            "AllGather", ALU.bypass,
            replica_groups=[[0, 1, 2, 3], [4, 5, 6, 7]],
            ins=[kv_own_d.ap], outs=[kv_all_d.ap]),
            reads=[kv_own_d], writes=[kv_all_d], sem=cc, step=1)
        P.barrier(skip_cc=True)
        p1.close()

    if phases >= 2:
        p2s = ExitStack()
        S_b = sbuf(p2s, "S_b", [64, C])
        Sg_b = sbuf(p2s, "Sg_b", [64, C])
        gl_b = sbuf(p2s, "gl_b", [64, C])
        wA_b = sbuf(p2s, "wA_b", [128, NCH * H_ATT])
        dt_b = sbuf(p2s, "dt_b", [L, H_ATT * L])
        P.dma("sync", sd, gl_b.ap[:], gl_c, writes=[gl_b])
        P.dma("sync", sd, wA_b.ap[:], wstA_c, writes=[wA_b])
        P.dma("sync", sd, dt_b.ap[:], dt_mask, writes=[dt_b])
        ks_in = [sbuf(p2s, f"ks_in{i}", [128, C]) for i in range(4)]
        v_in = [sbuf(p2s, f"v_in{i}", [128, C]) for i in range(4)]
        kw_b = [sbuf(p2s, f"kw_b{i}", [128, C]) for i in range(2)]
        kfh_in = sbuf(p2s, "kfh_in", [64, H_ATT * L], FP16)
        kfl_in = sbuf(p2s, "kfl_in", [64, H_ATT * L], FP16)
        q8h_in = sbuf(p2s, "q8h_in", [64, H_ATT * L], FP16)
        q8l_in = sbuf(p2s, "q8l_in", [64, H_ATT * L], FP16)
        qs_in = sbuf(p2s, "qs_in", [64, H_ATT * L])
        am_sb = sbuf(p2s, "am_sb", [128, H_ATT * L])
        ret_sb = [sbuf(p2s, f"ret_sb{i}", [128, C]) for i in range(2)]
        p_S = psum(p2s, "p_S", [64, C])
        p_A = psum(p2s, "p_A", [128, H_ATT * L])
        p_O = psum(p2s, "p_O", [128, C])

        P.op("vector", lambda e: e.memset(S_b.ap[:], 0.0), writes=[S_b])

        def state_mms(ksb, vb):
            for h in range(H_ATT):
                P.op("tensor",
                     lambda e, h=h, ksb=ksb, vb=vb: e.matmul(
                         p_S.ap[:, h * 64:(h + 1) * 64],
                         ksb.ap[:, h * 64:(h + 1) * 64],
                         vb.ap[:, h * 64:(h + 1) * 64],
                         start=True, stop=True),
                     reads=[ksb, vb],
                     extra=P.war(p_S) if h == 0 else (),
                     writes=[p_S] if h == H_ATT - 1 else (), inc=True)

        for i in range(OCH):
            ksb, vb = ks_in[i % 4], v_in[i % 4]
            P.dma("sync", sd, ksb.ap[:], kv_own_d.ap[0, i],
                  reads=[kv_own_d], writes=[ksb])
            P.dma("sync", sd, vb.ap[:], kv_own_d.ap[1, i],
                  reads=[kv_own_d], writes=[vb])
            for db, db_d in ((kfh_in, kfh_d), (kfl_in, kfl_d),
                             (q8h_in, q8h_d), (q8l_in, q8l_d),
                             (qs_in, qs_d)):
                P.dma("sync", sd2, db.ap[:].rearrange(
                    "d (h t) -> d h t", h=H_ATT),
                    db_d.ap[i].rearrange("h d t -> d h t"),
                    reads=[db_d], writes=[db])
            for h in range(H_ATT):
                sl = h * L
                mm3(p_A.ap[:, sl:sl + L],
                    kfh_in.ap[:, h * L:(h + 1) * L],
                    kfl_in.ap[:, h * L:(h + 1) * L],
                    q8h_in.ap[:, h * L:(h + 1) * L],
                    q8l_in.ap[:, h * L:(h + 1) * L],
                    start=True, stop=True,
                    reads=[kfh_in, kfl_in, q8h_in, q8l_in],
                    writes=[p_A] if h == H_ATT - 1 else (), inc=True,
                    ps_buf=p_A if h == 0 else None)
            P.op("vector",
                 lambda e: e.tensor_mul(am_sb.ap[:], p_A.ap[:], dt_b.ap[:]),
                 reads=[p_A, dt_b], writes=[am_sb])
            for h in range(H_ATT):
                sl = h * L
                P.op("tensor",
                     lambda e, h=h, sl=sl, vb=vb: e.matmul(
                         p_O.ap[:, h * 64:(h + 1) * 64],
                         am_sb.ap[:, sl:sl + L],
                         vb.ap[:, h * 64:(h + 1) * 64],
                         start=True, stop=False),
                     reads=[am_sb, vb],
                     extra=P.war(p_O) if h == 0 else (), inc=False)
                P.op("tensor",
                     lambda e, h=h: e.matmul(
                         p_O.ap[:, h * 64:(h + 1) * 64],
                         qs_in.ap[:, h * L:(h + 1) * L],
                         S_b.ap[:, h * 64:(h + 1) * 64],
                         start=False, stop=True),
                     reads=[qs_in, S_b],
                     writes=[p_O] if h == H_ATT - 1 else (), inc=True)
            rsb = ret_sb[i % 2]
            P.op("scalar", lambda e, rsb=rsb: e.copy(rsb.ap[:], p_O.ap[:]),
                 reads=[p_O], writes=[rsb])
            P.dma("scalar", ad, ret_d.ap[i], rsb.ap[:], reads=[rsb],
                  writes=[ret_d])
            state_mms(ksb, vb)
            P.op("vector", lambda e: e.tensor_mul(Sg_b.ap[:], S_b.ap[:],
                                                  gl_b.ap[:]),
                 reads=[S_b, gl_b], writes=[Sg_b])
            P.op("vector", lambda e: e.tensor_add(S_b.ap[:], Sg_b.ap[:],
                                                  p_S.ap[:]),
                 reads=[Sg_b, p_S], writes=[S_b])

        for j in range(NCH):
            ksb, vb = ks_in[j % 4], v_in[j % 4]
            kw = kw_b[j % 2]
            P.dma("sync", sd, ksb.ap[:], kv_all_d.ap[j // OCH, 0, j % OCH],
                  reads=[kv_all_d], writes=[ksb],
                  extra=[(cc, 1)])
            P.dma("scalar", ad, vb.ap[:], kv_all_d.ap[j // OCH, 1, j % OCH],
                  reads=[kv_all_d], writes=[vb], extra=[(cc, 1)])
            P.op("vector", lambda e, j=j, ksb=ksb, kw=kw: e.tensor_mul(
                kw.ap[:].rearrange("d (h x) -> d h x", h=H_ATT),
                ksb.ap[:].rearrange("d (h x) -> d h x", h=H_ATT),
                wA_b.ap[:, j * H_ATT:(j + 1) * H_ATT].unsqueeze(2)
                .broadcast_to([128, H_ATT, DH_ATT])),
                reads=[ksb, wA_b], writes=[kw])
            for h in range(H_ATT):
                P.op("tensor",
                     lambda e, h=h, j=j, kw=kw, vb=vb: e.matmul(
                         p_S.ap[:, h * 64:(h + 1) * 64],
                         kw.ap[:, h * 64:(h + 1) * 64],
                         vb.ap[:, h * 64:(h + 1) * 64],
                         start=(j == 0 and h % 8 == 0),
                         stop=(j == NCH - 1 and h % 8 == 7)),
                     reads=[kw, vb],
                     extra=P.war(p_S) if (j == 0 and h == 0) else (),
                     writes=[p_S] if (j == NCH - 1 and h == H_ATT - 1)
                     else (), inc=True)
        P.op("scalar", lambda e: e.copy(Sg_b.ap[:], p_S.ap[:]),
             reads=[p_S], writes=[Sg_b])
        P.dma("scalar", ad, sa_d.ap, Sg_b.ap[:], reads=[Sg_b], writes=[sa_d])
        P.barrier()
        p2s.close()

    if phases >= 3:
        p3s = ExitStack()
        gnw_b = sbuf(p3s, "gnw_b", [128, C])
        gnb_b = sbuf(p3s, "gnb_b", [128, C])
        Tc_b = sbuf(p3s, "Tc_b", [64, C])
        Tc2_b = sbuf(p3s, "Tc2_b", [64, C])
        gl3_b = sbuf(p3s, "gl3_b", [64, C])
        qs3_in = [sbuf(p3s, f"qs3_{i}", [64, H_ATT * L]) for i in range(2)]
        ri2_b = [sbuf(p3s, f"ri2_{i}", [128, C]) for i in range(2)]
        p_corr = psum(p3s, "p_corr", [128, C])
        P.dma("sync", sd, gnw_b.ap[:], gnw_rep, writes=[gnw_b])
        P.dma("sync", sd, gnb_b.ap[:], gnb_rep, writes=[gnb_b])
        P.dma("sync", sd, gl3_b.ap[:], gl_c, writes=[gl3_b])
        P.dma("sync", sd, Tc_b.ap[:], sa_d.ap, reads=[sa_d], writes=[Tc_b])
        ret_in = [sbuf(p3s, f"ret_in{i}", [128, C]) for i in range(2)]
        xo_in = [sbuf(p3s, f"xo_in{i}", [128, C]) for i in range(2)]
        gstatl = [sbuf(p3s, f"gstat{i}", [128, 64]) for i in range(2)]
        gd_bl = [sbuf(p3s, f"gd_b{i}", [128, C]) for i in range(2)]
        gsq_bl = [sbuf(p3s, f"gsq_b{i}", [128, C]) for i in range(2)]
        gt_bl = [sbuf(p3s, f"gt_b{i}", [128, C]) for i in range(2)]
        gn_bl = [sbuf(p3s, f"gn_b{i}", [128, C]) for i in range(2)]
        g16al = [sbuf(p3s, f"g16a{i}", [128, C], FP16) for i in range(2)]
        g16bl = [sbuf(p3s, f"g16b{i}", [128, C], FP16) for i in range(2)]
        gfhl = [sbuf(p3s, f"gfh{i}", [128, C], FP16) for i in range(2)]
        gfll = [sbuf(p3s, f"gfl{i}", [128, C], FP16) for i in range(2)]
        x1_sb = [sbuf(p3s, f"x1_sb{i}", [128, C]) for i in range(2)]
        p_ghl = [psum(p3s, f"p_gh{i}", [128, C], FP16) for i in range(2)]
        p_gll = [psum(p3s, f"p_gl{i}", [128, C], FP16) for i in range(2)]
        p_W = psum(p3s, "p_W", [128, C])
        wsl3 = mk_wslots(p3s, ("a",))
        load_weight(wsl3, "a", wo_t)
        HA = H_ATT

        Tcl = [Tc_b, Tc2_b]

        def c_loads(i):
            ri = ret_in[i % 2]
            P.dma("sync", sd, ri.ap[:], ret_d.ap[i], reads=[ret_d], writes=[ri])
            xo = xo_in[i % 2]
            P.dma("sync", sd2, xo.ap[:],
                  x_attn.rearrange("(n p) c -> n p c", p=128)[PCH + i],
                  writes=[xo])
            qs3 = qs3_in[i % 2]
            P.dma("sync", sd2, qs3.ap[:].rearrange(
                "d (h t) -> d h t", h=H_ATT),
                qs_d.ap[i].rearrange("h d t -> d h t"),
                reads=[qs_d], writes=[qs3])

        def corr_mms(i):
            qs3, Tc = qs3_in[i % 2], Tcl[i % 2]
            for h in range(H_ATT):
                P.op("tensor",
                     lambda e, h=h, qs3=qs3, Tc=Tc: e.matmul(
                         p_corr.ap[:, h * 64:(h + 1) * 64],
                         qs3.ap[:, h * L:(h + 1) * L],
                         Tc.ap[:, h * 64:(h + 1) * 64],
                         start=True, stop=True),
                     reads=[qs3, Tc],
                     extra=P.war(p_corr) if h == 0 else (),
                     writes=[p_corr] if h == H_ATT - 1 else (), inc=True)

        c_loads(0)
        corr_mms(0)
        for i in range(OCH):
            gstat = gstatl[i % 2]
            gd_b, gsq_b = gd_bl[i % 2], gsq_bl[i % 2]
            gt_b, gn_b = gt_bl[i % 2], gn_bl[i % 2]
            g16a, g16b = g16al[i % 2], g16bl[i % 2]
            gfh, gfl = gfhl[i % 2], gfll[i % 2]
            p_gh, p_gl = p_ghl[i % 2], p_gll[i % 2]
            xo = xo_in[i % 2]
            ri2 = ri2_b[i % 2]
            ri = ret_in[i % 2]
            P.op("vector", lambda e, ri2=ri2, ri=ri: e.tensor_add(
                ri2.ap[:], ri.ap[:], p_corr.ap[:]),
                reads=[ri, p_corr], writes=[ri2])
            if i < OCH - 1:
                P.op("vector", lambda e, i=i: e.tensor_mul(
                    Tcl[(i + 1) % 2].ap[:], Tcl[i % 2].ap[:], gl3_b.ap[:]),
                    reads=[Tcl[i % 2], gl3_b], writes=[Tcl[(i + 1) % 2]])
                c_loads(i + 1)
                corr_mms(i + 1)
            ri = ri2
            r3 = ri.ap[:].rearrange("p (h d) -> p h d", h=HA)
            P.op("vector", lambda e, r3=r3, gstat=gstat: e.reduce_sum(
                gstat.ap[:, 0:HA], r3, axis=AX.X), reads=[ri], writes=[gstat])
            P.op("vector", lambda e, gstat=gstat: e.tensor_scalar_mul(
                gstat.ap[:, HA:2 * HA], gstat.ap[:, 0:HA], 1.0 / DH_ATT),
                reads=[gstat], writes=[gstat])
            mub = gstat.ap[:, HA:2 * HA].unsqueeze(2).broadcast_to(
                [128, HA, DH_ATT])
            P.op("vector", lambda e, r3=r3, mub=mub, gd_b=gd_b: e.tensor_sub(
                gd_b.ap[:].rearrange("p (h d) -> p h d", h=HA), r3, mub),
                reads=[ri, gstat], writes=[gd_b])
            P.op("scalar", lambda e, gsq_b=gsq_b, gd_b=gd_b: e.activation(
                gsq_b.ap[:], gd_b.ap[:], AF.Square),
                 reads=[gd_b], writes=[gsq_b])
            P.op("vector", lambda e, gstat=gstat, gsq_b=gsq_b: e.reduce_sum(
                gstat.ap[:, 2 * HA:3 * HA],
                gsq_b.ap[:].rearrange("p (h d) -> p h d", h=HA), axis=AX.X),
                reads=[gsq_b], writes=[gstat])
            P.op("vector", lambda e, gstat=gstat: e.tensor_scalar(
                gstat.ap[:, 3 * HA:4 * HA], gstat.ap[:, 2 * HA:3 * HA],
                1.0 / DH_ATT, EPS, op0=ALU.mult, op1=ALU.add),
                reads=[gstat], writes=[gstat])
            P.op("vector", lambda e, gstat=gstat: e.reciprocal(
                gstat.ap[:, 0:HA], gstat.ap[:, 3 * HA:4 * HA]),
                reads=[gstat], writes=[gstat])
            P.op("scalar", lambda e, gstat=gstat: e.activation(
                gstat.ap[:, HA:2 * HA], gstat.ap[:, 0:HA], AF.Sqrt),
                reads=[gstat], writes=[gstat])
            rb = gstat.ap[:, HA:2 * HA].unsqueeze(2).broadcast_to(
                [128, HA, DH_ATT])
            P.op("vector", lambda e, rb=rb, gt_b=gt_b, gd_b=gd_b: e.tensor_mul(
                gt_b.ap[:].rearrange("p (h d) -> p h d", h=HA),
                gd_b.ap[:].rearrange("p (h d) -> p h d", h=HA), rb),
                reads=[gd_b, gstat], writes=[gt_b])
            P.op("vector", lambda e, gn_b=gn_b, gt_b=gt_b: e.tensor_mul(
                gn_b.ap[:], gt_b.ap[:], gnw_b.ap[:]),
                 reads=[gt_b, gnw_b], writes=[gn_b])
            P.op("vector", lambda e, gn_b=gn_b: e.tensor_add(
                gn_b.ap[:], gn_b.ap[:], gnb_b.ap[:]),
                 reads=[gn_b, gnb_b], writes=[gn_b])
            split16(gn_b, gn_b.ap[:], g16a, g16a.ap[:], g16b, g16b.ap[:],
                    sub_eng="gpsimd")
            transpose16(g16a, g16a.ap, p_gh)
            transpose16(g16b, g16b.ap, p_gl)
            P.op("scalar", lambda e, gfh=gfh, p_gh=p_gh: e.copy(
                gfh.ap[:], p_gh.ap[:]),
                 reads=[p_gh], writes=[gfh])
            P.op("scalar", lambda e, gfl=gfl, p_gl=p_gl: e.copy(
                gfl.ap[:], p_gl.ap[:]),
                 reads=[p_gl], writes=[gfl])
            proj_tm(wsl3, p_W, "a", gfh, gfl)
            x1 = x1_sb[i % 2]
            P.op("vector", lambda e, x1=x1, xo=xo: e.tensor_add(
                x1.ap[:], p_W.ap[:], xo.ap[:]),
                reads=[p_W, xo], writes=[x1])
            P.dma("scalar", ad, ag_in.ap.rearrange("(n p) c -> n p c", p=128)[i],
                  x1.ap[:], reads=[x1], writes=[ag_in])
            if "x1" in dbg_aps:
                P.dma("scalar", ad,
                      dbg_aps["x1"].rearrange("(n p) c -> n p c", p=128)[i],
                      x1.ap[:], reads=[x1])
        P.barrier()
        p3s.close()

    if phases >= 4:
        p4s = ExitStack()
        fnw_b = sbuf(p4s, "fnw_b", [128, C])
        fnb_b = sbuf(p4s, "fnb_b", [128, C])
        fvb_b = sbuf(p4s, "fvb_b", [128, C])
        fqb_b = sbuf(p4s, "fqb_b", [128, 8])
        fkb_b = sbuf(p4s, "fkb_b", [128, 8])
        for bb, ap in ((fnw_b, fnw_rep), (fnb_b, fnb_rep), (fvb_b, fvb_rep),
                       (fqb_b, fqb_fm), (fkb_b, fkb_fm)):
            P.dma("sync", sd, bb.ap[:], ap, writes=[bb])
        xgrpl = [sbuf(p4s, f"xgrp{i}", [128, 4 * C]) for i in range(2)]
        sq4l = [sbuf(p4s, f"sq4_{i}", [128, C]) for i in range(2)]
        stat4l = [sbuf(p4s, f"stat4_{i}", [128, 64]) for i in range(2)]
        s4dl = [sbuf(p4s, f"s4d{i}", [128, C]) for i in range(2)]
        xff_tl = [sbuf(p4s, f"xff_t{i}", [128, C]) for i in range(2)]
        f16al = [sbuf(p4s, f"f16a{i}", [128, C], FP16) for i in range(2)]
        f16bl = [sbuf(p4s, f"f16b{i}", [128, C], FP16) for i in range(2)]
        xg_hil = [sbuf(p4s, f"xg_hi{i}", [128, 8 * 512], FP16) for i in range(2)]
        xg_lol = [sbuf(p4s, f"xg_lo{i}", [128, 8 * 512], FP16) for i in range(2)]
        qf_tl = [sbuf(p4s, f"qf_t{i}", [128, 512]) for i in range(2)]
        q16hl = [sbuf(p4s, f"q16h{i}", [128, 512], FP16) for i in range(2)]
        q16ll = [sbuf(p4s, f"q16l{i}", [128, 512], FP16) for i in range(2)]
        p_trh4l = [psum(p4s, f"p_trh4{i}", [128, C], FP16) for i in range(2)]
        p_trl4l = [psum(p4s, f"p_trl4{i}", [128, C], FP16) for i in range(2)]
        p_p4l = [psum(p4s, f"p_p4{i}", [128, 512]) for i in range(2)]
        wsl4 = mk_wslots(p4s, ("a", "b"))

        def ln2_group(src_dram, src_buf, g, dst_dbg=None):
            gp = g % 2
            xg_hi, xg_lo = xg_hil[gp], xg_lol[gp]
            xgr = xgrpl[gp]
            st4 = stat4l[gp]
            P.dma("sync", sd, xgr.ap[:].rearrange("p (u c) -> p u c", u=4),
                  src_dram.rearrange("(gg u p) c -> gg p u c", u=4, p=128)[g],
                  reads=[src_buf] if src_buf is not None else (),
                  writes=[xgr])
            x4 = xgr.ap[:].rearrange("p (u c) -> p u c", u=4)
            P.op("vector", lambda e, st4=st4, x4=x4: e.reduce_sum(
                st4.ap[:, 0:4], x4, axis=AX.X), reads=[xgr], writes=[st4])
            for u in range(4):
                sq4 = sq4l[u % 2]
                P.op("scalar", lambda e, sq4=sq4, xgr=xgr, u=u: e.activation(
                    sq4.ap[:], xgr.ap[:, u * C:(u + 1) * C], AF.Square),
                    reads=[xgr], writes=[sq4])
                P.op("vector", lambda e, st4=st4, sq4=sq4, u=u: e.reduce_sum(
                    st4.ap[:, 4 + u:5 + u], sq4.ap[:], axis=AX.X),
                    reads=[sq4], writes=[st4])
            P.op("vector", lambda e, st4=st4: e.tensor_scalar(
                st4.ap[:, 8:12], st4.ap[:, 0:4], 1.0 / C, None, op0=ALU.mult),
                reads=[st4], writes=[st4])
            P.op("vector", lambda e, st4=st4: e.tensor_mul(
                st4.ap[:, 12:16], st4.ap[:, 8:12], st4.ap[:, 8:12]),
                reads=[st4], writes=[st4])
            P.op("vector", lambda e, st4=st4: e.tensor_scalar(
                st4.ap[:, 16:20], st4.ap[:, 4:8], 1.0 / C, None, op0=ALU.mult),
                reads=[st4], writes=[st4])
            P.op("vector", lambda e, st4=st4: e.tensor_sub(
                st4.ap[:, 20:24], st4.ap[:, 16:20], st4.ap[:, 12:16]),
                reads=[st4], writes=[st4])
            P.op("vector", lambda e, st4=st4: e.tensor_scalar(
                st4.ap[:, 24:28], st4.ap[:, 20:24], EPS, None, op0=ALU.add),
                reads=[st4], writes=[st4])
            P.op("vector", lambda e, st4=st4: e.reciprocal(
                st4.ap[:, 28:32], st4.ap[:, 24:28]),
                reads=[st4], writes=[st4])
            P.op("scalar", lambda e, st4=st4: e.activation(
                st4.ap[:, 32:36], st4.ap[:, 28:32], AF.Sqrt),
                reads=[st4], writes=[st4])
            P.op("vector", lambda e, st4=st4: e.tensor_mul(
                st4.ap[:, 36:40], st4.ap[:, 20:24], st4.ap[:, 28:32]),
                reads=[st4], writes=[st4])
            P.op("vector", lambda e, st4=st4: e.tensor_scalar(
                st4.ap[:, 40:44], st4.ap[:, 36:40], EPS, None, op0=ALU.add),
                reads=[st4], writes=[st4])
            P.op("vector", lambda e, st4=st4: e.reciprocal(
                st4.ap[:, 44:48], st4.ap[:, 40:44]),
                reads=[st4], writes=[st4])
            P.op("scalar", lambda e, st4=st4: e.activation(
                st4.ap[:, 48:52], st4.ap[:, 44:48], AF.Sqrt),
                reads=[st4], writes=[st4])
            P.op("vector", lambda e, st4=st4: e.tensor_mul(
                st4.ap[:, 52:56], st4.ap[:, 32:36], st4.ap[:, 48:52]),
                reads=[st4], writes=[st4])
            for u in range(4):
                par = u % 2
                xff_t = xff_tl[par]
                f16a, f16b = f16al[par], f16bl[par]
                p_trh4, p_trl4 = p_trh4l[par], p_trl4l[par]
                xsl = xgr.ap[:, u * C:(u + 1) * C]
                s4d = s4dl[par]
                P.op("vector", lambda e, st4=st4, xsl=xsl, u=u,
                     s4d=s4d: e.tensor_scalar(
                         s4d.ap[:], xsl, st4.ap[:, 8 + u:9 + u],
                         st4.ap[:, 52 + u:53 + u],
                         op0=ALU.subtract, op1=ALU.mult),
                     reads=[xgr, st4], writes=[s4d])
                P.op("vector", lambda e, xff_t=xff_t, s4d=s4d: e.tensor_mul(
                    xff_t.ap[:], s4d.ap[:], fnw_b.ap[:]),
                    reads=[s4d, fnw_b], writes=[xff_t])
                P.op("vector", lambda e, xff_t=xff_t: e.tensor_add(
                    xff_t.ap[:], xff_t.ap[:], fnb_b.ap[:]),
                    reads=[xff_t, fnb_b], writes=[xff_t])
                if dst_dbg is not None:
                    P.dma("scalar", ad, dst_dbg[g * 4 + u], xff_t.ap[:],
                          reads=[xff_t])
                split16(xff_t, xff_t.ap[:], f16a, f16a.ap[:], f16b,
                        f16b.ap[:], sub_eng="gpsimd")
                transpose16(f16a, f16a.ap, p_trh4)
                transpose16(f16b, f16b.ap, p_trl4)
                for (pbuf, dbuf) in ((p_trh4, xg_hi), (p_trl4, xg_lo)):
                    P.op("scalar",
                         lambda e, pbuf=pbuf, dbuf=dbuf, u=u: e.copy(
                             dbuf.ap[:].rearrange(
                                 "p (cc t) -> p cc t", cc=8)[:, :,
                                                             u * 128:(u + 1) * 128],
                             pbuf.ap[:].rearrange("p (cc t) -> p cc t", cc=8)),
                         reads=[pbuf], writes=[dbuf] if u == 3 else ())
            return xg_hi, xg_lo

        load_weight(wsl4, "a", fkw_t)
        load_weight(wsl4, "b", fvw_t)
        kf_tl = [sbuf(p4s, f"kf_t{i}", [128, 512]) for i in range(2)]
        kf16_l = [sbuf(p4s, f"kf16{i}", [128, 512], FP16) for i in range(2)]
        kf16l_l = [sbuf(p4s, f"kf16l{i}", [128, 512], FP16) for i in range(2)]
        vt4l = [sbuf(p4s, f"vt4{i}", [128, C]) for i in range(2)]
        p_v4 = psum(p4s, "p_v4", [128, C])
        for g in range(OG):
            xg_hi, xg_lo = ln2_group(ag_in.ap, ag_in, g,
                                     dst_dbg=dbg_aps.get("xff") and
                                     dbg_aps["xff"].rearrange(
                                         "(n p) c -> n p c", p=128))
            mcc = 8
            for h in range(H_FFN):
                p_p4 = p_p4l[h % 2]
                kf_t = kf_tl[h % 2]
                kf16, kf16l = kf16_l[h % 2], kf16l_l[h % 2]
                for ccx in range(mcc):
                    last = ccx == mcc - 1
                    mm3(p_p4.ap[:],
                        wsl4[0]["a"].ap[:, ccx * C + h * 128:ccx * C + (h + 1) * 128],
                        wsl4[1]["a"].ap[:, ccx * C + h * 128:ccx * C + (h + 1) * 128],
                        xg_hi.ap[:, ccx * 512:(ccx + 1) * 512],
                        xg_lo.ap[:, ccx * 512:(ccx + 1) * 512],
                        start=(ccx == 0), stop=last,
                        reads=[wsl4[0]["a"], wsl4[1]["a"], xg_hi, xg_lo],
                        writes=[p_p4] if last else (), inc=last,
                        ps_buf=p_p4 if ccx == 0 else None)
                P.op("vector", lambda e, h=h, p_p4=p_p4, kf_t=kf_t: e.tensor_scalar(
                    kf_t.ap[:], p_p4.ap[:], fkb_b.ap[:, h:h + 1], None,
                    op0=ALU.add), reads=[p_p4, fkb_b], writes=[kf_t])
                split16(kf_t, kf_t.ap[:], kf16, kf16.ap[:], kf16l,
                        kf16l.ap[:], sub_eng="gpsimd")
                P.dma("scalar", ad, pk_kf(0, g, h), kf16.ap[:],
                      reads=[kf16], writes=[pk_own_d])
                P.dma("scalar", ad, pk_kf(1, g, h), kf16l.ap[:],
                      reads=[kf16l], writes=[pk_own_d])
            for u in range(4):
                vt4 = vt4l[u % 2]
                for ccx in range(mcc):
                    for half in range(2):
                        last = ccx == mcc - 1 and half == 1
                        mm3(p_v4.ap[:, half * 512:(half + 1) * 512],
                            xg_hi.ap[:, ccx * 512 + u * 128:
                                     ccx * 512 + (u + 1) * 128],
                            xg_lo.ap[:, ccx * 512 + u * 128:
                                     ccx * 512 + (u + 1) * 128],
                            wsl4[0]["b"].ap[:, ccx * C + half * 512:
                                            ccx * C + (half + 1) * 512],
                            wsl4[1]["b"].ap[:, ccx * C + half * 512:
                                            ccx * C + (half + 1) * 512],
                            start=(ccx == 0), stop=(ccx == mcc - 1),
                            reads=[xg_hi, xg_lo, wsl4[0]["b"], wsl4[1]["b"]],
                            writes=[p_v4] if last else (), inc=last,
                            ps_buf=p_v4 if (ccx == 0 and half == 0) else None)
                P.op("vector", lambda e, vt4=vt4: e.tensor_add(
                    vt4.ap[:], p_v4.ap[:], fvb_b.ap[:]),
                    reads=[p_v4, fvb_b], writes=[vt4])
                P.dma("scalar", ad, vf_own_d.ap[g * 4 + u], vt4.ap[:],
                      reads=[vt4], writes=[vf_own_d])
                if "vff" in dbg_aps:
                    P.dma("scalar", ad, dbg_aps["vff"][g * 4 + u], vt4.ap[:],
                          reads=[vt4])
        P.op("gpsimd", lambda e: e.collective_compute(
            "AllGather", ALU.bypass,
            replica_groups=[[0, 1, 2, 3], [4, 5, 6, 7]],
            ins=[pk_own_d.ap], outs=[pk_all_d.ap]),
            reads=[pk_own_d], writes=[pk_all_d], sem=cc, step=1)
        P.op("gpsimd", lambda e: e.collective_compute(
            "AllGather", ALU.bypass,
            replica_groups=[[0, 1, 2, 3], [4, 5, 6, 7]],
            ins=[vf_own_d.ap], outs=[vf_all_d.ap]),
            reads=[vf_own_d], writes=[vf_all_d], sem=cc, step=1)
        load_weight(wsl4, "a", fqw_t)
        for g in range(OG):
            xg_hi, xg_lo = ln2_group(ag_in.ap, ag_in, g)
            for h in range(H_FFN):
                p_p4 = p_p4l[h % 2]
                qf_t = qf_tl[h % 2]
                for ccx in range(8):
                    last = ccx == 7
                    mm3(p_p4.ap[:],
                        wsl4[0]["a"].ap[:, ccx * C + h * 128:ccx * C + (h + 1) * 128],
                        wsl4[1]["a"].ap[:, ccx * C + h * 128:ccx * C + (h + 1) * 128],
                        xg_hi.ap[:, ccx * 512:(ccx + 1) * 512],
                        xg_lo.ap[:, ccx * 512:(ccx + 1) * 512],
                        start=(ccx == 0), stop=last,
                        reads=[wsl4[0]["a"], wsl4[1]["a"], xg_hi, xg_lo],
                        writes=[p_p4] if last else (), inc=last,
                        ps_buf=p_p4 if ccx == 0 else None)
                P.op("vector", lambda e, h=h, p_p4=p_p4, qf_t=qf_t: e.tensor_scalar(
                    qf_t.ap[:], p_p4.ap[:], fqb_b.ap[:, h:h + 1], None,
                    op0=ALU.add), reads=[p_p4, fqb_b], writes=[qf_t])
                q16h, q16l = q16hl[h % 2], q16ll[h % 2]
                split16(qf_t, qf_t.ap[:], q16h, q16h.ap[:], q16l,
                        q16l.ap[:], sub_eng="gpsimd")
                P.dma("scalar", ad, qh_d.ap[h][:, g * 512:(g + 1) * 512],
                      q16h.ap[:], reads=[q16h], writes=[qh_d])
                P.dma("scalar", ad, ql_d.ap[h][:, g * 512:(g + 1) * 512],
                      q16l.ap[:], reads=[q16l], writes=[ql_d])


        P.barrier(skip_cc=True)
        p4s.close()

    if phases >= 5:
        p5s = ExitStack()
        kv_h = [sbuf(p5s, f"kvh{i}", [128, 128], FP16) for i in range(3)]
        kv_l = [sbuf(p5s, f"kvl{i}", [128, 128], FP16) for i in range(3)]
        vv = [sbuf(p5s, f"vv{i}", [128, 128]) for i in range(3)]
        qsl_h = [sbuf(p5s, f"qslh{i}", [128, own], FP16) for i in range(2)]
        qsl_l = [sbuf(p5s, f"qsll{i}", [128, own], FP16) for i in range(2)]
        pP = [sbuf(p5s, f"pP{i}", [128, 512]) for i in range(4)]
        accg = [sbuf(p5s, f"accg{i}", [128, 512]) for i in range(QT)]
        accv = [sbuf(p5s, f"accv{i}", [128, 512]) for i in range(QT)]
        a_raw = sbuf(p5s, "a_raw", [128, 512])
        a_div = sbuf(p5s, "a_div", [128, 512])
        a_h16 = sbuf(p5s, "a_h16", [128, 512], FP16)
        a_l16 = sbuf(p5s, "a_l16", [128, 512], FP16)
        sums_sb = sbuf(p5s, "sums_sb", [1, 512])
        recip_sb = sbuf(p5s, "recip_sb", [1, 512])
        p_out = [psum(p5s, f"p_out{i}", [128, 512]) for i in range(QT)]
        p_sc = [psum(p5s, f"p_sc{i}", [128, 512]) for i in range(max(QT, 2))]

        pend_pv = []
        for h in range(H_FFN):
            qh_s, ql_s = qsl_h[h % 2], qsl_l[h % 2]
            P.dma("sync", sd2, qh_s.ap[:], qh_d.ap[h], reads=[qh_d],
                  writes=[qh_s])
            P.dma("sync", sd2, ql_s.ap[:], ql_d.ap[h], reads=[ql_d],
                  writes=[ql_s])
            for qt in range(QT):
                P.op("gpsimd", lambda e, qt=qt: e.memset(accg[qt].ap[:], 0.0),
                     writes=[accg[qt]])
                P.op("vector", lambda e, qt=qt: e.memset(accv[qt].ap[:], 0.0),
                     writes=[accv[qt]])
            for kb in range(NCH):
                sl = kb % 3
                kh, kl, vb = kv_h[sl], kv_l[sl], vv[sl]
                rk, gg, sub = kb // OCH, (kb % OCH) // 4, kb % 4
                kh_src = pk_kf(0, gg, h, rk)
                kl_src = pk_kf(1, gg, h, rk)
                P.dma("sync", sd, kh.ap[:],
                      bass.AP(kh_src.tensor, kh_src.offset + sub * 128,
                              [[512, DH_FFN], [1, 128]]),
                      reads=[pk_all_d], writes=[kh])
                P.dma("sync", sd, kl.ap[:],
                      bass.AP(kl_src.tensor, kl_src.offset + sub * 128,
                              [[512, DH_FFN], [1, 128]]),
                      reads=[pk_all_d], writes=[kl])
                P.dma("sync", sd2, vb.ap[:],
                      vf_all_d.ap[rk, kb % OCH].rearrange(
                          "p (hh d) -> p hh d", hh=H_FFN)[:, h],
                      reads=[vf_all_d], writes=[vb])
                for qt in range(QT):
                    psc = p_sc[qt]
                    co = qt * 512
                    P.op("tensor",
                         lambda e, psc=psc, kh=kh, co=co, qh_s=qh_s: e.matmul(
                             psc.ap[:], kh.ap[:], qh_s.ap[:, co:co + 512],
                             start=True, stop=False),
                         reads=[kh, kl, qh_s, ql_s],
                         extra=P.war(psc), inc=False)
                    P.op("tensor",
                         lambda e, psc=psc, kh=kh, co=co, ql_s=ql_s: e.matmul(
                             psc.ap[:], kh.ap[:], ql_s.ap[:, co:co + 512],
                             start=False, stop=False), inc=False)
                for qt in range(QT):
                    psc = p_sc[qt]
                    co = qt * 512
                    P.op("tensor",
                         lambda e, psc=psc, kl=kl, co=co, qh_s=qh_s: e.matmul(
                             psc.ap[:], kl.ap[:], qh_s.ap[:, co:co + 512],
                             start=False, stop=True),
                         reads=[kh, kl, qh_s, ql_s] if qt == QT - 1 else (),
                         writes=[psc], inc=True)
                for f in pend_pv:
                    f()
                pend_pv = []
                for qt in range(QT):
                    psc = p_sc[qt]
                    pp = pP[qt]
                    P.op("scalar", lambda e, pp=pp, psc=psc: e.activation(
                        pp.ap[:], psc.ap[:], AF.Exp, scale=float(1.0 / 32.0)),
                        reads=[psc], writes=[pp])

                    def emit_pv(pp=pp, vb=vb, qt=qt, kb=kb):
                        P.op("tensor",
                             lambda e, pp=pp, vb=vb, qt=qt, kb=kb: e.matmul(
                                 p_out[qt].ap[:], vb.ap[:], pp.ap[:],
                                 start=(kb == 0), stop=(kb == NCH - 1)),
                             reads=[vb, pp],
                             extra=P.war(p_out[qt]) if kb == 0 else (),
                             writes=[p_out[qt]] if kb == NCH - 1 else (),
                             inc=True)
                    pend_pv.append(emit_pv)
                    if kb % 2 == 0:
                        P.op("gpsimd", lambda e, pp=pp, qt=qt: e.tensor_add(
                            accg[qt].ap[:], accg[qt].ap[:], pp.ap[:]),
                            reads=[pp, accg[qt]], writes=[accg[qt]])
                    else:
                        P.op("vector", lambda e, pp=pp, qt=qt: e.tensor_add(
                            accv[qt].ap[:], accv[qt].ap[:], pp.ap[:]),
                            reads=[pp, accv[qt]], writes=[accv[qt]])
            for f in pend_pv:
                f()
            pend_pv = []
            for qt in range(QT):
                co = h * own + qt * 512
                P.op("scalar", lambda e, qt=qt: e.copy(a_raw.ap[:],
                                                       p_out[qt].ap[:]),
                     reads=[p_out[qt]], writes=[a_raw])
                P.op("tensor", lambda e, qt=qt: e.matmul(
                    p_sc[0].ap[0:1, :], onesc_b.ap[:], accg[qt].ap[:],
                    start=True, stop=False),
                    reads=[onesc_b, accg[qt]], extra=P.war(p_sc[0]),
                    inc=False)
                P.op("tensor", lambda e, qt=qt: e.matmul(
                    p_sc[0].ap[0:1, :], onesc_b.ap[:], accv[qt].ap[:],
                    start=False, stop=True),
                    reads=[onesc_b, accv[qt]], writes=[p_sc[0]], inc=True)
                P.op("scalar", lambda e: e.copy(sums_sb.ap[:],
                                                p_sc[0].ap[0:1, :]),
                     reads=[p_sc[0]], writes=[sums_sb])
                P.op("vector", lambda e: e.reciprocal(recip_sb.ap[:],
                                                      sums_sb.ap[:]),
                     reads=[sums_sb], writes=[recip_sb])
                P.op("tensor", lambda e: e.matmul(
                    p_sc[1].ap[:], onesr_b.ap[:], recip_sb.ap[:],
                    start=True, stop=True),
                    reads=[onesr_b, recip_sb], extra=P.war(p_sc[1]),
                    writes=[p_sc[1]], inc=True)
                P.op("vector", lambda e: e.tensor_mul(a_div.ap[:], a_raw.ap[:],
                                                      p_sc[1].ap[:]),
                     reads=[a_raw, p_sc[1]], writes=[a_div])
                split16(a_div, a_div.ap[:], a_h16, a_h16.ap[:],
                        a_l16, a_l16.ap[:], sub_eng="gpsimd")
                P.dma("scalar", ad, ah_d.ap[h][:, qt * 512:(qt + 1) * 512],
                      a_h16.ap[:], reads=[a_h16], writes=[ah_d])
                P.dma("scalar", ad, al_d.ap[h][:, qt * 512:(qt + 1) * 512],
                      a_l16.ap[:], reads=[a_l16], writes=[al_d])
        P.barrier()
        p5s.close()

    if phases >= 6:
        p6s = ExitStack()
        ph6 = {}
        ph6["stats"] = {k: sbuf(p6s, f"s6_{k}", [128, 4]) for k in
                        ("mu", "var", "r")}
        ph6["stats"]["d"] = sbuf(p6s, "s6_d", [128, C])
        ph6["stats"]["sq"] = sbuf(p6s, "s6_sq", [128, C])
        fob_b = sbuf(p6s, "fob_b", [128, C])
        P.dma("sync", sd, fob_b.ap[:], fob_rep, writes=[fob_b])
        x1_in = [sbuf(p6s, f"x1_in{i}", [128, C]) for i in range(2)]
        xn2f = sbuf(p6s, "xn2f", [128, C])
        ot_sb = [sbuf(p6s, f"ot_sb{i}", [128, C]) for i in range(2)]
        a_hin = [sbuf(p6s, f"a_hin{i}", [128, H_FFN * 128], FP16)
                 for i in range(2)]
        a_lin = [sbuf(p6s, f"a_lin{i}", [128, H_FFN * 128], FP16)
                 for i in range(2)]
        p_W6 = psum(p6s, "p_W6", [128, C])
        wsl6 = mk_wslots(p6s, ("a",))
        load_weight(wsl6, "a", fow_t)

        for i in range(OCH):
            xi = x1_in[i % 2]
            P.dma("sync", sd, xi.ap[:],
                  ag_in.ap.rearrange("(n p) c -> n p c", p=128)[i],
                  reads=[ag_in], writes=[xi])
            ah, al = a_hin[i % 2], a_lin[i % 2]
            P.dma("sync", sd2, ah.ap[:].rearrange("d (h t) -> d h t", h=H_FFN),
                  ah_d.ap[:, :, i * 128:(i + 1) * 128].transpose([1, 0, 2]),
                  reads=[ah_d], writes=[ah])
            P.dma("sync", sd2, al.ap[:].rearrange("d (h t) -> d h t", h=H_FFN),
                  al_d.ap[:, :, i * 128:(i + 1) * 128].transpose([1, 0, 2]),
                  reads=[al_d], writes=[al])
            ln_tile(ph6, xi, xi.ap[:], xn2f, xn2f.ap[:])
            for mc in range(8):
                for half in range(2):
                    last = mc == 7 and half == 1
                    mm3(p_W6.ap[:, half * 512:(half + 1) * 512],
                        ah.ap[:, mc * 128:(mc + 1) * 128],
                        al.ap[:, mc * 128:(mc + 1) * 128],
                        wsl6[0]["a"].ap[:, mc * C + half * 512:
                                        mc * C + (half + 1) * 512],
                        wsl6[1]["a"].ap[:, mc * C + half * 512:
                                        mc * C + (half + 1) * 512],
                        start=(mc == 0), stop=(mc == 7),
                        reads=[ah, al, wsl6[0]["a"], wsl6[1]["a"]],
                        writes=[p_W6] if last else (), inc=last,
                        ps_buf=p_W6 if (mc == 0 and half == 0) else None)
            ot = ot_sb[i % 2]
            P.op("vector", lambda e, ot=ot: e.tensor_add(ot.ap[:], p_W6.ap[:],
                                                         fob_b.ap[:]),
                 reads=[p_W6, fob_b], writes=[ot])
            P.op("vector", lambda e, ot=ot: e.tensor_add(ot.ap[:], ot.ap[:],
                                                         xn2f.ap[:]),
                 reads=[ot, xn2f], writes=[ot])
            P.op("vector", lambda e, ot=ot, xi=xi: e.tensor_add(
                ot.ap[:], ot.ap[:], xi.ap[:]),
                reads=[ot, xi], writes=[ot])
            P.dma("scalar", ad, out_d.rearrange("(n p) c -> n p c", p=128)[i],
                  ot.ap[:], reads=[ot])
        P.barrier()
        p6s.close()

    with nc.Block() as blk:
        P.resolve_and_replay(blk)
    st.close()
    return nc


def kernel(**inputs):
    T = T_GLOB
    own = T // 4
    nc = build_program(T)
    in_maps = []
    for core in range(NCORES):
        m = {}
        m.update(host_constants(T, core))
        m.update(host_inputs(T, core, inputs))
        in_maps.append(m)
    res = run_bass_kernel_spmd(nc, in_maps, core_ids=list(range(NCORES)))
    y = np.empty((B_GLOB, T, C), np.float32)
    for core in range(NCORES):
        beta, r = core // 4, core % 4
        y[beta, r * own:(r + 1) * own] = res.results[core]["out"]
    return y


# version 17 (final)
# speedup vs baseline: 1.0258x; 1.0258x over previous
import numpy as np
from contextlib import ExitStack

import concourse.bass as bass
import concourse.mybir as mybir
from concourse.bass_utils import run_bass_kernel_spmd

F32 = mybir.dt.float32
FP16 = mybir.dt.float16
AF = mybir.ActivationFunctionType
ALU = mybir.AluOpType
AX = mybir.AxisListType

B_GLOB, T_GLOB, C = 2, 8192, 1024
NCORES = 8
L = 128
H_ATT, DH_ATT = 16, 64
H_FFN, DH_FFN = 8, 128
EPS = 1e-5
ENGS = ("sync", "scalar", "vector", "tensor", "gpsimd")


class Sem:
    __slots__ = ("h", "n", "is_cc")

    def __init__(self, stack, nc, name):
        self.h = stack.enter_context(nc.semaphore(name))
        self.n = 0
        self.is_cc = False


class Buf:
    __slots__ = ("ap", "w", "rs")

    def __init__(self, ap=None):
        self.ap = ap
        self.w = None
        self.rs = []


class DmaPool:

    def __init__(self, prog, name, n):
        self.sems = [Sem(prog.stack, prog.nc, f"{name}{i}") for i in range(n)]
        self.last = [None] * n
        self.i = 0


class Tick:
    __slots__ = ("eng", "seq", "sem", "val")

    def __init__(self, eng, seq):
        self.eng = eng
        self.seq = seq
        self.sem = None
        self.val = None


class Prog:

    def __init__(self, nc, stack):
        self.nc = nc
        self.stack = stack
        self.ops = {e: [] for e in ENGS}
        self.tick = {e: Sem(stack, nc, f"tick_{e}") for e in ENGS}
        self.extra_sems = []

    def op(self, eng, fn, reads=(), writes=(), extra=(), inc=True, step=1,
           sem=None):
        deps = []
        for b in reads:
            if b.w is not None:
                deps.append(b.w)
        for b in writes:
            if b.w is not None:
                deps.append(b.w)
            deps.extend(b.rs)
        for d in extra:
            if d is None:
                continue
            if isinstance(d, tuple):
                t0 = Tick(d[0], -1)
                t0.sem, t0.val = d[0], d[1]
                deps.append(t0)
            else:
                deps.append(d)
        seq = len(self.ops[eng])
        if sem is not None:
            t = Tick(sem, seq)
            sem.n += step
            t.sem, t.val = sem, sem.n
        elif inc:
            t = Tick(eng, seq)
        else:
            t = None
        self.ops[eng].append((fn, deps, t, sem, step))
        if t is not None:
            for b in reads:
                b.rs.append(t)
            for b in writes:
                b.w = t
                b.rs = []
        return t

    @staticmethod
    def war(buf):
        d = list(buf.rs)
        if buf.w is not None:
            d.append(buf.w)
        return d

    def dma(self, eng, pool, out_ap, in_ap, reads=(), writes=(), extra=(),
            nobc=False):
        i = pool.i
        pool.i = (i + 1) % len(pool.sems)
        s = pool.sems[i]
        ex = list(extra)
        if pool.last[i] is not None:
            ex.append(pool.last[i])
        if nobc:
            t = self.op(eng, lambda e: e.dma_start(out=out_ap, in_=in_ap,
                                                   bounds_check=None),
                        reads=reads, writes=writes, extra=ex, sem=s, step=16)
        else:
            t = self.op(eng, lambda e: e.dma_start(out=out_ap, in_=in_ap),
                        reads=reads, writes=writes, extra=ex, sem=s, step=16)
        pool.last[i] = t
        return t

    def barrier(self, skip_cc=False):
        self._skip_cc = skip_cc
        marks = []
        for e in ENGS:
            for seq in range(len(self.ops[e]) - 1, -1, -1):
                fn = self.ops[e][seq][0]
                if fn == "BARRIER":
                    continue
                t_existing = self.ops[e][seq][2]
                sem_over = self.ops[e][seq][3]
                if sem_over is not None and skip_cc and \
                        getattr(sem_over, "is_cc", False):
                    continue
                if sem_over is not None and t_existing is not None:
                    marks.append((sem_over, t_existing.val))
                else:
                    marks.append(Tick(e, seq))
                break
        sems = [(s, s.n) for s in self.extra_sems
                if s.n > 0 and not (skip_cc and getattr(s, "is_cc", False))]
        for e in ENGS:
            deps = []
            sdeps = list(sems)
            for m in marks:
                if isinstance(m, tuple):
                    sdeps.append(m)
                elif m.eng != e:
                    deps.append(m)
            self.ops[e].append(("BARRIER", deps, None, sdeps, 1))

    def resolve_and_replay(self, blk):
        marked = {e: set() for e in ENGS}
        for e in ENGS:
            for (fn, deps, t, sem, step) in self.ops[e]:
                for d in deps:
                    if isinstance(d, Tick) and d.sem is None and d.val is None:
                        marked[d.eng].add(d.seq)
        tickval = {}
        for e in ENGS:
            for rank, seq in enumerate(sorted(marked[e])):
                tickval[(e, seq)] = rank + 1
        def run_engine(en, ename):
            wm = {}
            cnt = 0
            for (fn, deps, t, sem, step) in self.ops[ename]:
                waits = []
                if fn == "BARRIER":
                    for d in deps:
                        s = self.tick[d.eng]
                        v = tickval.get((d.eng, d.seq))
                        if v is None:
                            v = sum(1 for q in marked[d.eng] if q <= d.seq)
                        else:
                            pass
                        if v and wm.get(id(s), 0) < v:
                            wm[id(s)] = v
                            waits.append((s.h, v))
                    for (s, v) in sem:
                        if wm.get(id(s), 0) < v:
                            wm[id(s)] = v
                            waits.append((s.h, v))
                    for sh, v in waits:
                        en.wait_ge(sh, v)
                    continue
                for d in deps:
                    if d.sem is not None:
                        s, v = d.sem, d.val
                    else:
                        s = self.tick[d.eng]
                        v = tickval.get((d.eng, d.seq))
                        if v is None:
                            v = sum(1 for q in marked[d.eng] if q <= d.seq)
                        if v == 0:
                            continue
                    if wm.get(id(s), 0) < v:
                        wm[id(s)] = v
                        waits.append((s.h, v))
                for sh, v in waits:
                    en.wait_ge(sh, v)
                ins = fn(en)
                if sem is not None and t is not None:
                    ins.then_inc(sem.h, step)
                elif t is not None and t.seq in marked[ename]:
                    ins.then_inc(self.tick[ename].h, 1)
                cnt += 1
        for ename in ENGS:
            getattr(blk, ename)(lambda en, ename=ename: run_engine(en, ename))

def host_constants(T, core):
    r = core % 4
    own = T // 4
    q1 = own * (r + 1)
    g = 1.0 - 2.0 ** (-5.0 - np.arange(H_ATT, dtype=np.float64))
    cons = {}
    pos = np.arange(T, dtype=np.float64) + (q1 - T)
    invf = 10000.0 ** (-(np.arange(32, dtype=np.float64) / 32.0))
    fr = pos[:, None] * invf[None, :]
    emb = np.concatenate([fr, fr], axis=1)
    cons["cos_t"] = np.cos(emb).astype(np.float32)
    cons["sin_t"] = np.sin(emb).astype(np.float32)
    i_loc = np.arange(L, dtype=np.float64)
    dt_mask = np.zeros((L, H_ATT * L), dtype=np.float32)
    for h in range(H_ATT):
        M = np.where(i_loc[None, :] >= i_loc[:, None],
                     g[h] ** (i_loc[None, :] - i_loc[:, None]), 0.0) / 8.0
        dt_mask[:, h * L:(h + 1) * L] = M.astype(np.float32)
    cons["dt_mask"] = dt_mask
    cons["gq_c"] = (np.repeat(g[None, :] ** i_loc[:, None], DH_ATT,
                              axis=1) / 8.0).astype(np.float32)
    cons["gk_c"] = np.repeat(g[None, :] ** (L - i_loc)[:, None], DH_ATT,
                           axis=1).astype(np.float32)
    cons["gl_c"] = np.repeat(g[None, :] ** L, DH_ATT)[None, :].repeat(
        64, axis=0).astype(np.float32)
    nch = T // L
    J0 = (own // L) * r
    wst = np.zeros((128, nch * H_ATT), np.float64)
    for j in range(J0):
        wst[:, j * H_ATT:(j + 1) * H_ATT] = g ** ((J0 - 1 - j) * L)
    cons["wstA_c"] = wst.astype(np.float32)
    cons["ones_col"] = np.ones((128, 1), np.float32)
    cons["ones_row"] = np.ones((1, 128), np.float32)
    cons["ident32"] = np.eye(128, dtype=np.float32)
    cons["ident16"] = np.eye(128, dtype=np.float16)
    return cons


def host_inputs(T, core, inp):
    beta, r = core // 4, core % 4
    own = T // 4
    q1 = own * (r + 1)
    d = {}
    x = np.asarray(inp["L0_attn_x"])[:, :T]
    xa = np.zeros((T, C), np.float32)
    xa[T - q1:] = x[beta, 0:q1]
    d["x_attn"] = xa
    for nm, w in (("wq", "L0_attn_Wq"), ("wk", "L0_attn_Wk"),
                  ("wv", "L0_attn_Wv"), ("wo", "L0_attn_Wo")):
        d[nm + "_t"] = np.ascontiguousarray(np.asarray(inp[w]).T)
    d["gnw_rep"] = np.broadcast_to(np.asarray(inp["L0_attn_gn_weight"]),
                                   (128, C)).copy()
    d["gnb_rep"] = np.broadcast_to(np.asarray(inp["L0_attn_gn_bias"]),
                                   (128, C)).copy()
    m = C >> r
    qw = np.zeros((C, C), np.float32)
    qw[:m, :] = np.asarray(inp["L0_ffn_q_w"]).T[:m, :]
    d["fqw_t"] = qw
    kw = np.zeros((C, C), np.float32)
    kw[:m, :] = np.asarray(inp["L0_ffn_k_w"]).T[:m, :]
    d["fkw_t"] = kw
    vw = np.zeros((C, C), np.float32)
    vw[:m, :] = np.asarray(inp["L0_ffn_v_w"]).T[:m, :]
    d["fvw_t"] = vw
    ow = np.zeros((C, C), np.float32)
    ow[:m, :m] = np.asarray(inp["L0_ffn_o_w"]).T[:m, :m]
    d["fow_t"] = ow
    d["fqb_fm"] = np.ascontiguousarray(
        np.asarray(inp["L0_ffn_q_b"], np.float32).reshape(8, 128).T)
    d["fkb_fm"] = np.ascontiguousarray(
        np.asarray(inp["L0_ffn_k_b"], np.float32).reshape(8, 128).T)
    d["fvb_rep"] = np.broadcast_to(np.asarray(inp["L0_ffn_v_b"]),
                                   (128, C)).copy()
    ob = np.zeros(C, np.float32)
    ob[:m] = np.asarray(inp["L0_ffn_o_b"])[:m]
    d["fob_rep"] = np.broadcast_to(ob, (128, C)).copy()
    d["fnw_rep"] = np.broadcast_to(np.asarray(inp["L0_ffn_norm_w"]),
                                   (128, C)).copy()
    d["fnb_rep"] = np.broadcast_to(np.asarray(inp["L0_ffn_norm_b"]),
                                   (128, C)).copy()
    return d


def build_program(T=T_GLOB, dbg=(), phases=6):
    own = T // 4
    NCH = T // L
    OCH = own // L
    PCH = NCH - OCH
    NG = NCH // 4
    OG = OCH // 4
    QT = own // 512

    nc = bass.Bass("TRN2", target_bir_lowering=False)
    st = ExitStack()
    P = Prog(nc, st)

    def din(name, shape, dt=F32):
        return nc.dram_tensor(name, shape, dt, kind="ExternalInput").ap()

    x_attn = din("x_attn", [T, C])
    wq_t, wk_t, wv_t, wo_t = (din("wq_t", [C, C]), din("wk_t", [C, C]),
                              din("wv_t", [C, C]), din("wo_t", [C, C]))
    gnw_rep, gnb_rep = din("gnw_rep", [128, C]), din("gnb_rep", [128, C])
    fqw_t, fkw_t = din("fqw_t", [C, C]), din("fkw_t", [C, C])
    fvw_t, fow_t = din("fvw_t", [C, C]), din("fow_t", [C, C])
    fqb_fm, fkb_fm = din("fqb_fm", [128, 8]), din("fkb_fm", [128, 8])
    fvb_rep, fob_rep = din("fvb_rep", [128, C]), din("fob_rep", [128, C])
    fnw_rep, fnb_rep = din("fnw_rep", [128, C]), din("fnb_rep", [128, C])
    cos_t, sin_t = din("cos_t", [T, 64]), din("sin_t", [T, 64])
    dt_mask = din("dt_mask", [L, H_ATT * L])
    gq_c, gk_c, gl_c = din("gq_c", [L, C]), din("gk_c", [L, C]), din("gl_c", [64, C])
    wstA_c = din("wstA_c", [128, NCH * H_ATT])
    ones_col_d, ones_row_d = din("ones_col", [128, 1]), din("ones_row", [1, 128])
    id32_d = din("ident32", [128, 128])
    id16_d = din("ident16", [128, 128], FP16)

    out_d = nc.dram_tensor("out", [own, C], F32, kind="ExternalOutput").ap()

    def dscr(name, shape, dt=F32):
        return Buf(nc.dram_tensor(name, shape, dt).ap())

    kv_own_d = dscr("kv_own_d", [2, OCH, L, C])
    kv_all_d = dscr("kv_all_d", [4, 2, OCH, L, C])
    kfh_d = dscr("kfh_d", [OCH, H_ATT, DH_ATT, L], FP16)
    kfl_d = dscr("kfl_d", [OCH, H_ATT, DH_ATT, L], FP16)
    q8h_d = dscr("q8h_d", [OCH, H_ATT, DH_ATT, L], FP16)
    q8l_d = dscr("q8l_d", [OCH, H_ATT, DH_ATT, L], FP16)
    qs_d = dscr("qs_d", [OCH, H_ATT, DH_ATT, L])
    ret_d = dscr("ret_d", [OCH, L, C])
    ag_in = dscr("ag_in", [own, C])
    sa_d = dscr("sa_d", [64, C])
    HS = H_FFN * DH_FFN * 512
    pk_own_t = [nc.dram_tensor(f"pk_own{g}", [2 * HS], FP16)
                for g in range(OG)]
    pk_all_t = [nc.dram_tensor(f"pk_all{g}", [4 * 2 * HS], FP16)
                for g in range(OG)]
    pk_own_g = [Buf(t.ap()) for t in pk_own_t]
    pk_all_g = [Buf(t.ap()) for t in pk_all_t]
    vf_own_g = [dscr(f"vf_own{g}", [4, L, C]) for g in range(OG)]
    vf_all_g = [dscr(f"vf_all{g}", [4, 4, L, C]) for g in range(OG)]

    def pk_kf(hl, g, h, rk=None):
        off = (0 if rk is None else rk * 2 * HS) + hl * HS + \
            h * DH_FFN * 512
        t = pk_own_t[g] if rk is None else pk_all_t[g]
        return bass.AP(t, off, [[512, DH_FFN], [1, 512]])
    ah_d = dscr("ah_d", [H_FFN, DH_FFN, own], FP16)
    al_d = dscr("al_d", [H_FFN, DH_FFN, own], FP16)
    qh_d = dscr("qh_d", [H_FFN, DH_FFN, own], FP16)
    ql_d = dscr("ql_d", [H_FFN, DH_FFN, own], FP16)

    dbg_aps = {}
    for tag, shape in (("ret", [OCH, L, C]), ("x1", [own, C]),
                       ("ks", [NCH, L, C]), ("va", [NCH, L, C]),
                       ("xff", [NCH, L, C]),
                       ("vff", [NCH, L, C]), ("qf", [own, C]),
                       ("kff", [NCH, L, C]), ("fattn", [own, C])):
        if tag in dbg:
            dbg_aps[tag] = nc.dram_tensor("dbg_" + tag, shape, F32,
                                          kind="ExternalOutput").ap()

    sd = DmaPool(P, "sd", 12)
    sd2 = DmaPool(P, "se", 8)
    ad = DmaPool(P, "ad", 12)
    cc = Sem(st, nc, "cc")
    cc.is_cc = True
    P.extra_sems = sd.sems + sd2.sems + ad.sems + [cc]

    def sbuf(stack, name, shape, dt=F32):
        return Buf(stack.enter_context(nc.sbuf_tensor(name, shape, dt)).ap())

    def psum(stack, name, shape, dt=F32):
        return Buf(stack.enter_context(nc.psum_tensor(name, shape, dt)).ap())

    onesc_b = sbuf(st, "onesc_b", [128, 1])
    onesr_b = sbuf(st, "onesr_b", [1, 128])
    id16_b = sbuf(st, "id16_b", [128, 128], FP16)
    id32_b = sbuf(st, "id32_b", [128, 128])
    for bb, ap in ((onesc_b, ones_col_d), (onesr_b, ones_row_d),
                   (id16_b, id16_d), (id32_b, id32_d)):
        P.dma("sync", sd, bb.ap[:], ap, writes=[bb])

    _wsl_ctr = [0]

    def mk_wslots(stack, names):
        _wsl_ctr[0] += 1
        w_hi = {}
        w_lo = {}
        for wn in names:
            w_hi[wn] = sbuf(stack, f"w{_wsl_ctr[0]}_{wn}_hi", [128, 8 * C], FP16)
            w_lo[wn] = sbuf(stack, f"w{_wsl_ctr[0]}_{wn}_lo", [128, 8 * C], FP16)
        wstage = [sbuf(stack, f"wstage{_wsl_ctr[0]}_{i}", [128, 512])
                  for i in range(2)]
        return w_hi, w_lo, wstage

    def load_weight_into(hi, lo, wstage, dram_ap):
        src = dram_ap.rearrange("(cc p) (hf o) -> cc hf p o", p=128, hf=2)
        for q in range(16):
            ws = wstage[q % 2]
            P.dma("sync", sd, ws.ap[:], src[q // 2, q % 2], writes=[ws])
            P.op("vector",
                 lambda e, q=q, ws=ws: e.tensor_copy(
                     hi.ap[:, q * 512:(q + 1) * 512], ws.ap[:]),
                 reads=[ws], writes=[hi], inc=True)
            P.op("gpsimd",
                 lambda e, q=q, ws=ws: e.tensor_sub(
                     lo.ap[:, q * 512:(q + 1) * 512], ws.ap[:],
                     hi.ap[:, q * 512:(q + 1) * 512]),
                 reads=[ws, hi], writes=[lo], inc=True)

    def load_weight(slots, name, dram_ap):
        w_hi, w_lo, wstage = slots
        load_weight_into(w_hi[name], w_lo[name], wstage, dram_ap)

    def ln_tile(ph, x_buf, x_ap, out_buf, out_ap, scale_w=None, bias_w=None,
                double=False):
        s = ph["stats"]
        mu, var, r, sq_b = s["mu"], s["var"], s["r"], s["sq"]
        P.op("vector", lambda e: e.reduce_sum(mu.ap[:, 0:1], x_ap, axis=AX.X),
             reads=[x_buf], writes=[mu])
        P.op("scalar", lambda e: e.activation(sq_b.ap[:], x_ap, AF.Square),
             reads=[x_buf], writes=[sq_b])
        P.op("vector", lambda e: e.reduce_sum(var.ap[:, 0:1], sq_b.ap[:],
                                              axis=AX.X),
             reads=[sq_b], writes=[var])
        P.op("vector", lambda e: e.tensor_scalar_mul(mu.ap[:, 1:2],
                                                     mu.ap[:, 0:1], 1.0 / C),
             reads=[mu], writes=[mu])
        P.op("vector", lambda e: e.tensor_mul(r.ap[:, 3:4], mu.ap[:, 1:2],
                                              mu.ap[:, 1:2]),
             reads=[mu], writes=[r])
        P.op("vector", lambda e: e.tensor_scalar(var.ap[:, 1:2],
                                                 var.ap[:, 0:1],
                                                 1.0 / C, None, op0=ALU.mult),
             reads=[var], writes=[var])
        P.op("vector", lambda e: e.tensor_sub(var.ap[:, 2:3], var.ap[:, 1:2],
                                              r.ap[:, 3:4]),
             reads=[var, r], writes=[var])
        P.op("vector", lambda e: e.tensor_scalar(var.ap[:, 3:4],
                                                 var.ap[:, 2:3],
                                                 EPS, None, op0=ALU.add),
             reads=[var], writes=[var])
        P.op("vector", lambda e: e.reciprocal(r.ap[:, 0:1], var.ap[:, 3:4]),
             reads=[var], writes=[r])
        P.op("scalar", lambda e: e.activation(r.ap[:, 1:2], r.ap[:, 0:1],
                                              AF.Sqrt),
             reads=[r], writes=[r])
        rr_col = r.ap[:, 1:2]
        if double:
            P.op("vector", lambda e: e.tensor_mul(var.ap[:, 0:1],
                                                  var.ap[:, 2:3],
                                                  r.ap[:, 0:1]),
                 reads=[var, r], writes=[var])
            P.op("vector", lambda e: e.tensor_scalar(var.ap[:, 1:2],
                                                     var.ap[:, 0:1],
                                                     EPS, None, op0=ALU.add),
                 reads=[var], writes=[var])
            P.op("vector", lambda e: e.reciprocal(r.ap[:, 2:3],
                                                  var.ap[:, 1:2]),
                 reads=[var], writes=[r])
            P.op("scalar", lambda e: e.activation(r.ap[:, 3:4], r.ap[:, 2:3],
                                                  AF.Sqrt),
                 reads=[r], writes=[r])
            P.op("vector", lambda e: e.tensor_mul(r.ap[:, 2:3], r.ap[:, 1:2],
                                                  r.ap[:, 3:4]),
                 reads=[r], writes=[r])
            rr_col = r.ap[:, 2:3]
        if scale_w is None:
            P.op("vector", lambda e: e.tensor_scalar(out_ap, x_ap,
                                                     mu.ap[:, 1:2], rr_col,
                                                     op0=ALU.subtract,
                                                     op1=ALU.mult),
                 reads=[x_buf, mu, r], writes=[out_buf])
        else:
            P.op("vector", lambda e: e.tensor_scalar(s["d"].ap[:], x_ap,
                                                     mu.ap[:, 1:2], rr_col,
                                                     op0=ALU.subtract,
                                                     op1=ALU.mult),
                 reads=[x_buf, mu, r], writes=[s["d"]])
            P.op("vector",
                 lambda e: e.tensor_mul(out_ap, s["d"].ap[:], scale_w.ap[:]),
                 reads=[s["d"], scale_w], writes=[out_buf])
            P.op("vector", lambda e: e.tensor_add(out_ap, out_ap, bias_w.ap[:]),
                 reads=[out_buf, bias_w], writes=[out_buf])

    def split16(src_buf, src_ap, hi_buf, hi_ap, lo_buf, lo_ap,
                sub_eng="vector"):
        P.op("vector", lambda e: e.tensor_copy(hi_ap, src_ap),
             reads=[src_buf], writes=[hi_buf])
        P.op(sub_eng, lambda e: e.tensor_sub(lo_ap, src_ap, hi_ap),
             reads=[src_buf, hi_buf], writes=[lo_buf])

    def transpose16(src_buf, src_ap_base, ps_buf, n_blocks=8):
        for j in range(n_blocks):
            last = j == n_blocks - 1
            P.op("tensor",
                 lambda e, j=j: e.transpose(
                     ps_buf.ap[:, j * 128:(j + 1) * 128],
                     src_ap_base[:, j * 128:(j + 1) * 128], id16_b.ap[:]),
                 reads=[src_buf, id16_b],
                 extra=P.war(ps_buf) if j == 0 else (),
                 writes=[ps_buf] if last else (), inc=last)

    def mm3(ps_ap, lT_hi, lT_lo, r_hi, r_lo, start, stop, reads=(), writes=(),
            inc=False, ps_buf=None):
        extra0 = P.war(ps_buf) if (start and ps_buf is not None) else ()
        P.op("tensor", lambda e: e.matmul(ps_ap, lT_hi, r_hi,
                                          start=start, stop=False),
             reads=reads, extra=extra0, inc=False)
        P.op("tensor", lambda e: e.matmul(ps_ap, lT_hi, r_lo,
                                          start=False, stop=False), inc=False)
        return P.op("tensor", lambda e: e.matmul(ps_ap, lT_lo, r_hi,
                                                 start=False, stop=stop),
                    reads=reads, writes=writes, inc=inc)

    p1 = ExitStack()
    ph = {}
    ph["stats"] = {k: sbuf(p1, f"st_{k}", [128, 4]) for k in ("mu", "var", "r")}
    cosl = [sbuf(p1, f"cosl{i}", [128, 64]) for i in range(2)]
    sinl = [sbuf(p1, f"sinl{i}", [128, 64]) for i in range(2)]
    gq_b = sbuf(p1, "gq_b", [L, C])
    gk_b = sbuf(p1, "gk_b", [L, C])
    P.dma("sync", sd, gq_b.ap[:], gq_c, writes=[gq_b])
    P.dma("sync", sd, gk_b.ap[:], gk_c, writes=[gk_b])

    x_in = [sbuf(p1, f"x_in{i}", [128, C]) for i in range(3)]
    xn_tm = [sbuf(p1, f"xn_tm{i}", [128, C]) for i in range(2)]
    t16al = [sbuf(p1, f"t16a{i}", [128, C], FP16) for i in range(2)]
    t16bl = [sbuf(p1, f"t16b{i}", [128, C], FP16) for i in range(2)]
    xf_hi = [sbuf(p1, f"xf_hi{i}", [128, C], FP16) for i in range(2)]
    xf_lo = [sbuf(p1, f"xf_lo{i}", [128, C], FP16) for i in range(2)]
    rt1_s = sbuf(p1, "rt1", [128, C])
    rt2_s = sbuf(p1, "rt2", [128, C])
    rt1l = [rt1_s, rt1_s]
    rt2l = [rt2_s, rt2_s]
    ph["stats"]["sq"] = rt2_s
    krfl = [sbuf(p1, f"krf{i}", [128, C]) for i in range(2)]
    kraw = [sbuf(p1, f"kraw{i}", [128, C]) for i in range(2)]
    qraw = sbuf(p1, "qraw", [128, C])
    ksc = [sbuf(p1, f"ksc{i}", [128, C]) for i in range(2)]
    vtm = [sbuf(p1, f"vtm{i}", [128, C]) for i in range(2)]
    q8 = sbuf(p1, "q8", [128, C])
    qsh = sbuf(p1, "qsh", [128, C])
    fm16 = sbuf(p1, "fm16", [128, C], FP16)
    fm16b = sbuf(p1, "fm16b", [128, C], FP16)
    fm32 = sbuf(p1, "fm32", [128, C])

    p_trh = psum(p1, "p_trh", [128, C], FP16)
    p_trl = psum(p1, "p_trl", [128, C], FP16)
    p_k = psum(p1, "p_k", [128, C])
    p_v = psum(p1, "p_v", [128, C])
    p_q = psum(p1, "p_q", [128, C])

    wsl1 = mk_wslots(p1, ("a", "b", "c"))
    load_weight(wsl1, "a", wk_t)
    load_weight(wsl1, "b", wv_t)
    load_weight(wsl1, "c", wq_t)

    def proj_tm(slots, ps_buf, slot, fh, fl):
        hiw, low = slots[0][slot], slots[1][slot]
        for ccx in range(8):
            lh = fh.ap[:, ccx * 128:(ccx + 1) * 128]
            ll = fl.ap[:, ccx * 128:(ccx + 1) * 128]
            for half in range(2):
                wh = hiw.ap[:, ccx * C + half * 512:ccx * C + (half + 1) * 512]
                wl = low.ap[:, ccx * C + half * 512:ccx * C + (half + 1) * 512]
                po = ps_buf.ap[:, half * 512:(half + 1) * 512]
                P.op("tensor", lambda e, lh=lh, wh=wh, po=po, s=(ccx == 0):
                     e.matmul(po, lh, wh, start=s, stop=False),
                     reads=[fh, fl, hiw, low],
                     extra=P.war(ps_buf) if (ccx == 0 and half == 0) else (),
                     inc=False)
                P.op("tensor", lambda e, lh=lh, wl=wl, po=po:
                     e.matmul(po, lh, wl, start=False, stop=False), inc=False)
            for half in range(2):
                wh = hiw.ap[:, ccx * C + half * 512:ccx * C + (half + 1) * 512]
                po = ps_buf.ap[:, half * 512:(half + 1) * 512]
                last = ccx == 7 and half == 1
                P.op("tensor", lambda e, ll=ll, wh=wh, po=po, s=(ccx == 7):
                     e.matmul(po, ll, wh, start=False, stop=s),
                     reads=[fh, fl, hiw, low] if last else (),
                     writes=[ps_buf] if last else (), inc=last)

    def rope(ps_buf, cos_sb, sin_sb, out_buf, out_ap, rt1, rt2):
        ps3 = ps_buf.ap[:].rearrange("p (h d) -> p h d", h=H_ATT)
        cosv = cos_sb.ap[:].unsqueeze(1).broadcast_to([128, H_ATT, 64])
        sinv4 = sin_sb.ap[:].unsqueeze(1).broadcast_to(
            [128, H_ATT, 64]).rearrange("p h (a b) -> p h a b", a=2)
        P.op("vector", lambda e: e.tensor_mul(
            rt1.ap[:].rearrange("p (h d) -> p h d", h=H_ATT), ps3, cosv),
            reads=[ps_buf, cos_sb], writes=[rt1])
        swap = bass.AP(ps_buf.ap.tensor, ps_buf.ap.offset + 32,
                       [list(ps_buf.ap.ap[0]), [64, H_ATT], [-32, 2], [1, 32]])
        P.op("vector", lambda e: e.tensor_mul(
            rt2.ap[:].rearrange("p (h a b) -> p h a b", h=H_ATT, a=2),
            swap, sinv4),
            reads=[ps_buf, sin_sb], writes=[rt2])
        r4 = rt1.ap[:].rearrange("p (h a b) -> p h a b", h=H_ATT, a=2)
        o4 = rt2.ap[:].rearrange("p (h a b) -> p h a b", h=H_ATT, a=2)
        d4 = out_ap.rearrange("p (h a b) -> p h a b", h=H_ATT, a=2)
        P.op("vector", lambda e: e.tensor_sub(d4[:, :, 0], r4[:, :, 0],
                                              o4[:, :, 0]),
             reads=[rt1, rt2], writes=[out_buf])
        P.op("vector", lambda e: e.tensor_add(d4[:, :, 1], r4[:, :, 1],
                                              o4[:, :, 1]),
             reads=[rt1, rt2], writes=[out_buf])

    def store_fm16(dram_buf, i, src_buf):
        view = src_buf.ap[:].rearrange("p (cc t) -> p cc t", cc=8)
        d4 = dram_buf.ap[i].rearrange("(hh sub) d t -> sub d hh t", sub=2)
        P.dma("scalar", ad, d4[0], view[0:64], reads=[src_buf],
              writes=[dram_buf])
        P.dma("scalar", ad, d4[1], view[64:128], reads=[src_buf],
              writes=[dram_buf])

    if phases >= 1:
        for tt in range(PCH, NCH):
            xi = x_in[tt % 3]
            P.dma("sync", sd, xi.ap[:],
                  x_attn.rearrange("(n p) c -> n p c", p=128)[tt], writes=[xi])
            cs, sn = cosl[tt % 2], sinl[tt % 2]
            P.dma("sync", sd2, cs.ap[:],
                  cos_t.rearrange("(n p) d -> n p d", p=128)[tt], writes=[cs])
            P.dma("sync", sd2, sn.ap[:],
                  sin_t.rearrange("(n p) d -> n p d", p=128)[tt], writes=[sn])
            xn = xn_tm[tt % 2]
            t16a, t16b = t16al[tt % 2], t16bl[tt % 2]
            ln_tile(ph, xi, xi.ap[:], xn, xn.ap[:])
            split16(xn, xn.ap[:], t16a, t16a.ap[:], t16b, t16b.ap[:],
                    sub_eng="gpsimd")
            transpose16(t16a, t16a.ap, p_trh)
            transpose16(t16b, t16b.ap, p_trl)
            fh, fl = xf_hi[tt % 2], xf_lo[tt % 2]
            P.op("scalar", lambda e, fh=fh: e.copy(fh.ap[:], p_trh.ap[:]),
                 reads=[p_trh], writes=[fh])
            P.op("scalar", lambda e, fl=fl: e.copy(fl.ap[:], p_trl.ap[:]),
                 reads=[p_trl], writes=[fl])
            own_tile = True

            proj_tm(wsl1, p_k, "a", fh, fl)
            proj_tm(wsl1, p_v, "b", fh, fl)
            if own_tile:
                proj_tm(wsl1, p_q, "c", fh, fl)

            kr_ = kraw[tt % 2]
            P.op("scalar", lambda e, kr_=kr_: e.copy(kr_.ap[:], p_k.ap[:]),
                 reads=[p_k], writes=[kr_])
            krf = krfl[tt % 2]
            rope(kr_, cs, sn, krf, krf.ap[:], rt1l[tt % 2], rt2l[tt % 2])
            ks = ksc[tt % 2]
            P.op("vector", lambda e, ks=ks, krf=krf: e.tensor_mul(
                ks.ap[:], krf.ap[:], gk_b.ap[:]),
                 reads=[krf, gk_b], writes=[ks])
            vt = vtm[tt % 2]
            P.op("scalar", lambda e, vt=vt: e.copy(vt.ap[:], p_v.ap[:]),
                 reads=[p_v], writes=[vt])
            P.dma("scalar", ad, kv_own_d.ap[0, tt - PCH], ks.ap[:],
                  reads=[ks], writes=[kv_own_d])
            P.dma("scalar", ad, kv_own_d.ap[1, tt - PCH], vt.ap[:],
                  reads=[vt], writes=[kv_own_d])

            if own_tile:
                i = tt - PCH
                split16(krf, krf.ap[:], t16a, t16a.ap[:], t16b, t16b.ap[:])
                transpose16(t16a, t16a.ap, p_trh)
                transpose16(t16b, t16b.ap, p_trl)
                P.op("vector", lambda e: e.tensor_copy(fm16.ap[:], p_trh.ap[:]),
                     reads=[p_trh], writes=[fm16])
                P.op("vector", lambda e: e.tensor_copy(fm16b.ap[:],
                                                       p_trl.ap[:]),
                     reads=[p_trl], writes=[fm16b])
                store_fm16(kfh_d, i, fm16)
                store_fm16(kfl_d, i, fm16b)
                P.op("scalar", lambda e: e.copy(qraw.ap[:], p_q.ap[:]),
                     reads=[p_q], writes=[qraw])
                rope(qraw, cs, sn, q8, q8.ap[:], rt1l[(tt + 1) % 2],
                     rt2l[(tt + 1) % 2])
                P.op("vector", lambda e: e.tensor_mul(qsh.ap[:], q8.ap[:],
                                                      gq_b.ap[:]),
                     reads=[q8, gq_b], writes=[qsh])
                split16(q8, q8.ap[:], t16a, t16a.ap[:], t16b, t16b.ap[:])
                transpose16(t16a, t16a.ap, p_trh)
                transpose16(t16b, t16b.ap, p_trl)
                P.op("vector", lambda e: e.tensor_copy(fm16.ap[:], p_trh.ap[:]),
                     reads=[p_trh], writes=[fm16])
                P.op("vector", lambda e: e.tensor_copy(fm16b.ap[:],
                                                       p_trl.ap[:]),
                     reads=[p_trl], writes=[fm16b])
                store_fm16(q8h_d, i, fm16)
                store_fm16(q8l_d, i, fm16b)
                for j in range(8):
                    last = j == 7
                    P.op("tensor",
                         lambda e, j=j: e.transpose(
                             p_k.ap[:, j * 128:(j + 1) * 128],
                             qsh.ap[:, j * 128:(j + 1) * 128], id32_b.ap[:]),
                         reads=[qsh, id32_b],
                         extra=P.war(p_k) if j == 0 else (),
                         writes=[p_k] if last else (), inc=last)
                P.op("vector", lambda e: e.tensor_copy(fm32.ap[:], p_k.ap[:]),
                     reads=[p_k], writes=[fm32])
                v32 = fm32.ap[:].rearrange("p (cc t) -> p cc t", cc=8)
                d4 = qs_d.ap[i].rearrange("(hh sub) d t -> sub d hh t", sub=2)
                P.dma("scalar", ad, d4[0], v32[0:64], reads=[fm32],
                      writes=[qs_d])
                P.dma("scalar", ad, d4[1], v32[64:128], reads=[fm32],
                      writes=[qs_d])

        P.op("gpsimd", lambda e: e.collective_compute(
            "AllGather", ALU.bypass,
            replica_groups=[[0, 1, 2, 3], [4, 5, 6, 7]],
            ins=[kv_own_d.ap], outs=[kv_all_d.ap]),
            reads=[kv_own_d], writes=[kv_all_d], sem=cc, step=1)
        P.barrier(skip_cc=True)
        p1.close()

    if phases >= 2:
        p2s = ExitStack()
        S_b = sbuf(p2s, "S_b", [64, C])
        Sg_b = sbuf(p2s, "Sg_b", [64, C])
        gl_b = sbuf(p2s, "gl_b", [64, C])
        wA_b = sbuf(p2s, "wA_b", [128, NCH * H_ATT])
        dt_b = sbuf(p2s, "dt_b", [L, H_ATT * L])
        P.dma("sync", sd, gl_b.ap[:], gl_c, writes=[gl_b])
        P.dma("sync", sd, wA_b.ap[:], wstA_c, writes=[wA_b])
        P.dma("sync", sd, dt_b.ap[:], dt_mask, writes=[dt_b])
        ks_in = [sbuf(p2s, f"ks_in{i}", [128, C]) for i in range(4)]
        v_in = [sbuf(p2s, f"v_in{i}", [128, C]) for i in range(4)]
        kw_b = [sbuf(p2s, f"kw_b{i}", [128, C]) for i in range(2)]
        kfh_in = sbuf(p2s, "kfh_in", [64, H_ATT * L], FP16)
        kfl_in = sbuf(p2s, "kfl_in", [64, H_ATT * L], FP16)
        q8h_in = sbuf(p2s, "q8h_in", [64, H_ATT * L], FP16)
        q8l_in = sbuf(p2s, "q8l_in", [64, H_ATT * L], FP16)
        qs_in = sbuf(p2s, "qs_in", [64, H_ATT * L])
        am_sb = sbuf(p2s, "am_sb", [128, H_ATT * L])
        ret_sb = [sbuf(p2s, f"ret_sb{i}", [128, C]) for i in range(2)]
        p_S = psum(p2s, "p_S", [64, C])
        p_A = psum(p2s, "p_A", [128, H_ATT * L])
        p_O = psum(p2s, "p_O", [128, C])

        P.op("vector", lambda e: e.memset(S_b.ap[:], 0.0), writes=[S_b])

        def state_mms(ksb, vb):
            for h in range(H_ATT):
                P.op("tensor",
                     lambda e, h=h, ksb=ksb, vb=vb: e.matmul(
                         p_S.ap[:, h * 64:(h + 1) * 64],
                         ksb.ap[:, h * 64:(h + 1) * 64],
                         vb.ap[:, h * 64:(h + 1) * 64],
                         start=True, stop=True),
                     reads=[ksb, vb],
                     extra=P.war(p_S) if h == 0 else (),
                     writes=[p_S] if h == H_ATT - 1 else (), inc=True)

        for i in range(OCH):
            ksb, vb = ks_in[i % 4], v_in[i % 4]
            P.dma("sync", sd, ksb.ap[:], kv_own_d.ap[0, i],
                  reads=[kv_own_d], writes=[ksb])
            P.dma("sync", sd, vb.ap[:], kv_own_d.ap[1, i],
                  reads=[kv_own_d], writes=[vb])
            for db, db_d in ((kfh_in, kfh_d), (kfl_in, kfl_d),
                             (q8h_in, q8h_d), (q8l_in, q8l_d),
                             (qs_in, qs_d)):
                P.dma("sync", sd2, db.ap[:].rearrange(
                    "d (h t) -> d h t", h=H_ATT),
                    db_d.ap[i].rearrange("h d t -> d h t"),
                    reads=[db_d], writes=[db])
            for h in range(H_ATT):
                sl = h * L
                mm3(p_A.ap[:, sl:sl + L],
                    kfh_in.ap[:, h * L:(h + 1) * L],
                    kfl_in.ap[:, h * L:(h + 1) * L],
                    q8h_in.ap[:, h * L:(h + 1) * L],
                    q8l_in.ap[:, h * L:(h + 1) * L],
                    start=True, stop=True,
                    reads=[kfh_in, kfl_in, q8h_in, q8l_in],
                    writes=[p_A] if h == H_ATT - 1 else (), inc=True,
                    ps_buf=p_A if h == 0 else None)
            P.op("vector",
                 lambda e: e.tensor_mul(am_sb.ap[:], p_A.ap[:], dt_b.ap[:]),
                 reads=[p_A, dt_b], writes=[am_sb])
            for h in range(H_ATT):
                sl = h * L
                P.op("tensor",
                     lambda e, h=h, sl=sl, vb=vb: e.matmul(
                         p_O.ap[:, h * 64:(h + 1) * 64],
                         am_sb.ap[:, sl:sl + L],
                         vb.ap[:, h * 64:(h + 1) * 64],
                         start=True, stop=False),
                     reads=[am_sb, vb],
                     extra=P.war(p_O) if h == 0 else (), inc=False)
                P.op("tensor",
                     lambda e, h=h: e.matmul(
                         p_O.ap[:, h * 64:(h + 1) * 64],
                         qs_in.ap[:, h * L:(h + 1) * L],
                         S_b.ap[:, h * 64:(h + 1) * 64],
                         start=False, stop=True),
                     reads=[qs_in, S_b],
                     writes=[p_O] if h == H_ATT - 1 else (), inc=True)
            rsb = ret_sb[i % 2]
            P.op("scalar", lambda e, rsb=rsb: e.copy(rsb.ap[:], p_O.ap[:]),
                 reads=[p_O], writes=[rsb])
            P.dma("scalar", ad, ret_d.ap[i], rsb.ap[:], reads=[rsb],
                  writes=[ret_d])
            state_mms(ksb, vb)
            P.op("vector", lambda e: e.tensor_mul(Sg_b.ap[:], S_b.ap[:],
                                                  gl_b.ap[:]),
                 reads=[S_b, gl_b], writes=[Sg_b])
            P.op("vector", lambda e: e.tensor_add(S_b.ap[:], Sg_b.ap[:],
                                                  p_S.ap[:]),
                 reads=[Sg_b, p_S], writes=[S_b])

        for j in range(NCH):
            ksb, vb = ks_in[j % 4], v_in[j % 4]
            kw = kw_b[j % 2]
            P.dma("sync", sd, ksb.ap[:], kv_all_d.ap[j // OCH, 0, j % OCH],
                  reads=[kv_all_d], writes=[ksb],
                  extra=[(cc, 1)])
            P.dma("scalar", ad, vb.ap[:], kv_all_d.ap[j // OCH, 1, j % OCH],
                  reads=[kv_all_d], writes=[vb], extra=[(cc, 1)])
            P.op("vector", lambda e, j=j, ksb=ksb, kw=kw: e.tensor_mul(
                kw.ap[:].rearrange("d (h x) -> d h x", h=H_ATT),
                ksb.ap[:].rearrange("d (h x) -> d h x", h=H_ATT),
                wA_b.ap[:, j * H_ATT:(j + 1) * H_ATT].unsqueeze(2)
                .broadcast_to([128, H_ATT, DH_ATT])),
                reads=[ksb, wA_b], writes=[kw])
            for h in range(H_ATT):
                P.op("tensor",
                     lambda e, h=h, j=j, kw=kw, vb=vb: e.matmul(
                         p_S.ap[:, h * 64:(h + 1) * 64],
                         kw.ap[:, h * 64:(h + 1) * 64],
                         vb.ap[:, h * 64:(h + 1) * 64],
                         start=(j == 0 and h % 8 == 0),
                         stop=(j == NCH - 1 and h % 8 == 7)),
                     reads=[kw, vb],
                     extra=P.war(p_S) if (j == 0 and h == 0) else (),
                     writes=[p_S] if (j == NCH - 1 and h == H_ATT - 1)
                     else (), inc=True)
        P.op("scalar", lambda e: e.copy(Sg_b.ap[:], p_S.ap[:]),
             reads=[p_S], writes=[Sg_b])
        P.dma("scalar", ad, sa_d.ap, Sg_b.ap[:], reads=[Sg_b], writes=[sa_d])
        P.barrier()
        p2s.close()

    if phases >= 3:
        p3s = ExitStack()
        gnw_b = sbuf(p3s, "gnw_b", [128, C])
        gnb_b = sbuf(p3s, "gnb_b", [128, C])
        Tc_b = sbuf(p3s, "Tc_b", [64, C])
        Tc2_b = sbuf(p3s, "Tc2_b", [64, C])
        gl3_b = sbuf(p3s, "gl3_b", [64, C])
        qs3_in = [sbuf(p3s, f"qs3_{i}", [64, H_ATT * L]) for i in range(2)]
        ri2_b = [sbuf(p3s, f"ri2_{i}", [128, C]) for i in range(2)]
        p_corr = psum(p3s, "p_corr", [128, C])
        P.dma("sync", sd, gnw_b.ap[:], gnw_rep, writes=[gnw_b])
        P.dma("sync", sd, gnb_b.ap[:], gnb_rep, writes=[gnb_b])
        P.dma("sync", sd, gl3_b.ap[:], gl_c, writes=[gl3_b])
        P.dma("sync", sd, Tc_b.ap[:], sa_d.ap, reads=[sa_d], writes=[Tc_b])
        ret_in = [sbuf(p3s, f"ret_in{i}", [128, C]) for i in range(2)]
        xo_in = [sbuf(p3s, f"xo_in{i}", [128, C]) for i in range(2)]
        gstatl = [sbuf(p3s, f"gstat{i}", [128, 64]) for i in range(2)]
        gd_bl = [sbuf(p3s, f"gd_b{i}", [128, C]) for i in range(2)]
        gsq_bl = [sbuf(p3s, f"gsq_b{i}", [128, C]) for i in range(2)]
        gt_bl = [sbuf(p3s, f"gt_b{i}", [128, C]) for i in range(2)]
        gn_bl = [sbuf(p3s, f"gn_b{i}", [128, C]) for i in range(2)]
        g16al = [sbuf(p3s, f"g16a{i}", [128, C], FP16) for i in range(2)]
        g16bl = [sbuf(p3s, f"g16b{i}", [128, C], FP16) for i in range(2)]
        gfhl = [sbuf(p3s, f"gfh{i}", [128, C], FP16) for i in range(2)]
        gfll = [sbuf(p3s, f"gfl{i}", [128, C], FP16) for i in range(2)]
        x1_sb = [sbuf(p3s, f"x1_sb{i}", [128, C]) for i in range(2)]
        p_ghl = [psum(p3s, f"p_gh{i}", [128, C], FP16) for i in range(2)]
        p_gll = [psum(p3s, f"p_gl{i}", [128, C], FP16) for i in range(2)]
        p_W = psum(p3s, "p_W", [128, C])
        wsl3 = mk_wslots(p3s, ("a",))
        load_weight(wsl3, "a", wo_t)
        HA = H_ATT

        Tcl = [Tc_b, Tc2_b]

        def c_loads(i):
            ri = ret_in[i % 2]
            P.dma("sync", sd, ri.ap[:], ret_d.ap[i], reads=[ret_d], writes=[ri])
            xo = xo_in[i % 2]
            P.dma("sync", sd2, xo.ap[:],
                  x_attn.rearrange("(n p) c -> n p c", p=128)[PCH + i],
                  writes=[xo])
            qs3 = qs3_in[i % 2]
            P.dma("sync", sd2, qs3.ap[:].rearrange(
                "d (h t) -> d h t", h=H_ATT),
                qs_d.ap[i].rearrange("h d t -> d h t"),
                reads=[qs_d], writes=[qs3])

        def corr_mms(i):
            qs3, Tc = qs3_in[i % 2], Tcl[i % 2]
            for h in range(H_ATT):
                P.op("tensor",
                     lambda e, h=h, qs3=qs3, Tc=Tc: e.matmul(
                         p_corr.ap[:, h * 64:(h + 1) * 64],
                         qs3.ap[:, h * L:(h + 1) * L],
                         Tc.ap[:, h * 64:(h + 1) * 64],
                         start=True, stop=True),
                     reads=[qs3, Tc],
                     extra=P.war(p_corr) if h == 0 else (),
                     writes=[p_corr] if h == H_ATT - 1 else (), inc=True)

        c_loads(0)
        corr_mms(0)
        for i in range(OCH):
            gstat = gstatl[i % 2]
            gd_b, gsq_b = gd_bl[i % 2], gsq_bl[i % 2]
            gt_b, gn_b = gt_bl[i % 2], gn_bl[i % 2]
            g16a, g16b = g16al[i % 2], g16bl[i % 2]
            gfh, gfl = gfhl[i % 2], gfll[i % 2]
            p_gh, p_gl = p_ghl[i % 2], p_gll[i % 2]
            xo = xo_in[i % 2]
            ri2 = ri2_b[i % 2]
            ri = ret_in[i % 2]
            P.op("vector", lambda e, ri2=ri2, ri=ri: e.tensor_add(
                ri2.ap[:], ri.ap[:], p_corr.ap[:]),
                reads=[ri, p_corr], writes=[ri2])
            if i < OCH - 1:
                P.op("vector", lambda e, i=i: e.tensor_mul(
                    Tcl[(i + 1) % 2].ap[:], Tcl[i % 2].ap[:], gl3_b.ap[:]),
                    reads=[Tcl[i % 2], gl3_b], writes=[Tcl[(i + 1) % 2]])
                c_loads(i + 1)
                corr_mms(i + 1)
            ri = ri2
            r3 = ri.ap[:].rearrange("p (h d) -> p h d", h=HA)
            P.op("vector", lambda e, r3=r3, gstat=gstat: e.reduce_sum(
                gstat.ap[:, 0:HA], r3, axis=AX.X), reads=[ri], writes=[gstat])
            P.op("vector", lambda e, gstat=gstat: e.tensor_scalar_mul(
                gstat.ap[:, HA:2 * HA], gstat.ap[:, 0:HA], 1.0 / DH_ATT),
                reads=[gstat], writes=[gstat])
            mub = gstat.ap[:, HA:2 * HA].unsqueeze(2).broadcast_to(
                [128, HA, DH_ATT])
            P.op("vector", lambda e, r3=r3, mub=mub, gd_b=gd_b: e.tensor_sub(
                gd_b.ap[:].rearrange("p (h d) -> p h d", h=HA), r3, mub),
                reads=[ri, gstat], writes=[gd_b])
            P.op("scalar", lambda e, gsq_b=gsq_b, gd_b=gd_b: e.activation(
                gsq_b.ap[:], gd_b.ap[:], AF.Square),
                 reads=[gd_b], writes=[gsq_b])
            P.op("vector", lambda e, gstat=gstat, gsq_b=gsq_b: e.reduce_sum(
                gstat.ap[:, 2 * HA:3 * HA],
                gsq_b.ap[:].rearrange("p (h d) -> p h d", h=HA), axis=AX.X),
                reads=[gsq_b], writes=[gstat])
            P.op("vector", lambda e, gstat=gstat: e.tensor_scalar(
                gstat.ap[:, 3 * HA:4 * HA], gstat.ap[:, 2 * HA:3 * HA],
                1.0 / DH_ATT, EPS, op0=ALU.mult, op1=ALU.add),
                reads=[gstat], writes=[gstat])
            P.op("vector", lambda e, gstat=gstat: e.reciprocal(
                gstat.ap[:, 0:HA], gstat.ap[:, 3 * HA:4 * HA]),
                reads=[gstat], writes=[gstat])
            P.op("scalar", lambda e, gstat=gstat: e.activation(
                gstat.ap[:, HA:2 * HA], gstat.ap[:, 0:HA], AF.Sqrt),
                reads=[gstat], writes=[gstat])
            rb = gstat.ap[:, HA:2 * HA].unsqueeze(2).broadcast_to(
                [128, HA, DH_ATT])
            P.op("vector", lambda e, rb=rb, gt_b=gt_b, gd_b=gd_b: e.tensor_mul(
                gt_b.ap[:].rearrange("p (h d) -> p h d", h=HA),
                gd_b.ap[:].rearrange("p (h d) -> p h d", h=HA), rb),
                reads=[gd_b, gstat], writes=[gt_b])
            P.op("vector", lambda e, gn_b=gn_b, gt_b=gt_b: e.tensor_mul(
                gn_b.ap[:], gt_b.ap[:], gnw_b.ap[:]),
                 reads=[gt_b, gnw_b], writes=[gn_b])
            P.op("vector", lambda e, gn_b=gn_b: e.tensor_add(
                gn_b.ap[:], gn_b.ap[:], gnb_b.ap[:]),
                 reads=[gn_b, gnb_b], writes=[gn_b])
            split16(gn_b, gn_b.ap[:], g16a, g16a.ap[:], g16b, g16b.ap[:],
                    sub_eng="gpsimd")
            transpose16(g16a, g16a.ap, p_gh)
            transpose16(g16b, g16b.ap, p_gl)
            P.op("scalar", lambda e, gfh=gfh, p_gh=p_gh: e.copy(
                gfh.ap[:], p_gh.ap[:]),
                 reads=[p_gh], writes=[gfh])
            P.op("scalar", lambda e, gfl=gfl, p_gl=p_gl: e.copy(
                gfl.ap[:], p_gl.ap[:]),
                 reads=[p_gl], writes=[gfl])
            proj_tm(wsl3, p_W, "a", gfh, gfl)
            x1 = x1_sb[i % 2]
            P.op("vector", lambda e, x1=x1, xo=xo: e.tensor_add(
                x1.ap[:], p_W.ap[:], xo.ap[:]),
                reads=[p_W, xo], writes=[x1])
            P.dma("scalar", ad, ag_in.ap.rearrange("(n p) c -> n p c", p=128)[i],
                  x1.ap[:], reads=[x1], writes=[ag_in])
            if "x1" in dbg_aps:
                P.dma("scalar", ad,
                      dbg_aps["x1"].rearrange("(n p) c -> n p c", p=128)[i],
                      x1.ap[:], reads=[x1])
        P.barrier()
        p3s.close()

    if phases >= 4:
        p4s = ExitStack()
        fnw_b = sbuf(p4s, "fnw_b", [128, C])
        fnb_b = sbuf(p4s, "fnb_b", [128, C])
        fvb_b = sbuf(p4s, "fvb_b", [128, C])
        fqb_b = sbuf(p4s, "fqb_b", [128, 8])
        fkb_b = sbuf(p4s, "fkb_b", [128, 8])
        for bb, ap in ((fnw_b, fnw_rep), (fnb_b, fnb_rep), (fvb_b, fvb_rep),
                       (fqb_b, fqb_fm), (fkb_b, fkb_fm)):
            P.dma("sync", sd, bb.ap[:], ap, writes=[bb])
        xgrpl = [sbuf(p4s, f"xgrp{i}", [128, 4 * C]) for i in range(2)]
        sq4l = [sbuf(p4s, f"sq4_{i}", [128, C]) for i in range(2)]
        stat4l = [sbuf(p4s, f"stat4_{i}", [128, 64]) for i in range(2)]
        s4dl = [sbuf(p4s, f"s4d{i}", [128, C]) for i in range(2)]
        xff_tl = [sbuf(p4s, f"xff_t{i}", [128, C]) for i in range(2)]
        f16al = [sbuf(p4s, f"f16a{i}", [128, C], FP16) for i in range(2)]
        f16bl = [sbuf(p4s, f"f16b{i}", [128, C], FP16) for i in range(2)]
        xg_hil = [sbuf(p4s, f"xg_hi{i}", [128, 8 * 512], FP16)
                  for i in range(2)]
        xg_lol = [sbuf(p4s, f"xg_lo{i}", [128, 8 * 512], FP16)
                  for i in range(2)]
        xgd_hi = dscr("xgd_hi", [OG, 128, 8 * 512], FP16)
        xgd_lo = dscr("xgd_lo", [OG, 128, 8 * 512], FP16)
        qf_tl = [sbuf(p4s, f"qf_t{i}", [128, 512]) for i in range(2)]
        q16hl = [sbuf(p4s, f"q16h{i}", [128, 512], FP16) for i in range(2)]
        q16ll = [sbuf(p4s, f"q16l{i}", [128, 512], FP16) for i in range(2)]
        p_trh4l = [psum(p4s, f"p_trh4{i}", [128, C], FP16) for i in range(2)]
        p_trl4l = [psum(p4s, f"p_trl4{i}", [128, C], FP16) for i in range(2)]
        p_p4l = [psum(p4s, f"p_p4{i}", [128, 512]) for i in range(2)]
        wsl4 = mk_wslots(p4s, ("a", "b"))

        def ln2_group(src_dram, src_buf, g, dst_dbg=None):
            gp = g % 2
            xg_hi, xg_lo = xg_hil[gp], xg_lol[gp]
            xgr = xgrpl[gp]
            st4 = stat4l[gp]
            P.dma("sync", sd, xgr.ap[:].rearrange("p (u c) -> p u c", u=4),
                  src_dram.rearrange("(gg u p) c -> gg p u c", u=4, p=128)[g],
                  reads=[src_buf] if src_buf is not None else (),
                  writes=[xgr])
            x4 = xgr.ap[:].rearrange("p (u c) -> p u c", u=4)
            P.op("vector", lambda e, st4=st4, x4=x4: e.reduce_sum(
                st4.ap[:, 0:4], x4, axis=AX.X), reads=[xgr], writes=[st4])
            for u in range(4):
                sq4 = sq4l[u % 2]
                P.op("scalar", lambda e, sq4=sq4, xgr=xgr, u=u: e.activation(
                    sq4.ap[:], xgr.ap[:, u * C:(u + 1) * C], AF.Square),
                    reads=[xgr], writes=[sq4])
                P.op("vector", lambda e, st4=st4, sq4=sq4, u=u: e.reduce_sum(
                    st4.ap[:, 4 + u:5 + u], sq4.ap[:], axis=AX.X),
                    reads=[sq4], writes=[st4])
            P.op("vector", lambda e, st4=st4: e.tensor_scalar(
                st4.ap[:, 8:12], st4.ap[:, 0:4], 1.0 / C, None, op0=ALU.mult),
                reads=[st4], writes=[st4])
            P.op("vector", lambda e, st4=st4: e.tensor_mul(
                st4.ap[:, 12:16], st4.ap[:, 8:12], st4.ap[:, 8:12]),
                reads=[st4], writes=[st4])
            P.op("vector", lambda e, st4=st4: e.tensor_scalar(
                st4.ap[:, 16:20], st4.ap[:, 4:8], 1.0 / C, None, op0=ALU.mult),
                reads=[st4], writes=[st4])
            P.op("vector", lambda e, st4=st4: e.tensor_sub(
                st4.ap[:, 20:24], st4.ap[:, 16:20], st4.ap[:, 12:16]),
                reads=[st4], writes=[st4])
            P.op("vector", lambda e, st4=st4: e.tensor_scalar(
                st4.ap[:, 24:28], st4.ap[:, 20:24], EPS, None, op0=ALU.add),
                reads=[st4], writes=[st4])
            P.op("vector", lambda e, st4=st4: e.reciprocal(
                st4.ap[:, 28:32], st4.ap[:, 24:28]),
                reads=[st4], writes=[st4])
            P.op("scalar", lambda e, st4=st4: e.activation(
                st4.ap[:, 32:36], st4.ap[:, 28:32], AF.Sqrt),
                reads=[st4], writes=[st4])
            P.op("vector", lambda e, st4=st4: e.tensor_mul(
                st4.ap[:, 36:40], st4.ap[:, 20:24], st4.ap[:, 28:32]),
                reads=[st4], writes=[st4])
            P.op("vector", lambda e, st4=st4: e.tensor_scalar(
                st4.ap[:, 40:44], st4.ap[:, 36:40], EPS, None, op0=ALU.add),
                reads=[st4], writes=[st4])
            P.op("vector", lambda e, st4=st4: e.reciprocal(
                st4.ap[:, 44:48], st4.ap[:, 40:44]),
                reads=[st4], writes=[st4])
            P.op("scalar", lambda e, st4=st4: e.activation(
                st4.ap[:, 48:52], st4.ap[:, 44:48], AF.Sqrt),
                reads=[st4], writes=[st4])
            P.op("vector", lambda e, st4=st4: e.tensor_mul(
                st4.ap[:, 52:56], st4.ap[:, 32:36], st4.ap[:, 48:52]),
                reads=[st4], writes=[st4])
            for u in range(4):
                par = u % 2
                xff_t = xff_tl[par]
                f16a, f16b = f16al[par], f16bl[par]
                p_trh4, p_trl4 = p_trh4l[par], p_trl4l[par]
                xsl = xgr.ap[:, u * C:(u + 1) * C]
                s4d = s4dl[par]
                P.op("vector", lambda e, st4=st4, xsl=xsl, u=u,
                     s4d=s4d: e.tensor_scalar(
                         s4d.ap[:], xsl, st4.ap[:, 8 + u:9 + u],
                         st4.ap[:, 52 + u:53 + u],
                         op0=ALU.subtract, op1=ALU.mult),
                     reads=[xgr, st4], writes=[s4d])
                P.op("vector", lambda e, xff_t=xff_t, s4d=s4d: e.tensor_mul(
                    xff_t.ap[:], s4d.ap[:], fnw_b.ap[:]),
                    reads=[s4d, fnw_b], writes=[xff_t])
                P.op("vector", lambda e, xff_t=xff_t: e.tensor_add(
                    xff_t.ap[:], xff_t.ap[:], fnb_b.ap[:]),
                    reads=[xff_t, fnb_b], writes=[xff_t])
                if dst_dbg is not None:
                    P.dma("scalar", ad, dst_dbg[g * 4 + u], xff_t.ap[:],
                          reads=[xff_t])
                split16(xff_t, xff_t.ap[:], f16a, f16a.ap[:], f16b,
                        f16b.ap[:], sub_eng="gpsimd")
                transpose16(f16a, f16a.ap, p_trh4)
                transpose16(f16b, f16b.ap, p_trl4)
                for (pbuf, dbuf) in ((p_trh4, xg_hi), (p_trl4, xg_lo)):
                    P.op("scalar",
                         lambda e, pbuf=pbuf, dbuf=dbuf, u=u: e.copy(
                             dbuf.ap[:].rearrange(
                                 "p (cc t) -> p cc t", cc=8)[:, :,
                                                             u * 128:(u + 1) * 128],
                             pbuf.ap[:].rearrange("p (cc t) -> p cc t", cc=8)),
                         reads=[pbuf], writes=[dbuf] if u == 3 else ())
            return xg_hi, xg_lo

        load_weight(wsl4, "a", fkw_t)
        load_weight(wsl4, "b", fvw_t)
        kf_tl = [sbuf(p4s, f"kf_t{i}", [128, 512]) for i in range(2)]
        kf16_l = [sbuf(p4s, f"kf16{i}", [128, 512], FP16) for i in range(2)]
        kf16l_l = [sbuf(p4s, f"kf16l{i}", [128, 512], FP16) for i in range(2)]
        vt4l = [sbuf(p4s, f"vt4{i}", [128, C]) for i in range(2)]
        p_v4 = psum(p4s, "p_v4", [128, C])
        for g in range(OG):
            xg_hi, xg_lo = ln2_group(ag_in.ap, ag_in, g,
                                     dst_dbg=dbg_aps.get("xff") and
                                     dbg_aps["xff"].rearrange(
                                         "(n p) c -> n p c", p=128))
            P.dma("scalar", ad, xgd_hi.ap[g], xg_hi.ap[:],
                  reads=[xg_hi], writes=[xgd_hi])
            P.dma("scalar", ad, xgd_lo.ap[g], xg_lo.ap[:],
                  reads=[xg_lo], writes=[xgd_lo])
            mcc = 8
            for h in range(H_FFN):
                p_p4 = p_p4l[h % 2]
                kf_t = kf_tl[h % 2]
                kf16, kf16l = kf16_l[h % 2], kf16l_l[h % 2]
                for ccx in range(mcc):
                    last = ccx == mcc - 1
                    mm3(p_p4.ap[:],
                        wsl4[0]["a"].ap[:, ccx * C + h * 128:ccx * C + (h + 1) * 128],
                        wsl4[1]["a"].ap[:, ccx * C + h * 128:ccx * C + (h + 1) * 128],
                        xg_hi.ap[:, ccx * 512:(ccx + 1) * 512],
                        xg_lo.ap[:, ccx * 512:(ccx + 1) * 512],
                        start=(ccx == 0), stop=last,
                        reads=[wsl4[0]["a"], wsl4[1]["a"], xg_hi, xg_lo],
                        writes=[p_p4] if last else (), inc=last,
                        ps_buf=p_p4 if ccx == 0 else None)
                P.op("vector", lambda e, h=h, p_p4=p_p4, kf_t=kf_t: e.tensor_scalar(
                    kf_t.ap[:], p_p4.ap[:], fkb_b.ap[:, h:h + 1], None,
                    op0=ALU.add), reads=[p_p4, fkb_b], writes=[kf_t])
                split16(kf_t, kf_t.ap[:], kf16, kf16.ap[:], kf16l,
                        kf16l.ap[:], sub_eng="gpsimd")
                P.dma("scalar", ad, pk_kf(0, g, h), kf16.ap[:],
                      reads=[kf16], writes=[pk_own_g[g]])
                P.dma("scalar", ad, pk_kf(1, g, h), kf16l.ap[:],
                      reads=[kf16l], writes=[pk_own_g[g]])
            P.op("gpsimd", lambda e, g=g: e.collective_compute(
                "AllGather", ALU.bypass,
                replica_groups=[[0, 1, 2, 3], [4, 5, 6, 7]],
                ins=[pk_own_g[g].ap], outs=[pk_all_g[g].ap]),
                reads=[pk_own_g[g]], writes=[pk_all_g[g]], sem=cc, step=1)
            for u in range(4):
                vt4 = vt4l[u % 2]
                for ccx in range(mcc):
                    for half in range(2):
                        last = ccx == mcc - 1 and half == 1
                        mm3(p_v4.ap[:, half * 512:(half + 1) * 512],
                            xg_hi.ap[:, ccx * 512 + u * 128:
                                     ccx * 512 + (u + 1) * 128],
                            xg_lo.ap[:, ccx * 512 + u * 128:
                                     ccx * 512 + (u + 1) * 128],
                            wsl4[0]["b"].ap[:, ccx * C + half * 512:
                                            ccx * C + (half + 1) * 512],
                            wsl4[1]["b"].ap[:, ccx * C + half * 512:
                                            ccx * C + (half + 1) * 512],
                            start=(ccx == 0), stop=(ccx == mcc - 1),
                            reads=[xg_hi, xg_lo, wsl4[0]["b"], wsl4[1]["b"]],
                            writes=[p_v4] if last else (), inc=last,
                            ps_buf=p_v4 if (ccx == 0 and half == 0) else None)
                P.op("vector", lambda e, vt4=vt4: e.tensor_add(
                    vt4.ap[:], p_v4.ap[:], fvb_b.ap[:]),
                    reads=[p_v4, fvb_b], writes=[vt4])
                P.dma("scalar", ad, vf_own_g[g].ap[u], vt4.ap[:],
                      reads=[vt4], writes=[vf_own_g[g]])
                if "vff" in dbg_aps:
                    P.dma("scalar", ad, dbg_aps["vff"][g * 4 + u], vt4.ap[:],
                          reads=[vt4])
            P.op("gpsimd", lambda e, g=g: e.collective_compute(
                "AllGather", ALU.bypass,
                replica_groups=[[0, 1, 2, 3], [4, 5, 6, 7]],
                ins=[vf_own_g[g].ap], outs=[vf_all_g[g].ap]),
                reads=[vf_own_g[g]], writes=[vf_all_g[g]], sem=cc, step=1)
        load_weight(wsl4, "a", fqw_t)
        for g in range(OG):
            gp = g % 2
            xg_hi, xg_lo = xg_hil[gp], xg_lol[gp]
            P.dma("sync", sd, xg_hi.ap[:], xgd_hi.ap[g],
                  reads=[xgd_hi], writes=[xg_hi])
            P.dma("sync", sd2, xg_lo.ap[:], xgd_lo.ap[g],
                  reads=[xgd_lo], writes=[xg_lo])
            for h in range(H_FFN):
                p_p4 = p_p4l[h % 2]
                qf_t = qf_tl[h % 2]
                for ccx in range(8):
                    last = ccx == 7
                    mm3(p_p4.ap[:],
                        wsl4[0]["a"].ap[:, ccx * C + h * 128:ccx * C + (h + 1) * 128],
                        wsl4[1]["a"].ap[:, ccx * C + h * 128:ccx * C + (h + 1) * 128],
                        xg_hi.ap[:, ccx * 512:(ccx + 1) * 512],
                        xg_lo.ap[:, ccx * 512:(ccx + 1) * 512],
                        start=(ccx == 0), stop=last,
                        reads=[wsl4[0]["a"], wsl4[1]["a"], xg_hi, xg_lo],
                        writes=[p_p4] if last else (), inc=last,
                        ps_buf=p_p4 if ccx == 0 else None)
                P.op("vector", lambda e, h=h, p_p4=p_p4, qf_t=qf_t: e.tensor_scalar(
                    qf_t.ap[:], p_p4.ap[:], fqb_b.ap[:, h:h + 1], None,
                    op0=ALU.add), reads=[p_p4, fqb_b], writes=[qf_t])
                q16h, q16l = q16hl[h % 2], q16ll[h % 2]
                split16(qf_t, qf_t.ap[:], q16h, q16h.ap[:], q16l,
                        q16l.ap[:], sub_eng="gpsimd")
                P.dma("scalar", ad, qh_d.ap[h][:, g * 512:(g + 1) * 512],
                      q16h.ap[:], reads=[q16h], writes=[qh_d])
                P.dma("scalar", ad, ql_d.ap[h][:, g * 512:(g + 1) * 512],
                      q16l.ap[:], reads=[q16l], writes=[ql_d])


        P.barrier(skip_cc=True)
        p4s.close()

    if phases >= 5:
        p5s = ExitStack()
        kv_h = [sbuf(p5s, f"kvh{i}", [128, 128], FP16) for i in range(3)]
        kv_l = [sbuf(p5s, f"kvl{i}", [128, 128], FP16) for i in range(3)]
        vv = [sbuf(p5s, f"vv{i}", [128, 128]) for i in range(3)]
        qsl_h = [sbuf(p5s, f"qslh{i}", [128, own], FP16) for i in range(2)]
        qsl_l = [sbuf(p5s, f"qsll{i}", [128, own], FP16) for i in range(2)]
        pP = [sbuf(p5s, f"pP{i}", [128, 512]) for i in range(4)]
        accg = [sbuf(p5s, f"accg{i}", [128, 512]) for i in range(QT)]
        accv = [sbuf(p5s, f"accv{i}", [128, 512]) for i in range(QT)]
        a_raw = sbuf(p5s, "a_raw", [128, 512])
        a_div = sbuf(p5s, "a_div", [128, 512])
        a_h16 = sbuf(p5s, "a_h16", [128, 512], FP16)
        a_l16 = sbuf(p5s, "a_l16", [128, 512], FP16)
        sums_sb = sbuf(p5s, "sums_sb", [1, 512])
        recip_sb = sbuf(p5s, "recip_sb", [1, 512])
        p_out = [psum(p5s, f"p_out{i}", [128, 512]) for i in range(QT)]
        p_sc = [psum(p5s, f"p_sc{i}", [128, 512]) for i in range(max(QT, 2))]

        pend_pv = []
        for h in range(H_FFN):
            qh_s, ql_s = qsl_h[h % 2], qsl_l[h % 2]
            P.dma("sync", sd2, qh_s.ap[:], qh_d.ap[h], reads=[qh_d],
                  writes=[qh_s])
            P.dma("sync", sd2, ql_s.ap[:], ql_d.ap[h], reads=[ql_d],
                  writes=[ql_s])
            for qt in range(QT):
                P.op("gpsimd", lambda e, qt=qt: e.memset(accg[qt].ap[:], 0.0),
                     writes=[accg[qt]])
                P.op("vector", lambda e, qt=qt: e.memset(accv[qt].ap[:], 0.0),
                     writes=[accv[qt]])
            for kb in range(NCH):
                sl = kb % 3
                kh, kl, vb = kv_h[sl], kv_l[sl], vv[sl]
                rk, gg, sub = kb // OCH, (kb % OCH) // 4, kb % 4
                kh_src = pk_kf(0, gg, h, rk)
                kl_src = pk_kf(1, gg, h, rk)
                P.dma("sync", sd, kh.ap[:],
                      bass.AP(kh_src.tensor, kh_src.offset + sub * 128,
                              [[512, DH_FFN], [1, 128]]),
                      reads=[pk_all_g[gg]], writes=[kh])
                P.dma("sync", sd, kl.ap[:],
                      bass.AP(kl_src.tensor, kl_src.offset + sub * 128,
                              [[512, DH_FFN], [1, 128]]),
                      reads=[pk_all_g[gg]], writes=[kl])
                P.dma("sync", sd2, vb.ap[:],
                      vf_all_g[gg].ap[rk, sub].rearrange(
                          "p (hh d) -> p hh d", hh=H_FFN)[:, h],
                      reads=[vf_all_g[gg]], writes=[vb])
                for qt in range(QT):
                    psc = p_sc[qt]
                    co = qt * 512
                    P.op("tensor",
                         lambda e, psc=psc, kh=kh, co=co, qh_s=qh_s: e.matmul(
                             psc.ap[:], kh.ap[:], qh_s.ap[:, co:co + 512],
                             start=True, stop=False),
                         reads=[kh, kl, qh_s, ql_s],
                         extra=P.war(psc), inc=False)
                    P.op("tensor",
                         lambda e, psc=psc, kh=kh, co=co, ql_s=ql_s: e.matmul(
                             psc.ap[:], kh.ap[:], ql_s.ap[:, co:co + 512],
                             start=False, stop=False), inc=False)
                for qt in range(QT):
                    psc = p_sc[qt]
                    co = qt * 512
                    P.op("tensor",
                         lambda e, psc=psc, kl=kl, co=co, qh_s=qh_s: e.matmul(
                             psc.ap[:], kl.ap[:], qh_s.ap[:, co:co + 512],
                             start=False, stop=True),
                         reads=[kh, kl, qh_s, ql_s] if qt == QT - 1 else (),
                         writes=[psc], inc=True)
                for f in pend_pv:
                    f()
                pend_pv = []
                for qt in range(QT):
                    psc = p_sc[qt]
                    pp = pP[qt]
                    P.op("scalar", lambda e, pp=pp, psc=psc: e.activation(
                        pp.ap[:], psc.ap[:], AF.Exp, scale=float(1.0 / 32.0)),
                        reads=[psc], writes=[pp])

                    def emit_pv(pp=pp, vb=vb, qt=qt, kb=kb):
                        P.op("tensor",
                             lambda e, pp=pp, vb=vb, qt=qt, kb=kb: e.matmul(
                                 p_out[qt].ap[:], vb.ap[:], pp.ap[:],
                                 start=(kb == 0), stop=(kb == NCH - 1)),
                             reads=[vb, pp],
                             extra=P.war(p_out[qt]) if kb == 0 else (),
                             writes=[p_out[qt]] if kb == NCH - 1 else (),
                             inc=True)
                    pend_pv.append(emit_pv)
                    if kb % 2 == 0:
                        P.op("gpsimd", lambda e, pp=pp, qt=qt: e.tensor_add(
                            accg[qt].ap[:], accg[qt].ap[:], pp.ap[:]),
                            reads=[pp, accg[qt]], writes=[accg[qt]])
                    else:
                        P.op("vector", lambda e, pp=pp, qt=qt: e.tensor_add(
                            accv[qt].ap[:], accv[qt].ap[:], pp.ap[:]),
                            reads=[pp, accv[qt]], writes=[accv[qt]])
            for f in pend_pv:
                f()
            pend_pv = []
            for qt in range(QT):
                co = h * own + qt * 512
                P.op("scalar", lambda e, qt=qt: e.copy(a_raw.ap[:],
                                                       p_out[qt].ap[:]),
                     reads=[p_out[qt]], writes=[a_raw])
                P.op("tensor", lambda e, qt=qt: e.matmul(
                    p_sc[0].ap[0:1, :], onesc_b.ap[:], accg[qt].ap[:],
                    start=True, stop=False),
                    reads=[onesc_b, accg[qt]], extra=P.war(p_sc[0]),
                    inc=False)
                P.op("tensor", lambda e, qt=qt: e.matmul(
                    p_sc[0].ap[0:1, :], onesc_b.ap[:], accv[qt].ap[:],
                    start=False, stop=True),
                    reads=[onesc_b, accv[qt]], writes=[p_sc[0]], inc=True)
                P.op("scalar", lambda e: e.copy(sums_sb.ap[:],
                                                p_sc[0].ap[0:1, :]),
                     reads=[p_sc[0]], writes=[sums_sb])
                P.op("vector", lambda e: e.reciprocal(recip_sb.ap[:],
                                                      sums_sb.ap[:]),
                     reads=[sums_sb], writes=[recip_sb])
                P.op("tensor", lambda e: e.matmul(
                    p_sc[1].ap[:], onesr_b.ap[:], recip_sb.ap[:],
                    start=True, stop=True),
                    reads=[onesr_b, recip_sb], extra=P.war(p_sc[1]),
                    writes=[p_sc[1]], inc=True)
                P.op("vector", lambda e: e.tensor_mul(a_div.ap[:], a_raw.ap[:],
                                                      p_sc[1].ap[:]),
                     reads=[a_raw, p_sc[1]], writes=[a_div])
                split16(a_div, a_div.ap[:], a_h16, a_h16.ap[:],
                        a_l16, a_l16.ap[:], sub_eng="gpsimd")
                P.dma("scalar", ad, ah_d.ap[h][:, qt * 512:(qt + 1) * 512],
                      a_h16.ap[:], reads=[a_h16], writes=[ah_d])
                P.dma("scalar", ad, al_d.ap[h][:, qt * 512:(qt + 1) * 512],
                      a_l16.ap[:], reads=[a_l16], writes=[al_d])
        P.barrier()
        p5s.close()

    if phases >= 6:
        p6s = ExitStack()
        ph6 = {}
        ph6["stats"] = {k: sbuf(p6s, f"s6_{k}", [128, 4]) for k in
                        ("mu", "var", "r")}
        ph6["stats"]["d"] = sbuf(p6s, "s6_d", [128, C])
        ph6["stats"]["sq"] = sbuf(p6s, "s6_sq", [128, C])
        fob_b = sbuf(p6s, "fob_b", [128, C])
        P.dma("sync", sd, fob_b.ap[:], fob_rep, writes=[fob_b])
        x1_in = [sbuf(p6s, f"x1_in{i}", [128, C]) for i in range(2)]
        xn2f = sbuf(p6s, "xn2f", [128, C])
        ot_sb = [sbuf(p6s, f"ot_sb{i}", [128, C]) for i in range(2)]
        a_hin = [sbuf(p6s, f"a_hin{i}", [128, H_FFN * 128], FP16)
                 for i in range(2)]
        a_lin = [sbuf(p6s, f"a_lin{i}", [128, H_FFN * 128], FP16)
                 for i in range(2)]
        p_W6 = psum(p6s, "p_W6", [128, C])
        wsl6 = mk_wslots(p6s, ("a",))
        load_weight(wsl6, "a", fow_t)

        for i in range(OCH):
            xi = x1_in[i % 2]
            P.dma("sync", sd, xi.ap[:],
                  ag_in.ap.rearrange("(n p) c -> n p c", p=128)[i],
                  reads=[ag_in], writes=[xi])
            ah, al = a_hin[i % 2], a_lin[i % 2]
            P.dma("sync", sd2, ah.ap[:].rearrange("d (h t) -> d h t", h=H_FFN),
                  ah_d.ap[:, :, i * 128:(i + 1) * 128].transpose([1, 0, 2]),
                  reads=[ah_d], writes=[ah])
            P.dma("sync", sd2, al.ap[:].rearrange("d (h t) -> d h t", h=H_FFN),
                  al_d.ap[:, :, i * 128:(i + 1) * 128].transpose([1, 0, 2]),
                  reads=[al_d], writes=[al])
            ln_tile(ph6, xi, xi.ap[:], xn2f, xn2f.ap[:])
            for mc in range(8):
                for half in range(2):
                    last = mc == 7 and half == 1
                    mm3(p_W6.ap[:, half * 512:(half + 1) * 512],
                        ah.ap[:, mc * 128:(mc + 1) * 128],
                        al.ap[:, mc * 128:(mc + 1) * 128],
                        wsl6[0]["a"].ap[:, mc * C + half * 512:
                                        mc * C + (half + 1) * 512],
                        wsl6[1]["a"].ap[:, mc * C + half * 512:
                                        mc * C + (half + 1) * 512],
                        start=(mc == 0), stop=(mc == 7),
                        reads=[ah, al, wsl6[0]["a"], wsl6[1]["a"]],
                        writes=[p_W6] if last else (), inc=last,
                        ps_buf=p_W6 if (mc == 0 and half == 0) else None)
            ot = ot_sb[i % 2]
            P.op("vector", lambda e, ot=ot: e.tensor_add(ot.ap[:], p_W6.ap[:],
                                                         fob_b.ap[:]),
                 reads=[p_W6, fob_b], writes=[ot])
            P.op("vector", lambda e, ot=ot: e.tensor_add(ot.ap[:], ot.ap[:],
                                                         xn2f.ap[:]),
                 reads=[ot, xn2f], writes=[ot])
            P.op("vector", lambda e, ot=ot, xi=xi: e.tensor_add(
                ot.ap[:], ot.ap[:], xi.ap[:]),
                reads=[ot, xi], writes=[ot])
            P.dma("scalar", ad, out_d.rearrange("(n p) c -> n p c", p=128)[i],
                  ot.ap[:], reads=[ot])
        P.barrier()
        p6s.close()

    with nc.Block() as blk:
        P.resolve_and_replay(blk)
    st.close()
    return nc


def kernel(**inputs):
    T = T_GLOB
    own = T // 4
    nc = build_program(T)
    in_maps = []
    for core in range(NCORES):
        m = {}
        m.update(host_constants(T, core))
        m.update(host_inputs(T, core, inputs))
        in_maps.append(m)
    res = run_bass_kernel_spmd(nc, in_maps, core_ids=list(range(NCORES)))
    y = np.empty((B_GLOB, T, C), np.float32)
    for core in range(NCORES):
        beta, r = core // 4, core % 4
        y[beta, r * own:(r + 1) * own] = res.results[core]["out"]
    return y


# version 18
# speedup vs baseline: 1.0372x; 1.0111x over previous
import numpy as np
from contextlib import ExitStack

import concourse.bass as bass
import concourse.mybir as mybir
from concourse.bass_utils import run_bass_kernel_spmd

F32 = mybir.dt.float32
FP16 = mybir.dt.float16
AF = mybir.ActivationFunctionType
ALU = mybir.AluOpType
AX = mybir.AxisListType

B_GLOB, T_GLOB, C = 2, 8192, 1024
NCORES = 8
L = 128
H_ATT, DH_ATT = 16, 64
H_FFN, DH_FFN = 8, 128
EPS = 1e-5
ENGS = ("sync", "scalar", "vector", "tensor", "gpsimd")


class Sem:
    __slots__ = ("h", "n", "is_cc")

    def __init__(self, stack, nc, name):
        self.h = stack.enter_context(nc.semaphore(name))
        self.n = 0
        self.is_cc = False


class Buf:
    __slots__ = ("ap", "w", "rs")

    def __init__(self, ap=None):
        self.ap = ap
        self.w = None
        self.rs = []


class DmaPool:

    def __init__(self, prog, name, n):
        self.sems = [Sem(prog.stack, prog.nc, f"{name}{i}") for i in range(n)]
        self.last = [None] * n
        self.i = 0


class Tick:
    __slots__ = ("eng", "seq", "sem", "val")

    def __init__(self, eng, seq):
        self.eng = eng
        self.seq = seq
        self.sem = None
        self.val = None


class Prog:

    def __init__(self, nc, stack):
        self.nc = nc
        self.stack = stack
        self.ops = {e: [] for e in ENGS}
        self.tick = {e: Sem(stack, nc, f"tick_{e}") for e in ENGS}
        self.extra_sems = []

    def op(self, eng, fn, reads=(), writes=(), extra=(), inc=True, step=1,
           sem=None):
        deps = []
        for b in reads:
            if b.w is not None:
                deps.append(b.w)
        for b in writes:
            if b.w is not None:
                deps.append(b.w)
            deps.extend(b.rs)
        for d in extra:
            if d is None:
                continue
            if isinstance(d, tuple):
                t0 = Tick(d[0], -1)
                t0.sem, t0.val = d[0], d[1]
                deps.append(t0)
            else:
                deps.append(d)
        seq = len(self.ops[eng])
        if sem is not None:
            t = Tick(sem, seq)
            sem.n += step
            t.sem, t.val = sem, sem.n
        elif inc:
            t = Tick(eng, seq)
        else:
            t = None
        self.ops[eng].append((fn, deps, t, sem, step))
        if t is not None:
            for b in reads:
                b.rs.append(t)
            for b in writes:
                b.w = t
                b.rs = []
        return t

    @staticmethod
    def war(buf):
        d = list(buf.rs)
        if buf.w is not None:
            d.append(buf.w)
        return d

    def dma(self, eng, pool, out_ap, in_ap, reads=(), writes=(), extra=(),
            nobc=False):
        i = pool.i
        pool.i = (i + 1) % len(pool.sems)
        s = pool.sems[i]
        ex = list(extra)
        if pool.last[i] is not None:
            ex.append(pool.last[i])
        if nobc:
            t = self.op(eng, lambda e: e.dma_start(out=out_ap, in_=in_ap,
                                                   bounds_check=None),
                        reads=reads, writes=writes, extra=ex, sem=s, step=16)
        else:
            t = self.op(eng, lambda e: e.dma_start(out=out_ap, in_=in_ap),
                        reads=reads, writes=writes, extra=ex, sem=s, step=16)
        pool.last[i] = t
        return t

    def barrier(self, skip_cc=False):
        self._skip_cc = skip_cc
        marks = []
        for e in ENGS:
            for seq in range(len(self.ops[e]) - 1, -1, -1):
                fn = self.ops[e][seq][0]
                if fn == "BARRIER":
                    continue
                t_existing = self.ops[e][seq][2]
                sem_over = self.ops[e][seq][3]
                if sem_over is not None and skip_cc and \
                        getattr(sem_over, "is_cc", False):
                    continue
                if sem_over is not None and t_existing is not None:
                    marks.append((sem_over, t_existing.val))
                else:
                    marks.append(Tick(e, seq))
                break
        sems = [(s, s.n) for s in self.extra_sems
                if s.n > 0 and not (skip_cc and getattr(s, "is_cc", False))]
        for e in ENGS:
            deps = []
            sdeps = list(sems)
            for m in marks:
                if isinstance(m, tuple):
                    sdeps.append(m)
                elif m.eng != e:
                    deps.append(m)
            self.ops[e].append(("BARRIER", deps, None, sdeps, 1))

    def resolve_and_replay(self, blk):
        marked = {e: set() for e in ENGS}
        for e in ENGS:
            for (fn, deps, t, sem, step) in self.ops[e]:
                for d in deps:
                    if isinstance(d, Tick) and d.sem is None and d.val is None:
                        marked[d.eng].add(d.seq)
        tickval = {}
        for e in ENGS:
            for rank, seq in enumerate(sorted(marked[e])):
                tickval[(e, seq)] = rank + 1
        def run_engine(en, ename):
            wm = {}
            cnt = 0
            for (fn, deps, t, sem, step) in self.ops[ename]:
                waits = []
                if fn == "BARRIER":
                    for d in deps:
                        s = self.tick[d.eng]
                        v = tickval.get((d.eng, d.seq))
                        if v is None:
                            v = sum(1 for q in marked[d.eng] if q <= d.seq)
                        else:
                            pass
                        if v and wm.get(id(s), 0) < v:
                            wm[id(s)] = v
                            waits.append((s.h, v))
                    for (s, v) in sem:
                        if wm.get(id(s), 0) < v:
                            wm[id(s)] = v
                            waits.append((s.h, v))
                    for sh, v in waits:
                        en.wait_ge(sh, v)
                    continue
                for d in deps:
                    if d.sem is not None:
                        s, v = d.sem, d.val
                    else:
                        s = self.tick[d.eng]
                        v = tickval.get((d.eng, d.seq))
                        if v is None:
                            v = sum(1 for q in marked[d.eng] if q <= d.seq)
                        if v == 0:
                            continue
                    if wm.get(id(s), 0) < v:
                        wm[id(s)] = v
                        waits.append((s.h, v))
                for sh, v in waits:
                    en.wait_ge(sh, v)
                ins = fn(en)
                if sem is not None and t is not None:
                    ins.then_inc(sem.h, step)
                elif t is not None and t.seq in marked[ename]:
                    ins.then_inc(self.tick[ename].h, 1)
                cnt += 1
        for ename in ENGS:
            getattr(blk, ename)(lambda en, ename=ename: run_engine(en, ename))

def host_constants(T, core):
    r = core % 4
    own = T // 4
    q1 = own * (r + 1)
    g = 1.0 - 2.0 ** (-5.0 - np.arange(H_ATT, dtype=np.float64))
    cons = {}
    pos = np.arange(T, dtype=np.float64) + (q1 - T)
    invf = 10000.0 ** (-(np.arange(32, dtype=np.float64) / 32.0))
    fr = pos[:, None] * invf[None, :]
    emb = np.concatenate([fr, fr], axis=1)
    cons["cos_t"] = np.cos(emb).astype(np.float32)
    cons["sin_t"] = np.sin(emb).astype(np.float32)
    i_loc = np.arange(L, dtype=np.float64)
    dt_mask = np.zeros((L, H_ATT * L), dtype=np.float32)
    for h in range(H_ATT):
        M = np.where(i_loc[None, :] >= i_loc[:, None],
                     g[h] ** (i_loc[None, :] - i_loc[:, None]), 0.0) / 8.0
        dt_mask[:, h * L:(h + 1) * L] = M.astype(np.float32)
    cons["dt_mask"] = dt_mask
    cons["gq_c"] = (np.repeat(g[None, :] ** i_loc[:, None], DH_ATT,
                              axis=1) / 8.0).astype(np.float32)
    cons["gk_c"] = np.repeat(g[None, :] ** (L - i_loc)[:, None], DH_ATT,
                           axis=1).astype(np.float32)
    cons["gl_c"] = np.repeat(g[None, :] ** L, DH_ATT)[None, :].repeat(
        64, axis=0).astype(np.float32)
    nch = T // L
    J0 = (own // L) * r
    wst = np.zeros((128, nch * H_ATT), np.float64)
    for j in range(J0):
        wst[:, j * H_ATT:(j + 1) * H_ATT] = g ** ((J0 - 1 - j) * L)
    cons["wstA_c"] = wst.astype(np.float32)
    cons["ones_col"] = np.ones((128, 1), np.float32)
    cons["ones_row"] = np.ones((1, 128), np.float32)
    cons["ident32"] = np.eye(128, dtype=np.float32)
    cons["ident16"] = np.eye(128, dtype=np.float16)
    return cons


def host_inputs(T, core, inp):
    beta, r = core // 4, core % 4
    own = T // 4
    q1 = own * (r + 1)
    d = {}
    x = np.asarray(inp["L0_attn_x"])[:, :T]
    xa = np.zeros((T, C), np.float32)
    xa[T - q1:] = x[beta, 0:q1]
    d["x_attn"] = xa
    for nm, w in (("wq", "L0_attn_Wq"), ("wk", "L0_attn_Wk"),
                  ("wv", "L0_attn_Wv"), ("wo", "L0_attn_Wo")):
        d[nm + "_t"] = np.ascontiguousarray(np.asarray(inp[w]).T)
    d["gnw_rep"] = np.broadcast_to(np.asarray(inp["L0_attn_gn_weight"]),
                                   (128, C)).copy()
    d["gnb_rep"] = np.broadcast_to(np.asarray(inp["L0_attn_gn_bias"]),
                                   (128, C)).copy()
    m = C >> r
    qw = np.zeros((C, C), np.float32)
    qw[:m, :] = np.asarray(inp["L0_ffn_q_w"]).T[:m, :]
    d["fqw_t"] = qw
    kw = np.zeros((C, C), np.float32)
    kw[:m, :] = np.asarray(inp["L0_ffn_k_w"]).T[:m, :]
    d["fkw_t"] = kw
    vw = np.zeros((C, C), np.float32)
    vw[:m, :] = np.asarray(inp["L0_ffn_v_w"]).T[:m, :]
    d["fvw_t"] = vw
    ow = np.zeros((C, C), np.float32)
    ow[:m, :m] = np.asarray(inp["L0_ffn_o_w"]).T[:m, :m]
    d["fow_t"] = ow
    d["fqb_fm"] = np.ascontiguousarray(
        np.asarray(inp["L0_ffn_q_b"], np.float32).reshape(8, 128).T)
    d["fkb_fm"] = np.ascontiguousarray(
        np.asarray(inp["L0_ffn_k_b"], np.float32).reshape(8, 128).T)
    d["fvb_rep"] = np.broadcast_to(np.asarray(inp["L0_ffn_v_b"]),
                                   (128, C)).copy()
    ob = np.zeros(C, np.float32)
    ob[:m] = np.asarray(inp["L0_ffn_o_b"])[:m]
    d["fob_rep"] = np.broadcast_to(ob, (128, C)).copy()
    d["fnw_rep"] = np.broadcast_to(np.asarray(inp["L0_ffn_norm_w"]),
                                   (128, C)).copy()
    d["fnb_rep"] = np.broadcast_to(np.asarray(inp["L0_ffn_norm_b"]),
                                   (128, C)).copy()
    return d


def build_program(T=T_GLOB, dbg=(), phases=6):
    own = T // 4
    NCH = T // L
    OCH = own // L
    PCH = NCH - OCH
    NG = NCH // 4
    OG = OCH // 4
    QT = own // 512

    nc = bass.Bass("TRN2", target_bir_lowering=False)
    st = ExitStack()
    P = Prog(nc, st)

    def din(name, shape, dt=F32):
        return nc.dram_tensor(name, shape, dt, kind="ExternalInput").ap()

    x_attn = din("x_attn", [T, C])
    wq_t, wk_t, wv_t, wo_t = (din("wq_t", [C, C]), din("wk_t", [C, C]),
                              din("wv_t", [C, C]), din("wo_t", [C, C]))
    gnw_rep, gnb_rep = din("gnw_rep", [128, C]), din("gnb_rep", [128, C])
    fqw_t, fkw_t = din("fqw_t", [C, C]), din("fkw_t", [C, C])
    fvw_t, fow_t = din("fvw_t", [C, C]), din("fow_t", [C, C])
    fqb_fm, fkb_fm = din("fqb_fm", [128, 8]), din("fkb_fm", [128, 8])
    fvb_rep, fob_rep = din("fvb_rep", [128, C]), din("fob_rep", [128, C])
    fnw_rep, fnb_rep = din("fnw_rep", [128, C]), din("fnb_rep", [128, C])
    cos_t, sin_t = din("cos_t", [T, 64]), din("sin_t", [T, 64])
    dt_mask = din("dt_mask", [L, H_ATT * L])
    gq_c, gk_c, gl_c = din("gq_c", [L, C]), din("gk_c", [L, C]), din("gl_c", [64, C])
    wstA_c = din("wstA_c", [128, NCH * H_ATT])
    ones_col_d, ones_row_d = din("ones_col", [128, 1]), din("ones_row", [1, 128])
    id32_d = din("ident32", [128, 128])
    id16_d = din("ident16", [128, 128], FP16)

    out_d = nc.dram_tensor("out", [own, C], F32, kind="ExternalOutput").ap()

    def dscr(name, shape, dt=F32):
        return Buf(nc.dram_tensor(name, shape, dt).ap())

    OCQ = OCH // 4
    kv_own_s = [dscr(f"kv_own_s{s}", [2, OCQ, L, C]) for s in range(4)]
    kv_all_s = [dscr(f"kv_all_s{s}", [4, 2, OCQ, L, C]) for s in range(4)]
    kfh_d = dscr("kfh_d", [OCH, H_ATT, DH_ATT, L], FP16)
    kfl_d = dscr("kfl_d", [OCH, H_ATT, DH_ATT, L], FP16)
    q8h_d = dscr("q8h_d", [OCH, H_ATT, DH_ATT, L], FP16)
    q8l_d = dscr("q8l_d", [OCH, H_ATT, DH_ATT, L], FP16)
    qs_d = dscr("qs_d", [OCH, H_ATT, DH_ATT, L])
    ret_d = dscr("ret_d", [OCH, L, C])
    ag_in = dscr("ag_in", [own, C])
    sa_d = dscr("sa_d", [64, C])
    HS = H_FFN * DH_FFN * 512
    pk_own_t = [nc.dram_tensor(f"pk_own{g}", [2 * HS], FP16)
                for g in range(OG)]
    pk_all_t = [nc.dram_tensor(f"pk_all{g}", [4 * 2 * HS], FP16)
                for g in range(OG)]
    pk_own_g = [Buf(t.ap()) for t in pk_own_t]
    pk_all_g = [Buf(t.ap()) for t in pk_all_t]
    vf_own_g = [dscr(f"vf_own{g}", [4, L, C]) for g in range(OG)]
    vf_all_g = [dscr(f"vf_all{g}", [4, 4, L, C]) for g in range(OG)]

    def pk_kf(hl, g, h, rk=None):
        off = (0 if rk is None else rk * 2 * HS) + hl * HS + \
            h * DH_FFN * 512
        t = pk_own_t[g] if rk is None else pk_all_t[g]
        return bass.AP(t, off, [[512, DH_FFN], [1, 512]])
    ah_d = dscr("ah_d", [H_FFN, DH_FFN, own], FP16)
    al_d = dscr("al_d", [H_FFN, DH_FFN, own], FP16)
    qh_d = dscr("qh_d", [H_FFN, DH_FFN, own], FP16)
    ql_d = dscr("ql_d", [H_FFN, DH_FFN, own], FP16)

    dbg_aps = {}
    for tag, shape in (("ret", [OCH, L, C]), ("x1", [own, C]),
                       ("ks", [NCH, L, C]), ("va", [NCH, L, C]),
                       ("xff", [NCH, L, C]),
                       ("vff", [NCH, L, C]), ("qf", [own, C]),
                       ("kff", [NCH, L, C]), ("fattn", [own, C])):
        if tag in dbg:
            dbg_aps[tag] = nc.dram_tensor("dbg_" + tag, shape, F32,
                                          kind="ExternalOutput").ap()

    sd = DmaPool(P, "sd", 12)
    sd2 = DmaPool(P, "se", 8)
    ad = DmaPool(P, "ad", 12)
    cc = Sem(st, nc, "cc")
    cc.is_cc = True
    P.extra_sems = sd.sems + sd2.sems + ad.sems + [cc]

    def sbuf(stack, name, shape, dt=F32):
        return Buf(stack.enter_context(nc.sbuf_tensor(name, shape, dt)).ap())

    def psum(stack, name, shape, dt=F32):
        return Buf(stack.enter_context(nc.psum_tensor(name, shape, dt)).ap())

    onesc_b = sbuf(st, "onesc_b", [128, 1])
    onesr_b = sbuf(st, "onesr_b", [1, 128])
    id16_b = sbuf(st, "id16_b", [128, 128], FP16)
    id32_b = sbuf(st, "id32_b", [128, 128])
    for bb, ap in ((onesc_b, ones_col_d), (onesr_b, ones_row_d),
                   (id16_b, id16_d), (id32_b, id32_d)):
        P.dma("sync", sd, bb.ap[:], ap, writes=[bb])

    _wsl_ctr = [0]

    def mk_wslots(stack, names):
        _wsl_ctr[0] += 1
        w_hi = {}
        w_lo = {}
        for wn in names:
            w_hi[wn] = sbuf(stack, f"w{_wsl_ctr[0]}_{wn}_hi", [128, 8 * C], FP16)
            w_lo[wn] = sbuf(stack, f"w{_wsl_ctr[0]}_{wn}_lo", [128, 8 * C], FP16)
        wstage = [sbuf(stack, f"wstage{_wsl_ctr[0]}_{i}", [128, 512])
                  for i in range(2)]
        return w_hi, w_lo, wstage

    def load_weight_into(hi, lo, wstage, dram_ap):
        src = dram_ap.rearrange("(cc p) (hf o) -> cc hf p o", p=128, hf=2)
        for q in range(16):
            ws = wstage[q % 2]
            P.dma("sync", sd, ws.ap[:], src[q // 2, q % 2], writes=[ws])
            P.op("vector",
                 lambda e, q=q, ws=ws: e.tensor_copy(
                     hi.ap[:, q * 512:(q + 1) * 512], ws.ap[:]),
                 reads=[ws], writes=[hi], inc=True)
            P.op("gpsimd",
                 lambda e, q=q, ws=ws: e.tensor_sub(
                     lo.ap[:, q * 512:(q + 1) * 512], ws.ap[:],
                     hi.ap[:, q * 512:(q + 1) * 512]),
                 reads=[ws, hi], writes=[lo], inc=True)

    def load_weight(slots, name, dram_ap):
        w_hi, w_lo, wstage = slots
        load_weight_into(w_hi[name], w_lo[name], wstage, dram_ap)

    def ln_tile(ph, x_buf, x_ap, out_buf, out_ap, scale_w=None, bias_w=None,
                double=False):
        s = ph["stats"]
        mu, var, r, sq_b = s["mu"], s["var"], s["r"], s["sq"]
        P.op("vector", lambda e: e.reduce_sum(mu.ap[:, 0:1], x_ap, axis=AX.X),
             reads=[x_buf], writes=[mu])
        P.op("scalar", lambda e: e.activation(sq_b.ap[:], x_ap, AF.Square),
             reads=[x_buf], writes=[sq_b])
        P.op("vector", lambda e: e.reduce_sum(var.ap[:, 0:1], sq_b.ap[:],
                                              axis=AX.X),
             reads=[sq_b], writes=[var])
        P.op("vector", lambda e: e.tensor_scalar_mul(mu.ap[:, 1:2],
                                                     mu.ap[:, 0:1], 1.0 / C),
             reads=[mu], writes=[mu])
        P.op("vector", lambda e: e.tensor_mul(r.ap[:, 3:4], mu.ap[:, 1:2],
                                              mu.ap[:, 1:2]),
             reads=[mu], writes=[r])
        P.op("vector", lambda e: e.tensor_scalar(var.ap[:, 1:2],
                                                 var.ap[:, 0:1],
                                                 1.0 / C, None, op0=ALU.mult),
             reads=[var], writes=[var])
        P.op("vector", lambda e: e.tensor_sub(var.ap[:, 2:3], var.ap[:, 1:2],
                                              r.ap[:, 3:4]),
             reads=[var, r], writes=[var])
        P.op("vector", lambda e: e.tensor_scalar(var.ap[:, 3:4],
                                                 var.ap[:, 2:3],
                                                 EPS, None, op0=ALU.add),
             reads=[var], writes=[var])
        P.op("vector", lambda e: e.reciprocal(r.ap[:, 0:1], var.ap[:, 3:4]),
             reads=[var], writes=[r])
        P.op("scalar", lambda e: e.activation(r.ap[:, 1:2], r.ap[:, 0:1],
                                              AF.Sqrt),
             reads=[r], writes=[r])
        rr_col = r.ap[:, 1:2]
        if double:
            P.op("vector", lambda e: e.tensor_mul(var.ap[:, 0:1],
                                                  var.ap[:, 2:3],
                                                  r.ap[:, 0:1]),
                 reads=[var, r], writes=[var])
            P.op("vector", lambda e: e.tensor_scalar(var.ap[:, 1:2],
                                                     var.ap[:, 0:1],
                                                     EPS, None, op0=ALU.add),
                 reads=[var], writes=[var])
            P.op("vector", lambda e: e.reciprocal(r.ap[:, 2:3],
                                                  var.ap[:, 1:2]),
                 reads=[var], writes=[r])
            P.op("scalar", lambda e: e.activation(r.ap[:, 3:4], r.ap[:, 2:3],
                                                  AF.Sqrt),
                 reads=[r], writes=[r])
            P.op("vector", lambda e: e.tensor_mul(r.ap[:, 2:3], r.ap[:, 1:2],
                                                  r.ap[:, 3:4]),
                 reads=[r], writes=[r])
            rr_col = r.ap[:, 2:3]
        if scale_w is None:
            P.op("vector", lambda e: e.tensor_scalar(out_ap, x_ap,
                                                     mu.ap[:, 1:2], rr_col,
                                                     op0=ALU.subtract,
                                                     op1=ALU.mult),
                 reads=[x_buf, mu, r], writes=[out_buf])
        else:
            P.op("vector", lambda e: e.tensor_scalar(s["d"].ap[:], x_ap,
                                                     mu.ap[:, 1:2], rr_col,
                                                     op0=ALU.subtract,
                                                     op1=ALU.mult),
                 reads=[x_buf, mu, r], writes=[s["d"]])
            P.op("vector",
                 lambda e: e.tensor_mul(out_ap, s["d"].ap[:], scale_w.ap[:]),
                 reads=[s["d"], scale_w], writes=[out_buf])
            P.op("vector", lambda e: e.tensor_add(out_ap, out_ap, bias_w.ap[:]),
                 reads=[out_buf, bias_w], writes=[out_buf])

    def split16(src_buf, src_ap, hi_buf, hi_ap, lo_buf, lo_ap,
                sub_eng="vector"):
        P.op("vector", lambda e: e.tensor_copy(hi_ap, src_ap),
             reads=[src_buf], writes=[hi_buf])
        P.op(sub_eng, lambda e: e.tensor_sub(lo_ap, src_ap, hi_ap),
             reads=[src_buf, hi_buf], writes=[lo_buf])

    def transpose16(src_buf, src_ap_base, ps_buf, n_blocks=8):
        for j in range(n_blocks):
            last = j == n_blocks - 1
            P.op("tensor",
                 lambda e, j=j: e.transpose(
                     ps_buf.ap[:, j * 128:(j + 1) * 128],
                     src_ap_base[:, j * 128:(j + 1) * 128], id16_b.ap[:]),
                 reads=[src_buf, id16_b],
                 extra=P.war(ps_buf) if j == 0 else (),
                 writes=[ps_buf] if last else (), inc=last)

    def mm3(ps_ap, lT_hi, lT_lo, r_hi, r_lo, start, stop, reads=(), writes=(),
            inc=False, ps_buf=None):
        extra0 = P.war(ps_buf) if (start and ps_buf is not None) else ()
        P.op("tensor", lambda e: e.matmul(ps_ap, lT_hi, r_hi,
                                          start=start, stop=False),
             reads=reads, extra=extra0, inc=False)
        P.op("tensor", lambda e: e.matmul(ps_ap, lT_hi, r_lo,
                                          start=False, stop=False), inc=False)
        return P.op("tensor", lambda e: e.matmul(ps_ap, lT_lo, r_hi,
                                                 start=False, stop=stop),
                    reads=reads, writes=writes, inc=inc)

    p1 = ExitStack()
    ph = {}
    ph["stats"] = {k: sbuf(p1, f"st_{k}", [128, 4]) for k in ("mu", "var", "r")}
    cosl = [sbuf(p1, f"cosl{i}", [128, 64]) for i in range(2)]
    sinl = [sbuf(p1, f"sinl{i}", [128, 64]) for i in range(2)]
    gq_b = sbuf(p1, "gq_b", [L, C])
    gk_b = sbuf(p1, "gk_b", [L, C])
    P.dma("sync", sd, gq_b.ap[:], gq_c, writes=[gq_b])
    P.dma("sync", sd, gk_b.ap[:], gk_c, writes=[gk_b])

    x_in = [sbuf(p1, f"x_in{i}", [128, C]) for i in range(3)]
    xn_tm = [sbuf(p1, f"xn_tm{i}", [128, C]) for i in range(2)]
    t16al = [sbuf(p1, f"t16a{i}", [128, C], FP16) for i in range(2)]
    t16bl = [sbuf(p1, f"t16b{i}", [128, C], FP16) for i in range(2)]
    xf_hi = [sbuf(p1, f"xf_hi{i}", [128, C], FP16) for i in range(2)]
    xf_lo = [sbuf(p1, f"xf_lo{i}", [128, C], FP16) for i in range(2)]
    rt1_s = sbuf(p1, "rt1", [128, C])
    rt2_s = sbuf(p1, "rt2", [128, C])
    rt1l = [rt1_s, rt1_s]
    rt2l = [rt2_s, rt2_s]
    ph["stats"]["sq"] = rt2_s
    krfl = [sbuf(p1, f"krf{i}", [128, C]) for i in range(2)]
    kraw = [sbuf(p1, f"kraw{i}", [128, C]) for i in range(2)]
    qraw = sbuf(p1, "qraw", [128, C])
    ksc = [sbuf(p1, f"ksc{i}", [128, C]) for i in range(2)]
    vtm = [sbuf(p1, f"vtm{i}", [128, C]) for i in range(2)]
    q8 = sbuf(p1, "q8", [128, C])
    qsh = sbuf(p1, "qsh", [128, C])
    fm16 = sbuf(p1, "fm16", [128, C], FP16)
    fm16b = sbuf(p1, "fm16b", [128, C], FP16)
    fm32 = sbuf(p1, "fm32", [128, C])

    p_trh = psum(p1, "p_trh", [128, C], FP16)
    p_trl = psum(p1, "p_trl", [128, C], FP16)
    p_k = psum(p1, "p_k", [128, C])
    p_v = psum(p1, "p_v", [128, C])
    p_q = psum(p1, "p_q", [128, C])

    wsl1 = mk_wslots(p1, ("a", "b", "c"))
    load_weight(wsl1, "a", wk_t)
    load_weight(wsl1, "b", wv_t)
    load_weight(wsl1, "c", wq_t)

    def proj_tm(slots, ps_buf, slot, fh, fl):
        hiw, low = slots[0][slot], slots[1][slot]
        for ccx in range(8):
            lh = fh.ap[:, ccx * 128:(ccx + 1) * 128]
            ll = fl.ap[:, ccx * 128:(ccx + 1) * 128]
            for half in range(2):
                wh = hiw.ap[:, ccx * C + half * 512:ccx * C + (half + 1) * 512]
                wl = low.ap[:, ccx * C + half * 512:ccx * C + (half + 1) * 512]
                po = ps_buf.ap[:, half * 512:(half + 1) * 512]
                P.op("tensor", lambda e, lh=lh, wh=wh, po=po, s=(ccx == 0):
                     e.matmul(po, lh, wh, start=s, stop=False),
                     reads=[fh, fl, hiw, low],
                     extra=P.war(ps_buf) if (ccx == 0 and half == 0) else (),
                     inc=False)
                P.op("tensor", lambda e, lh=lh, wl=wl, po=po:
                     e.matmul(po, lh, wl, start=False, stop=False), inc=False)
            for half in range(2):
                wh = hiw.ap[:, ccx * C + half * 512:ccx * C + (half + 1) * 512]
                po = ps_buf.ap[:, half * 512:(half + 1) * 512]
                last = ccx == 7 and half == 1
                P.op("tensor", lambda e, ll=ll, wh=wh, po=po, s=(ccx == 7):
                     e.matmul(po, ll, wh, start=False, stop=s),
                     reads=[fh, fl, hiw, low] if last else (),
                     writes=[ps_buf] if last else (), inc=last)

    def rope(ps_buf, cos_sb, sin_sb, out_buf, out_ap, rt1, rt2):
        ps3 = ps_buf.ap[:].rearrange("p (h d) -> p h d", h=H_ATT)
        cosv = cos_sb.ap[:].unsqueeze(1).broadcast_to([128, H_ATT, 64])
        sinv4 = sin_sb.ap[:].unsqueeze(1).broadcast_to(
            [128, H_ATT, 64]).rearrange("p h (a b) -> p h a b", a=2)
        P.op("vector", lambda e: e.tensor_mul(
            rt1.ap[:].rearrange("p (h d) -> p h d", h=H_ATT), ps3, cosv),
            reads=[ps_buf, cos_sb], writes=[rt1])
        swap = bass.AP(ps_buf.ap.tensor, ps_buf.ap.offset + 32,
                       [list(ps_buf.ap.ap[0]), [64, H_ATT], [-32, 2], [1, 32]])
        P.op("vector", lambda e: e.tensor_mul(
            rt2.ap[:].rearrange("p (h a b) -> p h a b", h=H_ATT, a=2),
            swap, sinv4),
            reads=[ps_buf, sin_sb], writes=[rt2])
        r4 = rt1.ap[:].rearrange("p (h a b) -> p h a b", h=H_ATT, a=2)
        o4 = rt2.ap[:].rearrange("p (h a b) -> p h a b", h=H_ATT, a=2)
        d4 = out_ap.rearrange("p (h a b) -> p h a b", h=H_ATT, a=2)
        P.op("vector", lambda e: e.tensor_sub(d4[:, :, 0], r4[:, :, 0],
                                              o4[:, :, 0]),
             reads=[rt1, rt2], writes=[out_buf])
        P.op("vector", lambda e: e.tensor_add(d4[:, :, 1], r4[:, :, 1],
                                              o4[:, :, 1]),
             reads=[rt1, rt2], writes=[out_buf])

    def store_fm16(dram_buf, i, src_buf):
        view = src_buf.ap[:].rearrange("p (cc t) -> p cc t", cc=8)
        d4 = dram_buf.ap[i].rearrange("(hh sub) d t -> sub d hh t", sub=2)
        P.dma("scalar", ad, d4[0], view[0:64], reads=[src_buf],
              writes=[dram_buf])
        P.dma("scalar", ad, d4[1], view[64:128], reads=[src_buf],
              writes=[dram_buf])

    if phases >= 1:
        for tt in range(PCH, NCH):
            xi = x_in[tt % 3]
            P.dma("sync", sd, xi.ap[:],
                  x_attn.rearrange("(n p) c -> n p c", p=128)[tt], writes=[xi])
            cs, sn = cosl[tt % 2], sinl[tt % 2]
            P.dma("sync", sd2, cs.ap[:],
                  cos_t.rearrange("(n p) d -> n p d", p=128)[tt], writes=[cs])
            P.dma("sync", sd2, sn.ap[:],
                  sin_t.rearrange("(n p) d -> n p d", p=128)[tt], writes=[sn])
            xn = xn_tm[tt % 2]
            t16a, t16b = t16al[tt % 2], t16bl[tt % 2]
            ln_tile(ph, xi, xi.ap[:], xn, xn.ap[:])
            split16(xn, xn.ap[:], t16a, t16a.ap[:], t16b, t16b.ap[:],
                    sub_eng="gpsimd")
            transpose16(t16a, t16a.ap, p_trh)
            transpose16(t16b, t16b.ap, p_trl)
            fh, fl = xf_hi[tt % 2], xf_lo[tt % 2]
            P.op("scalar", lambda e, fh=fh: e.copy(fh.ap[:], p_trh.ap[:]),
                 reads=[p_trh], writes=[fh])
            P.op("scalar", lambda e, fl=fl: e.copy(fl.ap[:], p_trl.ap[:]),
                 reads=[p_trl], writes=[fl])
            own_tile = True

            proj_tm(wsl1, p_k, "a", fh, fl)
            proj_tm(wsl1, p_v, "b", fh, fl)
            if own_tile:
                proj_tm(wsl1, p_q, "c", fh, fl)

            kr_ = kraw[tt % 2]
            P.op("scalar", lambda e, kr_=kr_: e.copy(kr_.ap[:], p_k.ap[:]),
                 reads=[p_k], writes=[kr_])
            krf = krfl[tt % 2]
            rope(kr_, cs, sn, krf, krf.ap[:], rt1l[tt % 2], rt2l[tt % 2])
            ks = ksc[tt % 2]
            P.op("vector", lambda e, ks=ks, krf=krf: e.tensor_mul(
                ks.ap[:], krf.ap[:], gk_b.ap[:]),
                 reads=[krf, gk_b], writes=[ks])
            vt = vtm[tt % 2]
            P.op("scalar", lambda e, vt=vt: e.copy(vt.ap[:], p_v.ap[:]),
                 reads=[p_v], writes=[vt])
            kvs = kv_own_s[(tt - PCH) // OCQ]
            P.dma("scalar", ad, kvs.ap[0, (tt - PCH) % OCQ], ks.ap[:],
                  reads=[ks], writes=[kvs])
            P.dma("scalar", ad, kvs.ap[1, (tt - PCH) % OCQ], vt.ap[:],
                  reads=[vt], writes=[kvs])
            if (tt - PCH) % OCQ == OCQ - 1:
                s_ = (tt - PCH) // OCQ
                P.op("gpsimd", lambda e, s_=s_: e.collective_compute(
                    "AllGather", ALU.bypass,
                    replica_groups=[[0, 1, 2, 3], [4, 5, 6, 7]],
                    ins=[kv_own_s[s_].ap], outs=[kv_all_s[s_].ap]),
                    reads=[kv_own_s[s_]], writes=[kv_all_s[s_]],
                    sem=cc, step=1)

            if own_tile:
                i = tt - PCH
                split16(krf, krf.ap[:], t16a, t16a.ap[:], t16b, t16b.ap[:])
                transpose16(t16a, t16a.ap, p_trh)
                transpose16(t16b, t16b.ap, p_trl)
                P.op("vector", lambda e: e.tensor_copy(fm16.ap[:], p_trh.ap[:]),
                     reads=[p_trh], writes=[fm16])
                P.op("vector", lambda e: e.tensor_copy(fm16b.ap[:],
                                                       p_trl.ap[:]),
                     reads=[p_trl], writes=[fm16b])
                store_fm16(kfh_d, i, fm16)
                store_fm16(kfl_d, i, fm16b)
                P.op("scalar", lambda e: e.copy(qraw.ap[:], p_q.ap[:]),
                     reads=[p_q], writes=[qraw])
                rope(qraw, cs, sn, q8, q8.ap[:], rt1l[(tt + 1) % 2],
                     rt2l[(tt + 1) % 2])
                P.op("vector", lambda e: e.tensor_mul(qsh.ap[:], q8.ap[:],
                                                      gq_b.ap[:]),
                     reads=[q8, gq_b], writes=[qsh])
                split16(q8, q8.ap[:], t16a, t16a.ap[:], t16b, t16b.ap[:])
                transpose16(t16a, t16a.ap, p_trh)
                transpose16(t16b, t16b.ap, p_trl)
                P.op("vector", lambda e: e.tensor_copy(fm16.ap[:], p_trh.ap[:]),
                     reads=[p_trh], writes=[fm16])
                P.op("vector", lambda e: e.tensor_copy(fm16b.ap[:],
                                                       p_trl.ap[:]),
                     reads=[p_trl], writes=[fm16b])
                store_fm16(q8h_d, i, fm16)
                store_fm16(q8l_d, i, fm16b)
                for j in range(8):
                    last = j == 7
                    P.op("tensor",
                         lambda e, j=j: e.transpose(
                             p_k.ap[:, j * 128:(j + 1) * 128],
                             qsh.ap[:, j * 128:(j + 1) * 128], id32_b.ap[:]),
                         reads=[qsh, id32_b],
                         extra=P.war(p_k) if j == 0 else (),
                         writes=[p_k] if last else (), inc=last)
                P.op("vector", lambda e: e.tensor_copy(fm32.ap[:], p_k.ap[:]),
                     reads=[p_k], writes=[fm32])
                v32 = fm32.ap[:].rearrange("p (cc t) -> p cc t", cc=8)
                d4 = qs_d.ap[i].rearrange("(hh sub) d t -> sub d hh t", sub=2)
                P.dma("scalar", ad, d4[0], v32[0:64], reads=[fm32],
                      writes=[qs_d])
                P.dma("scalar", ad, d4[1], v32[64:128], reads=[fm32],
                      writes=[qs_d])

        P.barrier(skip_cc=True)
        p1.close()

    if phases >= 2:
        p2s = ExitStack()
        S_b = sbuf(p2s, "S_b", [64, C])
        Sg_b = sbuf(p2s, "Sg_b", [64, C])
        gl_b = sbuf(p2s, "gl_b", [64, C])
        wA_b = sbuf(p2s, "wA_b", [128, NCH * H_ATT])
        dt_b = sbuf(p2s, "dt_b", [L, H_ATT * L])
        P.dma("sync", sd, gl_b.ap[:], gl_c, writes=[gl_b])
        P.dma("sync", sd, wA_b.ap[:], wstA_c, writes=[wA_b])
        P.dma("sync", sd, dt_b.ap[:], dt_mask, writes=[dt_b])
        ks_in = [sbuf(p2s, f"ks_in{i}", [128, C]) for i in range(4)]
        v_in = [sbuf(p2s, f"v_in{i}", [128, C]) for i in range(4)]
        kw_b = [sbuf(p2s, f"kw_b{i}", [128, C]) for i in range(2)]
        kfh_in = sbuf(p2s, "kfh_in", [64, H_ATT * L], FP16)
        kfl_in = sbuf(p2s, "kfl_in", [64, H_ATT * L], FP16)
        q8h_in = sbuf(p2s, "q8h_in", [64, H_ATT * L], FP16)
        q8l_in = sbuf(p2s, "q8l_in", [64, H_ATT * L], FP16)
        qs_in = sbuf(p2s, "qs_in", [64, H_ATT * L])
        am_sb = sbuf(p2s, "am_sb", [128, H_ATT * L])
        ret_sb = [sbuf(p2s, f"ret_sb{i}", [128, C]) for i in range(2)]
        p_S = psum(p2s, "p_S", [64, C])
        p_A = psum(p2s, "p_A", [128, H_ATT * L])
        p_O = psum(p2s, "p_O", [128, C])

        P.op("vector", lambda e: e.memset(S_b.ap[:], 0.0), writes=[S_b])

        def state_mms(ksb, vb):
            for h in range(H_ATT):
                P.op("tensor",
                     lambda e, h=h, ksb=ksb, vb=vb: e.matmul(
                         p_S.ap[:, h * 64:(h + 1) * 64],
                         ksb.ap[:, h * 64:(h + 1) * 64],
                         vb.ap[:, h * 64:(h + 1) * 64],
                         start=True, stop=True),
                     reads=[ksb, vb],
                     extra=P.war(p_S) if h == 0 else (),
                     writes=[p_S] if h == H_ATT - 1 else (), inc=True)

        for i in range(OCH):
            ksb, vb = ks_in[i % 4], v_in[i % 4]
            kvs = kv_own_s[i // OCQ]
            P.dma("sync", sd, ksb.ap[:], kvs.ap[0, i % OCQ],
                  reads=[kvs], writes=[ksb])
            P.dma("sync", sd, vb.ap[:], kvs.ap[1, i % OCQ],
                  reads=[kvs], writes=[vb])
            for db, db_d in ((kfh_in, kfh_d), (kfl_in, kfl_d),
                             (q8h_in, q8h_d), (q8l_in, q8l_d),
                             (qs_in, qs_d)):
                P.dma("sync", sd2, db.ap[:].rearrange(
                    "d (h t) -> d h t", h=H_ATT),
                    db_d.ap[i].rearrange("h d t -> d h t"),
                    reads=[db_d], writes=[db])
            for h in range(H_ATT):
                sl = h * L
                mm3(p_A.ap[:, sl:sl + L],
                    kfh_in.ap[:, h * L:(h + 1) * L],
                    kfl_in.ap[:, h * L:(h + 1) * L],
                    q8h_in.ap[:, h * L:(h + 1) * L],
                    q8l_in.ap[:, h * L:(h + 1) * L],
                    start=True, stop=True,
                    reads=[kfh_in, kfl_in, q8h_in, q8l_in],
                    writes=[p_A] if h == H_ATT - 1 else (), inc=True,
                    ps_buf=p_A if h == 0 else None)
            P.op("vector",
                 lambda e: e.tensor_mul(am_sb.ap[:], p_A.ap[:], dt_b.ap[:]),
                 reads=[p_A, dt_b], writes=[am_sb])
            for h in range(H_ATT):
                sl = h * L
                P.op("tensor",
                     lambda e, h=h, sl=sl, vb=vb: e.matmul(
                         p_O.ap[:, h * 64:(h + 1) * 64],
                         am_sb.ap[:, sl:sl + L],
                         vb.ap[:, h * 64:(h + 1) * 64],
                         start=True, stop=False),
                     reads=[am_sb, vb],
                     extra=P.war(p_O) if h == 0 else (), inc=False)
                P.op("tensor",
                     lambda e, h=h: e.matmul(
                         p_O.ap[:, h * 64:(h + 1) * 64],
                         qs_in.ap[:, h * L:(h + 1) * L],
                         S_b.ap[:, h * 64:(h + 1) * 64],
                         start=False, stop=True),
                     reads=[qs_in, S_b],
                     writes=[p_O] if h == H_ATT - 1 else (), inc=True)
            rsb = ret_sb[i % 2]
            P.op("scalar", lambda e, rsb=rsb: e.copy(rsb.ap[:], p_O.ap[:]),
                 reads=[p_O], writes=[rsb])
            P.dma("scalar", ad, ret_d.ap[i], rsb.ap[:], reads=[rsb],
                  writes=[ret_d])
            state_mms(ksb, vb)
            P.op("vector", lambda e: e.tensor_mul(Sg_b.ap[:], S_b.ap[:],
                                                  gl_b.ap[:]),
                 reads=[S_b, gl_b], writes=[Sg_b])
            P.op("vector", lambda e: e.tensor_add(S_b.ap[:], Sg_b.ap[:],
                                                  p_S.ap[:]),
                 reads=[Sg_b, p_S], writes=[S_b])

        for j in range(NCH):
            ksb, vb = ks_in[j % 4], v_in[j % 4]
            kw = kw_b[j % 2]
            kva = kv_all_s[(j % OCH) // OCQ]
            P.dma("sync", sd, ksb.ap[:],
                  kva.ap[j // OCH, 0, (j % OCH) % OCQ],
                  reads=[kva], writes=[ksb])
            P.dma("scalar", ad, vb.ap[:],
                  kva.ap[j // OCH, 1, (j % OCH) % OCQ],
                  reads=[kva], writes=[vb])
            P.op("vector", lambda e, j=j, ksb=ksb, kw=kw: e.tensor_mul(
                kw.ap[:].rearrange("d (h x) -> d h x", h=H_ATT),
                ksb.ap[:].rearrange("d (h x) -> d h x", h=H_ATT),
                wA_b.ap[:, j * H_ATT:(j + 1) * H_ATT].unsqueeze(2)
                .broadcast_to([128, H_ATT, DH_ATT])),
                reads=[ksb, wA_b], writes=[kw])
            for h in range(H_ATT):
                P.op("tensor",
                     lambda e, h=h, j=j, kw=kw, vb=vb: e.matmul(
                         p_S.ap[:, h * 64:(h + 1) * 64],
                         kw.ap[:, h * 64:(h + 1) * 64],
                         vb.ap[:, h * 64:(h + 1) * 64],
                         start=(j == 0 and h % 8 == 0),
                         stop=(j == NCH - 1 and h % 8 == 7)),
                     reads=[kw, vb],
                     extra=P.war(p_S) if (j == 0 and h == 0) else (),
                     writes=[p_S] if (j == NCH - 1 and h == H_ATT - 1)
                     else (), inc=True)
        P.op("scalar", lambda e: e.copy(Sg_b.ap[:], p_S.ap[:]),
             reads=[p_S], writes=[Sg_b])
        P.dma("scalar", ad, sa_d.ap, Sg_b.ap[:], reads=[Sg_b], writes=[sa_d])
        P.barrier()
        p2s.close()

    if phases >= 3:
        p3s = ExitStack()
        gnw_b = sbuf(p3s, "gnw_b", [128, C])
        gnb_b = sbuf(p3s, "gnb_b", [128, C])
        Tc_b = sbuf(p3s, "Tc_b", [64, C])
        Tc2_b = sbuf(p3s, "Tc2_b", [64, C])
        gl3_b = sbuf(p3s, "gl3_b", [64, C])
        qs3_in = [sbuf(p3s, f"qs3_{i}", [64, H_ATT * L]) for i in range(2)]
        ri2_b = [sbuf(p3s, f"ri2_{i}", [128, C]) for i in range(2)]
        p_corr = psum(p3s, "p_corr", [128, C])
        P.dma("sync", sd, gnw_b.ap[:], gnw_rep, writes=[gnw_b])
        P.dma("sync", sd, gnb_b.ap[:], gnb_rep, writes=[gnb_b])
        P.dma("sync", sd, gl3_b.ap[:], gl_c, writes=[gl3_b])
        P.dma("sync", sd, Tc_b.ap[:], sa_d.ap, reads=[sa_d], writes=[Tc_b])
        ret_in = [sbuf(p3s, f"ret_in{i}", [128, C]) for i in range(2)]
        xo_in = [sbuf(p3s, f"xo_in{i}", [128, C]) for i in range(2)]
        gstatl = [sbuf(p3s, f"gstat{i}", [128, 64]) for i in range(2)]
        gd_bl = [sbuf(p3s, f"gd_b{i}", [128, C]) for i in range(2)]
        gsq_bl = [sbuf(p3s, f"gsq_b{i}", [128, C]) for i in range(2)]
        gt_bl = [sbuf(p3s, f"gt_b{i}", [128, C]) for i in range(2)]
        gn_bl = [sbuf(p3s, f"gn_b{i}", [128, C]) for i in range(2)]
        g16al = [sbuf(p3s, f"g16a{i}", [128, C], FP16) for i in range(2)]
        g16bl = [sbuf(p3s, f"g16b{i}", [128, C], FP16) for i in range(2)]
        gfhl = [sbuf(p3s, f"gfh{i}", [128, C], FP16) for i in range(2)]
        gfll = [sbuf(p3s, f"gfl{i}", [128, C], FP16) for i in range(2)]
        x1_sb = [sbuf(p3s, f"x1_sb{i}", [128, C]) for i in range(2)]
        p_ghl = [psum(p3s, f"p_gh{i}", [128, C], FP16) for i in range(2)]
        p_gll = [psum(p3s, f"p_gl{i}", [128, C], FP16) for i in range(2)]
        p_W = psum(p3s, "p_W", [128, C])
        wsl3 = mk_wslots(p3s, ("a",))
        load_weight(wsl3, "a", wo_t)
        HA = H_ATT

        Tcl = [Tc_b, Tc2_b]

        def c_loads(i):
            ri = ret_in[i % 2]
            P.dma("sync", sd, ri.ap[:], ret_d.ap[i], reads=[ret_d], writes=[ri])
            xo = xo_in[i % 2]
            P.dma("sync", sd2, xo.ap[:],
                  x_attn.rearrange("(n p) c -> n p c", p=128)[PCH + i],
                  writes=[xo])
            qs3 = qs3_in[i % 2]
            P.dma("sync", sd2, qs3.ap[:].rearrange(
                "d (h t) -> d h t", h=H_ATT),
                qs_d.ap[i].rearrange("h d t -> d h t"),
                reads=[qs_d], writes=[qs3])

        def corr_mms(i):
            qs3, Tc = qs3_in[i % 2], Tcl[i % 2]
            for h in range(H_ATT):
                P.op("tensor",
                     lambda e, h=h, qs3=qs3, Tc=Tc: e.matmul(
                         p_corr.ap[:, h * 64:(h + 1) * 64],
                         qs3.ap[:, h * L:(h + 1) * L],
                         Tc.ap[:, h * 64:(h + 1) * 64],
                         start=True, stop=True),
                     reads=[qs3, Tc],
                     extra=P.war(p_corr) if h == 0 else (),
                     writes=[p_corr] if h == H_ATT - 1 else (), inc=True)

        c_loads(0)
        corr_mms(0)
        for i in range(OCH):
            gstat = gstatl[i % 2]
            gd_b, gsq_b = gd_bl[i % 2], gsq_bl[i % 2]
            gt_b, gn_b = gt_bl[i % 2], gn_bl[i % 2]
            g16a, g16b = g16al[i % 2], g16bl[i % 2]
            gfh, gfl = gfhl[i % 2], gfll[i % 2]
            p_gh, p_gl = p_ghl[i % 2], p_gll[i % 2]
            xo = xo_in[i % 2]
            ri2 = ri2_b[i % 2]
            ri = ret_in[i % 2]
            P.op("vector", lambda e, ri2=ri2, ri=ri: e.tensor_add(
                ri2.ap[:], ri.ap[:], p_corr.ap[:]),
                reads=[ri, p_corr], writes=[ri2])
            if i < OCH - 1:
                P.op("vector", lambda e, i=i: e.tensor_mul(
                    Tcl[(i + 1) % 2].ap[:], Tcl[i % 2].ap[:], gl3_b.ap[:]),
                    reads=[Tcl[i % 2], gl3_b], writes=[Tcl[(i + 1) % 2]])
                c_loads(i + 1)
                corr_mms(i + 1)
            ri = ri2
            r3 = ri.ap[:].rearrange("p (h d) -> p h d", h=HA)
            P.op("vector", lambda e, r3=r3, gstat=gstat: e.reduce_sum(
                gstat.ap[:, 0:HA], r3, axis=AX.X), reads=[ri], writes=[gstat])
            P.op("vector", lambda e, gstat=gstat: e.tensor_scalar_mul(
                gstat.ap[:, HA:2 * HA], gstat.ap[:, 0:HA], 1.0 / DH_ATT),
                reads=[gstat], writes=[gstat])
            mub = gstat.ap[:, HA:2 * HA].unsqueeze(2).broadcast_to(
                [128, HA, DH_ATT])
            P.op("vector", lambda e, r3=r3, mub=mub, gd_b=gd_b: e.tensor_sub(
                gd_b.ap[:].rearrange("p (h d) -> p h d", h=HA), r3, mub),
                reads=[ri, gstat], writes=[gd_b])
            P.op("scalar", lambda e, gsq_b=gsq_b, gd_b=gd_b: e.activation(
                gsq_b.ap[:], gd_b.ap[:], AF.Square),
                 reads=[gd_b], writes=[gsq_b])
            P.op("vector", lambda e, gstat=gstat, gsq_b=gsq_b: e.reduce_sum(
                gstat.ap[:, 2 * HA:3 * HA],
                gsq_b.ap[:].rearrange("p (h d) -> p h d", h=HA), axis=AX.X),
                reads=[gsq_b], writes=[gstat])
            P.op("vector", lambda e, gstat=gstat: e.tensor_scalar(
                gstat.ap[:, 3 * HA:4 * HA], gstat.ap[:, 2 * HA:3 * HA],
                1.0 / DH_ATT, EPS, op0=ALU.mult, op1=ALU.add),
                reads=[gstat], writes=[gstat])
            P.op("vector", lambda e, gstat=gstat: e.reciprocal(
                gstat.ap[:, 0:HA], gstat.ap[:, 3 * HA:4 * HA]),
                reads=[gstat], writes=[gstat])
            P.op("scalar", lambda e, gstat=gstat: e.activation(
                gstat.ap[:, HA:2 * HA], gstat.ap[:, 0:HA], AF.Sqrt),
                reads=[gstat], writes=[gstat])
            rb = gstat.ap[:, HA:2 * HA].unsqueeze(2).broadcast_to(
                [128, HA, DH_ATT])
            P.op("vector", lambda e, rb=rb, gt_b=gt_b, gd_b=gd_b: e.tensor_mul(
                gt_b.ap[:].rearrange("p (h d) -> p h d", h=HA),
                gd_b.ap[:].rearrange("p (h d) -> p h d", h=HA), rb),
                reads=[gd_b, gstat], writes=[gt_b])
            P.op("vector", lambda e, gn_b=gn_b, gt_b=gt_b: e.tensor_mul(
                gn_b.ap[:], gt_b.ap[:], gnw_b.ap[:]),
                 reads=[gt_b, gnw_b], writes=[gn_b])
            P.op("vector", lambda e, gn_b=gn_b: e.tensor_add(
                gn_b.ap[:], gn_b.ap[:], gnb_b.ap[:]),
                 reads=[gn_b, gnb_b], writes=[gn_b])
            split16(gn_b, gn_b.ap[:], g16a, g16a.ap[:], g16b, g16b.ap[:],
                    sub_eng="gpsimd")
            transpose16(g16a, g16a.ap, p_gh)
            transpose16(g16b, g16b.ap, p_gl)
            P.op("scalar", lambda e, gfh=gfh, p_gh=p_gh: e.copy(
                gfh.ap[:], p_gh.ap[:]),
                 reads=[p_gh], writes=[gfh])
            P.op("scalar", lambda e, gfl=gfl, p_gl=p_gl: e.copy(
                gfl.ap[:], p_gl.ap[:]),
                 reads=[p_gl], writes=[gfl])
            proj_tm(wsl3, p_W, "a", gfh, gfl)
            x1 = x1_sb[i % 2]
            P.op("vector", lambda e, x1=x1, xo=xo: e.tensor_add(
                x1.ap[:], p_W.ap[:], xo.ap[:]),
                reads=[p_W, xo], writes=[x1])
            P.dma("scalar", ad, ag_in.ap.rearrange("(n p) c -> n p c", p=128)[i],
                  x1.ap[:], reads=[x1], writes=[ag_in])
            if "x1" in dbg_aps:
                P.dma("scalar", ad,
                      dbg_aps["x1"].rearrange("(n p) c -> n p c", p=128)[i],
                      x1.ap[:], reads=[x1])
        P.barrier()
        p3s.close()

    if phases >= 4:
        p4s = ExitStack()
        fnw_b = sbuf(p4s, "fnw_b", [128, C])
        fnb_b = sbuf(p4s, "fnb_b", [128, C])
        fvb_b = sbuf(p4s, "fvb_b", [128, C])
        fqb_b = sbuf(p4s, "fqb_b", [128, 8])
        fkb_b = sbuf(p4s, "fkb_b", [128, 8])
        for bb, ap in ((fnw_b, fnw_rep), (fnb_b, fnb_rep), (fvb_b, fvb_rep),
                       (fqb_b, fqb_fm), (fkb_b, fkb_fm)):
            P.dma("sync", sd, bb.ap[:], ap, writes=[bb])
        xgrpl = [sbuf(p4s, f"xgrp{i}", [128, 4 * C]) for i in range(2)]
        sq4l = [sbuf(p4s, f"sq4_{i}", [128, C]) for i in range(2)]
        stat4l = [sbuf(p4s, f"stat4_{i}", [128, 64]) for i in range(2)]
        s4dl = [sbuf(p4s, f"s4d{i}", [128, C]) for i in range(2)]
        xff_tl = [sbuf(p4s, f"xff_t{i}", [128, C]) for i in range(2)]
        f16al = [sbuf(p4s, f"f16a{i}", [128, C], FP16) for i in range(2)]
        f16bl = [sbuf(p4s, f"f16b{i}", [128, C], FP16) for i in range(2)]
        xg_hil = [sbuf(p4s, f"xg_hi{i}", [128, 8 * 512], FP16)
                  for i in range(2)]
        xg_lol = [sbuf(p4s, f"xg_lo{i}", [128, 8 * 512], FP16)
                  for i in range(2)]
        xgd_hi = dscr("xgd_hi", [OG, 128, 8 * 512], FP16)
        xgd_lo = dscr("xgd_lo", [OG, 128, 8 * 512], FP16)
        qf_tl = [sbuf(p4s, f"qf_t{i}", [128, 512]) for i in range(2)]
        q16hl = [sbuf(p4s, f"q16h{i}", [128, 512], FP16) for i in range(2)]
        q16ll = [sbuf(p4s, f"q16l{i}", [128, 512], FP16) for i in range(2)]
        p_trh4l = [psum(p4s, f"p_trh4{i}", [128, C], FP16) for i in range(2)]
        p_trl4l = [psum(p4s, f"p_trl4{i}", [128, C], FP16) for i in range(2)]
        p_p4l = [psum(p4s, f"p_p4{i}", [128, 512]) for i in range(2)]
        wsl4 = mk_wslots(p4s, ("a", "b"))

        def ln2_group(src_dram, src_buf, g, dst_dbg=None):
            gp = g % 2
            xg_hi, xg_lo = xg_hil[gp], xg_lol[gp]
            xgr = xgrpl[gp]
            st4 = stat4l[gp]
            P.dma("sync", sd, xgr.ap[:].rearrange("p (u c) -> p u c", u=4),
                  src_dram.rearrange("(gg u p) c -> gg p u c", u=4, p=128)[g],
                  reads=[src_buf] if src_buf is not None else (),
                  writes=[xgr])
            x4 = xgr.ap[:].rearrange("p (u c) -> p u c", u=4)
            P.op("vector", lambda e, st4=st4, x4=x4: e.reduce_sum(
                st4.ap[:, 0:4], x4, axis=AX.X), reads=[xgr], writes=[st4])
            for u in range(4):
                sq4 = sq4l[u % 2]
                P.op("scalar", lambda e, sq4=sq4, xgr=xgr, u=u: e.activation(
                    sq4.ap[:], xgr.ap[:, u * C:(u + 1) * C], AF.Square),
                    reads=[xgr], writes=[sq4])
                P.op("vector", lambda e, st4=st4, sq4=sq4, u=u: e.reduce_sum(
                    st4.ap[:, 4 + u:5 + u], sq4.ap[:], axis=AX.X),
                    reads=[sq4], writes=[st4])
            P.op("vector", lambda e, st4=st4: e.tensor_scalar(
                st4.ap[:, 8:12], st4.ap[:, 0:4], 1.0 / C, None, op0=ALU.mult),
                reads=[st4], writes=[st4])
            P.op("vector", lambda e, st4=st4: e.tensor_mul(
                st4.ap[:, 12:16], st4.ap[:, 8:12], st4.ap[:, 8:12]),
                reads=[st4], writes=[st4])
            P.op("vector", lambda e, st4=st4: e.tensor_scalar(
                st4.ap[:, 16:20], st4.ap[:, 4:8], 1.0 / C, None, op0=ALU.mult),
                reads=[st4], writes=[st4])
            P.op("vector", lambda e, st4=st4: e.tensor_sub(
                st4.ap[:, 20:24], st4.ap[:, 16:20], st4.ap[:, 12:16]),
                reads=[st4], writes=[st4])
            P.op("vector", lambda e, st4=st4: e.tensor_scalar(
                st4.ap[:, 24:28], st4.ap[:, 20:24], EPS, None, op0=ALU.add),
                reads=[st4], writes=[st4])
            P.op("vector", lambda e, st4=st4: e.reciprocal(
                st4.ap[:, 28:32], st4.ap[:, 24:28]),
                reads=[st4], writes=[st4])
            P.op("scalar", lambda e, st4=st4: e.activation(
                st4.ap[:, 32:36], st4.ap[:, 28:32], AF.Sqrt),
                reads=[st4], writes=[st4])
            P.op("vector", lambda e, st4=st4: e.tensor_mul(
                st4.ap[:, 36:40], st4.ap[:, 20:24], st4.ap[:, 28:32]),
                reads=[st4], writes=[st4])
            P.op("vector", lambda e, st4=st4: e.tensor_scalar(
                st4.ap[:, 40:44], st4.ap[:, 36:40], EPS, None, op0=ALU.add),
                reads=[st4], writes=[st4])
            P.op("vector", lambda e, st4=st4: e.reciprocal(
                st4.ap[:, 44:48], st4.ap[:, 40:44]),
                reads=[st4], writes=[st4])
            P.op("scalar", lambda e, st4=st4: e.activation(
                st4.ap[:, 48:52], st4.ap[:, 44:48], AF.Sqrt),
                reads=[st4], writes=[st4])
            P.op("vector", lambda e, st4=st4: e.tensor_mul(
                st4.ap[:, 52:56], st4.ap[:, 32:36], st4.ap[:, 48:52]),
                reads=[st4], writes=[st4])
            for u in range(4):
                par = u % 2
                xff_t = xff_tl[par]
                f16a, f16b = f16al[par], f16bl[par]
                p_trh4, p_trl4 = p_trh4l[par], p_trl4l[par]
                xsl = xgr.ap[:, u * C:(u + 1) * C]
                s4d = s4dl[par]
                P.op("vector", lambda e, st4=st4, xsl=xsl, u=u,
                     s4d=s4d: e.tensor_scalar(
                         s4d.ap[:], xsl, st4.ap[:, 8 + u:9 + u],
                         st4.ap[:, 52 + u:53 + u],
                         op0=ALU.subtract, op1=ALU.mult),
                     reads=[xgr, st4], writes=[s4d])
                P.op("vector", lambda e, xff_t=xff_t, s4d=s4d: e.tensor_mul(
                    xff_t.ap[:], s4d.ap[:], fnw_b.ap[:]),
                    reads=[s4d, fnw_b], writes=[xff_t])
                P.op("vector", lambda e, xff_t=xff_t: e.tensor_add(
                    xff_t.ap[:], xff_t.ap[:], fnb_b.ap[:]),
                    reads=[xff_t, fnb_b], writes=[xff_t])
                if dst_dbg is not None:
                    P.dma("scalar", ad, dst_dbg[g * 4 + u], xff_t.ap[:],
                          reads=[xff_t])
                split16(xff_t, xff_t.ap[:], f16a, f16a.ap[:], f16b,
                        f16b.ap[:], sub_eng="gpsimd")
                transpose16(f16a, f16a.ap, p_trh4)
                transpose16(f16b, f16b.ap, p_trl4)
                for (pbuf, dbuf) in ((p_trh4, xg_hi), (p_trl4, xg_lo)):
                    P.op("scalar",
                         lambda e, pbuf=pbuf, dbuf=dbuf, u=u: e.copy(
                             dbuf.ap[:].rearrange(
                                 "p (cc t) -> p cc t", cc=8)[:, :,
                                                             u * 128:(u + 1) * 128],
                             pbuf.ap[:].rearrange("p (cc t) -> p cc t", cc=8)),
                         reads=[pbuf], writes=[dbuf] if u == 3 else ())
            return xg_hi, xg_lo

        load_weight(wsl4, "a", fkw_t)
        load_weight(wsl4, "b", fvw_t)
        kf_tl = [sbuf(p4s, f"kf_t{i}", [128, 512]) for i in range(2)]
        kf16_l = [sbuf(p4s, f"kf16{i}", [128, 512], FP16) for i in range(2)]
        kf16l_l = [sbuf(p4s, f"kf16l{i}", [128, 512], FP16) for i in range(2)]
        vt4l = [sbuf(p4s, f"vt4{i}", [128, C]) for i in range(2)]
        p_v4 = psum(p4s, "p_v4", [128, C])
        for g in range(OG):
            xg_hi, xg_lo = ln2_group(ag_in.ap, ag_in, g,
                                     dst_dbg=dbg_aps.get("xff") and
                                     dbg_aps["xff"].rearrange(
                                         "(n p) c -> n p c", p=128))
            P.dma("scalar", ad, xgd_hi.ap[g], xg_hi.ap[:],
                  reads=[xg_hi], writes=[xgd_hi])
            P.dma("scalar", ad, xgd_lo.ap[g], xg_lo.ap[:],
                  reads=[xg_lo], writes=[xgd_lo])
            mcc = 8
            for h in range(H_FFN):
                p_p4 = p_p4l[h % 2]
                kf_t = kf_tl[h % 2]
                kf16, kf16l = kf16_l[h % 2], kf16l_l[h % 2]
                for ccx in range(mcc):
                    last = ccx == mcc - 1
                    mm3(p_p4.ap[:],
                        wsl4[0]["a"].ap[:, ccx * C + h * 128:ccx * C + (h + 1) * 128],
                        wsl4[1]["a"].ap[:, ccx * C + h * 128:ccx * C + (h + 1) * 128],
                        xg_hi.ap[:, ccx * 512:(ccx + 1) * 512],
                        xg_lo.ap[:, ccx * 512:(ccx + 1) * 512],
                        start=(ccx == 0), stop=last,
                        reads=[wsl4[0]["a"], wsl4[1]["a"], xg_hi, xg_lo],
                        writes=[p_p4] if last else (), inc=last,
                        ps_buf=p_p4 if ccx == 0 else None)
                P.op("vector", lambda e, h=h, p_p4=p_p4, kf_t=kf_t: e.tensor_scalar(
                    kf_t.ap[:], p_p4.ap[:], fkb_b.ap[:, h:h + 1], None,
                    op0=ALU.add), reads=[p_p4, fkb_b], writes=[kf_t])
                split16(kf_t, kf_t.ap[:], kf16, kf16.ap[:], kf16l,
                        kf16l.ap[:], sub_eng="gpsimd")
                P.dma("scalar", ad, pk_kf(0, g, h), kf16.ap[:],
                      reads=[kf16], writes=[pk_own_g[g]])
                P.dma("scalar", ad, pk_kf(1, g, h), kf16l.ap[:],
                      reads=[kf16l], writes=[pk_own_g[g]])
            P.op("gpsimd", lambda e, g=g: e.collective_compute(
                "AllGather", ALU.bypass,
                replica_groups=[[0, 1, 2, 3], [4, 5, 6, 7]],
                ins=[pk_own_g[g].ap], outs=[pk_all_g[g].ap]),
                reads=[pk_own_g[g]], writes=[pk_all_g[g]], sem=cc, step=1)
            for u in range(4):
                vt4 = vt4l[u % 2]
                for ccx in range(mcc):
                    for half in range(2):
                        last = ccx == mcc - 1 and half == 1
                        mm3(p_v4.ap[:, half * 512:(half + 1) * 512],
                            xg_hi.ap[:, ccx * 512 + u * 128:
                                     ccx * 512 + (u + 1) * 128],
                            xg_lo.ap[:, ccx * 512 + u * 128:
                                     ccx * 512 + (u + 1) * 128],
                            wsl4[0]["b"].ap[:, ccx * C + half * 512:
                                            ccx * C + (half + 1) * 512],
                            wsl4[1]["b"].ap[:, ccx * C + half * 512:
                                            ccx * C + (half + 1) * 512],
                            start=(ccx == 0), stop=(ccx == mcc - 1),
                            reads=[xg_hi, xg_lo, wsl4[0]["b"], wsl4[1]["b"]],
                            writes=[p_v4] if last else (), inc=last,
                            ps_buf=p_v4 if (ccx == 0 and half == 0) else None)
                P.op("vector", lambda e, vt4=vt4: e.tensor_add(
                    vt4.ap[:], p_v4.ap[:], fvb_b.ap[:]),
                    reads=[p_v4, fvb_b], writes=[vt4])
                P.dma("scalar", ad, vf_own_g[g].ap[u], vt4.ap[:],
                      reads=[vt4], writes=[vf_own_g[g]])
                if "vff" in dbg_aps:
                    P.dma("scalar", ad, dbg_aps["vff"][g * 4 + u], vt4.ap[:],
                          reads=[vt4])
            P.op("gpsimd", lambda e, g=g: e.collective_compute(
                "AllGather", ALU.bypass,
                replica_groups=[[0, 1, 2, 3], [4, 5, 6, 7]],
                ins=[vf_own_g[g].ap], outs=[vf_all_g[g].ap]),
                reads=[vf_own_g[g]], writes=[vf_all_g[g]], sem=cc, step=1)
        load_weight(wsl4, "a", fqw_t)
        for g in range(OG):
            gp = g % 2
            xg_hi, xg_lo = xg_hil[gp], xg_lol[gp]
            P.dma("sync", sd, xg_hi.ap[:], xgd_hi.ap[g],
                  reads=[xgd_hi], writes=[xg_hi])
            P.dma("sync", sd2, xg_lo.ap[:], xgd_lo.ap[g],
                  reads=[xgd_lo], writes=[xg_lo])
            for h in range(H_FFN):
                p_p4 = p_p4l[h % 2]
                qf_t = qf_tl[h % 2]
                for ccx in range(8):
                    last = ccx == 7
                    mm3(p_p4.ap[:],
                        wsl4[0]["a"].ap[:, ccx * C + h * 128:ccx * C + (h + 1) * 128],
                        wsl4[1]["a"].ap[:, ccx * C + h * 128:ccx * C + (h + 1) * 128],
                        xg_hi.ap[:, ccx * 512:(ccx + 1) * 512],
                        xg_lo.ap[:, ccx * 512:(ccx + 1) * 512],
                        start=(ccx == 0), stop=last,
                        reads=[wsl4[0]["a"], wsl4[1]["a"], xg_hi, xg_lo],
                        writes=[p_p4] if last else (), inc=last,
                        ps_buf=p_p4 if ccx == 0 else None)
                P.op("vector", lambda e, h=h, p_p4=p_p4, qf_t=qf_t: e.tensor_scalar(
                    qf_t.ap[:], p_p4.ap[:], fqb_b.ap[:, h:h + 1], None,
                    op0=ALU.add), reads=[p_p4, fqb_b], writes=[qf_t])
                q16h, q16l = q16hl[h % 2], q16ll[h % 2]
                split16(qf_t, qf_t.ap[:], q16h, q16h.ap[:], q16l,
                        q16l.ap[:], sub_eng="gpsimd")
                P.dma("scalar", ad, qh_d.ap[h][:, g * 512:(g + 1) * 512],
                      q16h.ap[:], reads=[q16h], writes=[qh_d])
                P.dma("scalar", ad, ql_d.ap[h][:, g * 512:(g + 1) * 512],
                      q16l.ap[:], reads=[q16l], writes=[ql_d])


        P.barrier(skip_cc=True)
        p4s.close()

    if phases >= 5:
        p5s = ExitStack()
        kv_h = [sbuf(p5s, f"kvh{i}", [128, 128], FP16) for i in range(3)]
        kv_l = [sbuf(p5s, f"kvl{i}", [128, 128], FP16) for i in range(3)]
        vv = [sbuf(p5s, f"vv{i}", [128, 128]) for i in range(3)]
        qsl_h = [sbuf(p5s, f"qslh{i}", [128, own], FP16) for i in range(2)]
        qsl_l = [sbuf(p5s, f"qsll{i}", [128, own], FP16) for i in range(2)]
        pP = [sbuf(p5s, f"pP{i}", [128, 512]) for i in range(4)]
        accg = [sbuf(p5s, f"accg{i}", [128, 512]) for i in range(QT)]
        accv = [sbuf(p5s, f"accv{i}", [128, 512]) for i in range(QT)]
        a_raw = sbuf(p5s, "a_raw", [128, 512])
        a_div = sbuf(p5s, "a_div", [128, 512])
        a_h16 = sbuf(p5s, "a_h16", [128, 512], FP16)
        a_l16 = sbuf(p5s, "a_l16", [128, 512], FP16)
        sums_sb = sbuf(p5s, "sums_sb", [1, 512])
        recip_sb = sbuf(p5s, "recip_sb", [1, 512])
        p_out = [psum(p5s, f"p_out{i}", [128, 512]) for i in range(QT)]
        p_sc = [psum(p5s, f"p_sc{i}", [128, 512]) for i in range(max(QT, 2))]

        pend_pv = []
        for h in range(H_FFN):
            qh_s, ql_s = qsl_h[h % 2], qsl_l[h % 2]
            P.dma("sync", sd2, qh_s.ap[:], qh_d.ap[h], reads=[qh_d],
                  writes=[qh_s])
            P.dma("sync", sd2, ql_s.ap[:], ql_d.ap[h], reads=[ql_d],
                  writes=[ql_s])
            for qt in range(QT):
                P.op("gpsimd", lambda e, qt=qt: e.memset(accg[qt].ap[:], 0.0),
                     writes=[accg[qt]])
                P.op("vector", lambda e, qt=qt: e.memset(accv[qt].ap[:], 0.0),
                     writes=[accv[qt]])
            for kb in range(NCH):
                sl = kb % 3
                kh, kl, vb = kv_h[sl], kv_l[sl], vv[sl]
                rk, gg, sub = kb // OCH, (kb % OCH) // 4, kb % 4
                kh_src = pk_kf(0, gg, h, rk)
                kl_src = pk_kf(1, gg, h, rk)
                P.dma("sync", sd, kh.ap[:],
                      bass.AP(kh_src.tensor, kh_src.offset + sub * 128,
                              [[512, DH_FFN], [1, 128]]),
                      reads=[pk_all_g[gg]], writes=[kh])
                P.dma("sync", sd, kl.ap[:],
                      bass.AP(kl_src.tensor, kl_src.offset + sub * 128,
                              [[512, DH_FFN], [1, 128]]),
                      reads=[pk_all_g[gg]], writes=[kl])
                P.dma("sync", sd2, vb.ap[:],
                      vf_all_g[gg].ap[rk, sub].rearrange(
                          "p (hh d) -> p hh d", hh=H_FFN)[:, h],
                      reads=[vf_all_g[gg]], writes=[vb])
                for qt in range(QT):
                    psc = p_sc[qt]
                    co = qt * 512
                    P.op("tensor",
                         lambda e, psc=psc, kh=kh, co=co, qh_s=qh_s: e.matmul(
                             psc.ap[:], kh.ap[:], qh_s.ap[:, co:co + 512],
                             start=True, stop=False),
                         reads=[kh, kl, qh_s, ql_s],
                         extra=P.war(psc), inc=False)
                    P.op("tensor",
                         lambda e, psc=psc, kh=kh, co=co, ql_s=ql_s: e.matmul(
                             psc.ap[:], kh.ap[:], ql_s.ap[:, co:co + 512],
                             start=False, stop=False), inc=False)
                for qt in range(QT):
                    psc = p_sc[qt]
                    co = qt * 512
                    P.op("tensor",
                         lambda e, psc=psc, kl=kl, co=co, qh_s=qh_s: e.matmul(
                             psc.ap[:], kl.ap[:], qh_s.ap[:, co:co + 512],
                             start=False, stop=True),
                         reads=[kh, kl, qh_s, ql_s] if qt == QT - 1 else (),
                         writes=[psc], inc=True)
                for f in pend_pv:
                    f()
                pend_pv = []
                for qt in range(QT):
                    psc = p_sc[qt]
                    pp = pP[qt]
                    P.op("scalar", lambda e, pp=pp, psc=psc: e.activation(
                        pp.ap[:], psc.ap[:], AF.Exp, scale=float(1.0 / 32.0)),
                        reads=[psc], writes=[pp])

                    def emit_pv(pp=pp, vb=vb, qt=qt, kb=kb):
                        P.op("tensor",
                             lambda e, pp=pp, vb=vb, qt=qt, kb=kb: e.matmul(
                                 p_out[qt].ap[:], vb.ap[:], pp.ap[:],
                                 start=(kb == 0), stop=(kb == NCH - 1)),
                             reads=[vb, pp],
                             extra=P.war(p_out[qt]) if kb == 0 else (),
                             writes=[p_out[qt]] if kb == NCH - 1 else (),
                             inc=True)
                    pend_pv.append(emit_pv)
                    if kb % 2 == 0:
                        P.op("gpsimd", lambda e, pp=pp, qt=qt: e.tensor_add(
                            accg[qt].ap[:], accg[qt].ap[:], pp.ap[:]),
                            reads=[pp, accg[qt]], writes=[accg[qt]])
                    else:
                        P.op("vector", lambda e, pp=pp, qt=qt: e.tensor_add(
                            accv[qt].ap[:], accv[qt].ap[:], pp.ap[:]),
                            reads=[pp, accv[qt]], writes=[accv[qt]])
            for f in pend_pv:
                f()
            pend_pv = []
            for qt in range(QT):
                co = h * own + qt * 512
                P.op("scalar", lambda e, qt=qt: e.copy(a_raw.ap[:],
                                                       p_out[qt].ap[:]),
                     reads=[p_out[qt]], writes=[a_raw])
                P.op("tensor", lambda e, qt=qt: e.matmul(
                    p_sc[0].ap[0:1, :], onesc_b.ap[:], accg[qt].ap[:],
                    start=True, stop=False),
                    reads=[onesc_b, accg[qt]], extra=P.war(p_sc[0]),
                    inc=False)
                P.op("tensor", lambda e, qt=qt: e.matmul(
                    p_sc[0].ap[0:1, :], onesc_b.ap[:], accv[qt].ap[:],
                    start=False, stop=True),
                    reads=[onesc_b, accv[qt]], writes=[p_sc[0]], inc=True)
                P.op("scalar", lambda e: e.copy(sums_sb.ap[:],
                                                p_sc[0].ap[0:1, :]),
                     reads=[p_sc[0]], writes=[sums_sb])
                P.op("vector", lambda e: e.reciprocal(recip_sb.ap[:],
                                                      sums_sb.ap[:]),
                     reads=[sums_sb], writes=[recip_sb])
                P.op("tensor", lambda e: e.matmul(
                    p_sc[1].ap[:], onesr_b.ap[:], recip_sb.ap[:],
                    start=True, stop=True),
                    reads=[onesr_b, recip_sb], extra=P.war(p_sc[1]),
                    writes=[p_sc[1]], inc=True)
                P.op("vector", lambda e: e.tensor_mul(a_div.ap[:], a_raw.ap[:],
                                                      p_sc[1].ap[:]),
                     reads=[a_raw, p_sc[1]], writes=[a_div])
                split16(a_div, a_div.ap[:], a_h16, a_h16.ap[:],
                        a_l16, a_l16.ap[:], sub_eng="gpsimd")
                P.dma("scalar", ad, ah_d.ap[h][:, qt * 512:(qt + 1) * 512],
                      a_h16.ap[:], reads=[a_h16], writes=[ah_d])
                P.dma("scalar", ad, al_d.ap[h][:, qt * 512:(qt + 1) * 512],
                      a_l16.ap[:], reads=[a_l16], writes=[al_d])
        P.barrier()
        p5s.close()

    if phases >= 6:
        p6s = ExitStack()
        ph6 = {}
        ph6["stats"] = {k: sbuf(p6s, f"s6_{k}", [128, 4]) for k in
                        ("mu", "var", "r")}
        ph6["stats"]["d"] = sbuf(p6s, "s6_d", [128, C])
        ph6["stats"]["sq"] = sbuf(p6s, "s6_sq", [128, C])
        fob_b = sbuf(p6s, "fob_b", [128, C])
        P.dma("sync", sd, fob_b.ap[:], fob_rep, writes=[fob_b])
        x1_in = [sbuf(p6s, f"x1_in{i}", [128, C]) for i in range(2)]
        xn2f = sbuf(p6s, "xn2f", [128, C])
        ot_sb = [sbuf(p6s, f"ot_sb{i}", [128, C]) for i in range(2)]
        a_hin = [sbuf(p6s, f"a_hin{i}", [128, H_FFN * 128], FP16)
                 for i in range(2)]
        a_lin = [sbuf(p6s, f"a_lin{i}", [128, H_FFN * 128], FP16)
                 for i in range(2)]
        p_W6 = psum(p6s, "p_W6", [128, C])
        wsl6 = mk_wslots(p6s, ("a",))
        load_weight(wsl6, "a", fow_t)

        for i in range(OCH):
            xi = x1_in[i % 2]
            P.dma("sync", sd, xi.ap[:],
                  ag_in.ap.rearrange("(n p) c -> n p c", p=128)[i],
                  reads=[ag_in], writes=[xi])
            ah, al = a_hin[i % 2], a_lin[i % 2]
            P.dma("sync", sd2, ah.ap[:].rearrange("d (h t) -> d h t", h=H_FFN),
                  ah_d.ap[:, :, i * 128:(i + 1) * 128].transpose([1, 0, 2]),
                  reads=[ah_d], writes=[ah])
            P.dma("sync", sd2, al.ap[:].rearrange("d (h t) -> d h t", h=H_FFN),
                  al_d.ap[:, :, i * 128:(i + 1) * 128].transpose([1, 0, 2]),
                  reads=[al_d], writes=[al])
            ln_tile(ph6, xi, xi.ap[:], xn2f, xn2f.ap[:])
            for mc in range(8):
                for half in range(2):
                    last = mc == 7 and half == 1
                    mm3(p_W6.ap[:, half * 512:(half + 1) * 512],
                        ah.ap[:, mc * 128:(mc + 1) * 128],
                        al.ap[:, mc * 128:(mc + 1) * 128],
                        wsl6[0]["a"].ap[:, mc * C + half * 512:
                                        mc * C + (half + 1) * 512],
                        wsl6[1]["a"].ap[:, mc * C + half * 512:
                                        mc * C + (half + 1) * 512],
                        start=(mc == 0), stop=(mc == 7),
                        reads=[ah, al, wsl6[0]["a"], wsl6[1]["a"]],
                        writes=[p_W6] if last else (), inc=last,
                        ps_buf=p_W6 if (mc == 0 and half == 0) else None)
            ot = ot_sb[i % 2]
            P.op("vector", lambda e, ot=ot: e.tensor_add(ot.ap[:], p_W6.ap[:],
                                                         fob_b.ap[:]),
                 reads=[p_W6, fob_b], writes=[ot])
            P.op("vector", lambda e, ot=ot: e.tensor_add(ot.ap[:], ot.ap[:],
                                                         xn2f.ap[:]),
                 reads=[ot, xn2f], writes=[ot])
            P.op("vector", lambda e, ot=ot, xi=xi: e.tensor_add(
                ot.ap[:], ot.ap[:], xi.ap[:]),
                reads=[ot, xi], writes=[ot])
            P.dma("scalar", ad, out_d.rearrange("(n p) c -> n p c", p=128)[i],
                  ot.ap[:], reads=[ot])
        P.barrier()
        p6s.close()

    with nc.Block() as blk:
        P.resolve_and_replay(blk)
    st.close()
    return nc


def kernel(**inputs):
    T = T_GLOB
    own = T // 4
    nc = build_program(T)
    in_maps = []
    for core in range(NCORES):
        m = {}
        m.update(host_constants(T, core))
        m.update(host_inputs(T, core, inputs))
        in_maps.append(m)
    res = run_bass_kernel_spmd(nc, in_maps, core_ids=list(range(NCORES)))
    y = np.empty((B_GLOB, T, C), np.float32)
    for core in range(NCORES):
        beta, r = core // 4, core % 4
        y[beta, r * own:(r + 1) * own] = res.results[core]["out"]
    return y
